# version 17
# baseline (speedup 1.0000x reference)
import numpy as np
import ml_dtypes
import concourse.bass as bass
import concourse.mybir as mybir
from concourse.bass_utils import run_bass_kernel_spmd

F32 = mybir.dt.float32
BF16 = mybir.dt.bfloat16
AF = mybir.ActivationFunctionType
ALU = mybir.AluOpType
AX = mybir.AxisListType
BF = ml_dtypes.bfloat16

B, T, D = 4, 4096, 1024
NT = 16
TOK = NT * 128
DC = D // 128
H, HD = 16, 64
KVH = 4
FF, FC = 2048, 16
EE = 8
NC2 = T // 128
EPS = 1e-5

ENGS = ("tensor", "vector", "scalar", "gpsimd", "sync")


class Sem:
    def __init__(self, h):
        self.h = h
        self.n = 0


class Buf:
    def __init__(self, n):
        self.free_at = [[] for _ in range(n)]
        self.i = 0
        self.n = n

    def acq(self):
        i = self.i % self.n
        self.i += 1
        w = self.free_at[i]
        self.free_at[i] = []
        return i, tuple(w)

    def rel(self, i, sem):
        self.free_at[i].append((sem, sem.n))


class Sched:
    def __init__(self):
        self.q = {e: [] for e in ENGS}
        self.count = 0
        self.selfsem = {}
        self.maxw = {e: {} for e in ENGS}

    def op(self, eng, fn, wait=(), inc=()):
        wl = []
        mw = self.maxw[eng]
        ss = self.selfsem.get(eng)
        if ss is not None and ss.n > 0:
            wl.append((ss.h, ss.n))
        for w in wait:
            s, v = w
            wl.append((s.h, s.n if v is None else v))
        wl2 = []
        for h, v in wl:
            hid = id(h)
            if v > mw.get(hid, 0):
                mw[hid] = v
                wl2.append((h, v))
        wl = wl2
        il = []
        for s, k in inc:
            s.n += k
            il.append((s.h, k))
        self.count += 1

        def thunk(e, fn=fn, wl=wl, il=il):
            for h, v in wl:
                if v > 0:
                    e.wait_ge(h, v)
            ins = fn(e)
            for h, k in il:
                ins.then_inc(h, k)
        self.q[eng].append(thunk)

    def flush(self, blk):
        for e in ENGS:
            thunks = self.q[e]
            if not thunks:
                continue

            def body(eng, thunks=thunks):
                for t in thunks:
                    t(eng)
            getattr(blk, e)(body)


import os
DBG = int(os.environ.get("KDBG", "0"))


def build_program():
    nc = bass.Bass("TRN2", target_bir_lowering=False,
                   detect_race_conditions=not int(os.environ.get("KNORACE", "0")))
    dbg_outs = {}

    def din(name, shape, dt=F32):
        return nc.dram_tensor(name, list(shape), dt, kind="ExternalInput").ap()

    x_in = din("x", [TOK, D])
    wq = din("wq", [512, D]); wk = din("wk", [512, 256]); wv = din("wv", [512, 256])
    wo = din("wo", [D, D])
    qpa = din("qpa", [1, H]); als = din("als", [1, H])
    fq = din("fq", [D, D]); fkw = din("fk", [D, D]); fvw = din("fv", [D, D])
    fow = din("fow", [D, D]); fgw = din("fgw", [D, D]); fgbT = din("fgbT", [128, DC])
    fswT = din("fswT", [HD, H])
    n1g = din("n1g", [1, D]); n1b = din("n1b", [1, D])
    n2g = din("n2g", [1, D]); n2b = din("n2b", [1, D])
    mgw = din("mgw", [D, EE]); mgb = din("mgb", [1, EE])
    w1 = din("w1", [EE, D, FF]); b1 = din("b1", [EE, FF])
    lngT = din("lngT", [EE, 128, FC]); lnbT = din("lnbT", [EE, 128, FC])
    w2 = din("w2", [EE, FF, D]); b2 = din("b2", [EE, D])
    rsc = din("rsc", [1, EE])
    identb = din("identb", [128, 128], BF16)
    identf = din("identf", [128, 128], F32)
    qrel = din("qrel", [4, TOK], BF16)
    krel = din("krel", [4, T], BF16)
    amask = din("amask", [2, 128, 128], F32)
    cmask = din("cmask", [2, 128, 128], BF16)
    iot8 = din("iot8", [1, EE])
    kcol = din("kcol", [128, NC2])

    out = nc.dram_tensor("out", [TOK, D], F32, kind="ExternalOutput").ap()
    xs = nc.dram_tensor("xs", [TOK, D], F32).ap()
    rowsd = nc.dram_tensor("rowsd", [24, 4096], F32).ap()

    KT_ELE = 64 * (H * TOK)
    VT_ELE = 128 * (NT * H * 65)
    CC1N = KT_ELE + VT_ELE
    KT2_ELE = 64 * (KVH * TOK)
    VT2_ELE = 128 * (NT * KVH * 65)
    CC2N = KT2_ELE + VT2_ELE
    cc_in = [nc.dram_tensor(f"cci{i}", [1, n], BF16).ap()
             for i, n in ((0, CC1N), (1, CC2N), (2, CC1N))]
    cc_out = [nc.dram_tensor(f"cco{i}", [2, n], BF16).ap()
              for i, n in ((0, CC1N), (1, CC2N), (2, CC1N))]

    sch = Sched()
    import contextlib
    stack = contextlib.ExitStack()
    sb = lambda name, shape, dt: stack.enter_context(nc.sbuf_tensor(name, list(shape), dt))
    ps = lambda name, shape, dt: stack.enter_context(nc.psum_tensor(name, list(shape), dt))

    WSA = sb("WSA", [128, 16384], BF16)
    WSB = sb("WSB", [128, 16384], BF16)
    FMB = sb("FMB", [128, 16384], BF16)
    QB = sb("QB", [128, 16384], BF16)
    SCR = sb("SCR", [128, 19456], BF16)
    XB = sb("XB", [128, 3072], F32)
    EB = sb("EB", [128, 2048], BF16)
    STT_ = sb("STAT", [128, 128], F32)
    ROWS = sb("ROWS", [128, 2560], F32)
    NGB = sb("NGB", [128, 2048], BF16)
    LNGB = sb("LNGB", [128, 64], F32)
    IDB = sb("IDB", [128, 128], BF16)
    SM16 = sb("SM16", [128, 128], F32)
    GSC = sb("GSC", [128, 48], F32)
    FCO = sb("FCO", [128, NT * 8], F32)
    SWB = sb("SWB", [128, NT], F32)
    CTB = sb("CTB", [8, NT * 128], BF16)
    FCB = sb("FCB", [128, 8], BF16)
    B2S = sb("B2S", [8, 1024], BF16)
    MGWS = sb("MGWS", [128, 64], BF16)
    ONE1 = sb("ONE1", [128, 1], BF16)
    EPSC = sb("EPSC", [128, 1], F32)
    PI2C = sb("PI2C", [128, 1], F32)
    SWCOL = sb("SWCOL", [128, 16], BF16)
    SCRF = SCR.bitcast(F32)
    ZS = SCR
    H2T = SCR
    XRS = SCR[:, 16384:17408]
    B1T = SCR[:, 17408:19456]
    vAt = SCR
    KRELS = SCR[0:4, 8320:12416]
    QRELS = SCR[0:4, 12416:14464]
    AMS = SCRF[:, 7232:7488]
    CMS = SCR[:, 14976:15232]
    MBUF = SCRF[:, 7616:7872]
    MH16 = SCRF[:, 7872:7888]
    MHB = SCR[:, 15776:15792]
    EXPM = SCRF[:, 7904:8160]
    WQSTG = SCRF[:, 0:2048]
    RCPB = XB
    SCBB = SCR[:, 0:2048]
    T1F = SCRF[:, 1024:2048]

    P0 = ps("P0", [128, 1024], F32)
    P1 = ps("P1", [128, 1024], F32)
    P2 = ps("P2", [128, 1024], F32)
    PT = ps("PT", [128, 1024], F32)

    mk = lambda name: Sem(stack.enter_context(nc.semaphore(name)))
    sT = mk("sT"); sV = mk("sV"); sA = mk("sA"); sCC = mk("sCC")
    sB = mk("sB")
    sch.selfsem = {"vector": sV, "scalar": sA}
    SYNCP = [mk(f"sq{i}") for i in range(8)]
    GPP = [mk(f"gq{i}") for i in range(4)]
    _qrr = {"sync": [0], "gpsimd": [0]}

    def ALLD():
        return tuple((s, s.n) for s in SYNCP)

    def ALLG():
        return tuple((s, s.n) for s in GPP)

    def phase_barrier(bufs):
        W = ALLD() + ALLG() + ((sT, sT.n), (sV, sV.n), (sA, sA.n), (sCC, sCC.n))
        for eng in ENGS:
            op(eng, lambda e: e.nop(), wait=W, inc=((sB, 1),))
        gate = sB.n
        allsems = [sT, sV, sA, sCC] + SYNCP + GPP
        def clear_all(e, hs=[s.h for s in allsems]):
            ins = None
            for h in hs:
                ins = e.sem_clear(h)
            return ins
        op("gpsimd", clear_all, wait=((sB, gate),), inc=((sB, 1),))
        gate2 = sB.n
        for s in allsems:
            s.n = 0
        for eng in ENGS:
            op(eng, lambda e: e.nop(), wait=((sB, gate2),))
        for b in bufs:
            b.free_at = [[] for _ in range(b.n)]
            b.i = 0

    bXB = Buf(3)
    bEB = Buf(2)
    bSP = Buf(3)
    bWH = Buf(2)
    bB1 = Buf(1)
    bLNG = Buf(2)
    bOS = Buf(1)
    bST = Buf(4)
    bRW = Buf(2)
    bZV = Buf(2)

    op = sch.op

    def dma(queue, outap, inap, wait=(), slow=False):
        pool = SYNCP if queue == "sync" else GPP
        rr = _qrr[queue]
        s = pool[rr[0] % len(pool)]
        rr[0] += 1
        op(queue, lambda e, o=outap, i=inap, slow=slow: e.dma_start(out=o, in_=i, allow_slow_non_contiguous=slow),
           wait=tuple(wait) + ((s, s.n),), inc=((s, 16),))
        return (s, s.n)

    def dmaT(queue, outap, inap, wait=()):
        pool = SYNCP if queue == "sync" else GPP
        rr = _qrr[queue]
        s = pool[rr[0] % len(pool)]
        rr[0] += 1
        op(queue, lambda e, o=outap, i=inap: e.dma_start_transpose(out=o, in_=i),
           wait=tuple(wait) + ((s, s.n),), inc=((s, 16),))
        return (s, s.n)

    def bounce_bcast(row_src, ncols, dst_ap, drow_idx, nrows, wait=(), wait2=()):
        w1_ = dma("sync", rowsd[drow_idx:drow_idx + 1, 0:ncols], row_src, wait=wait)
        bc = rowsd[drow_idx:drow_idx + 1, 0:ncols].broadcast_to([nrows, ncols])
        return dma("sync", dst_ap, bc, wait=(w1_,) + tuple(wait2))

    def dump(name, ap, wait):
        if not DBG:
            return
        if name not in dbg_outs:
            dbg_outs[name] = nc.dram_tensor("dbg_" + name, list(ap.shape), ap.dtype, kind="ExternalOutput").ap()
        dma("sync", dbg_outs[name], ap, wait=wait)

    dma("sync", IDB[:], identb)
    dma("sync", SM16[:, 48:56], iot8.broadcast_to([128, EE]))
    dma("sync", SM16[:, 56:64], rsc.broadcast_to([128, EE]))
    dma("sync", SM16[:, 64:72], mgb.broadcast_to([128, EE]))
    dma("gpsimd", SWCOL[0:64, :], fswT)
    dma("gpsimd", SWCOL[64:128, :], fswT)
    for k in range(DC):
        dma("gpsimd", MGWS[:, k * 8:(k + 1) * 8], mgw[k * 128:(k + 1) * 128, :])
    op("vector", lambda e: e.memset(ONE1[:], 1.0), inc=((sV, 1),))
    op("vector", lambda e: e.memset(EPSC[:], EPS), inc=((sV, 1),))
    op("vector", lambda e: e.memset(PI2C[:], 1.5707963267948966), inc=((sV, 1),))
    w_ = dma("sync", ROWS[0:1, 0:16], als)
    op("scalar", lambda e: e.activation(ROWS[0:1, 16:32], ROWS[0:1, 0:16], AF.Exp),
       wait=(w_,), inc=((sA, 1),))
    bounce_bcast(ROWS[0:1, 16:32], 16, SM16[:, 0:16], 0, 128, wait=((sA, None),))
    CW = ALLD() + ALLG() + ((sV, sV.n), (sA, sA.n))
    for eng in ENGS:
        op(eng, lambda e: e.nop(), wait=CW)

    def ln_stats(src_ap, twait=()):
        si, sw = bST.acq()
        st_ = STT_[:, si * 32:(si + 1) * 32]
        op("vector", lambda e, st_=st_: e.bn_stats(st_[:, 0:6], src_ap[:, 0:512]),
           wait=tuple(twait) + sw, inc=((sV, 1),))
        op("vector", lambda e, st_=st_: e.bn_stats(st_[:, 6:12], src_ap[:, 512:1024]), inc=((sV, 1),))
        op("vector", lambda e, st_=st_: e.bn_aggr(st_[:, 12:14], st_[:, 0:12].rearrange("p (n s) -> p n s", s=3)),
           inc=((sV, 1),))
        op("scalar", lambda e, st_=st_: e.activation(st_[:, 16:17], st_[:, 13:14], AF.Sqrt, bias=EPSC[:, 0:1]),
           wait=((sV, None),), inc=((sA, 1),))
        op("vector", lambda e, st_=st_: e.reciprocal(st_[:, 14:15], st_[:, 16:17]),
           wait=((sA, None),), inc=((sV, 1),))
        op("vector", lambda e, st_=st_: e.scalar_tensor_tensor(st_[:, 15:16], st_[:, 12:13], -1.0, st_[:, 14:15], ALU.mult, ALU.mult),
           inc=((sV, 1),))
        return si, st_[:, 14:15], st_[:, 15:16]

    def ln_stats4(srcs, twait=()):
        si, sw = bST.acq()
        st_ = STT_[:, si * 32:(si + 1) * 32]
        for ci, s_ in enumerate(srcs):
            op("vector", lambda e, st_=st_, ci=ci, s_=s_: e.bn_stats(st_[:, ci * 6:(ci + 1) * 6], s_),
               wait=(tuple(twait) + sw) if ci == 0 else (), inc=((sV, 1),))
        op("vector", lambda e, st_=st_: e.bn_aggr(st_[:, 24:26], st_[:, 0:24].rearrange("p (n s) -> p n s", s=3)),
           inc=((sV, 1),))
        op("scalar", lambda e, st_=st_: e.activation(st_[:, 28:29], st_[:, 25:26], AF.Sqrt, bias=EPSC[:, 0:1]),
           wait=((sV, None),), inc=((sA, 1),))
        op("vector", lambda e, st_=st_: e.reciprocal(st_[:, 26:27], st_[:, 28:29]),
           wait=((sA, None),), inc=((sV, 1),))
        op("vector", lambda e, st_=st_: e.scalar_tensor_tensor(st_[:, 27:28], st_[:, 24:25], -1.0, st_[:, 26:27], ALU.mult, ALU.mult),
           inc=((sV, 1),))
        return si, st_[:, 26:27], st_[:, 27:28]

    def ln0(src_dram, dst_fm):
        for t in range(NT):
            i, w = bXB.acq()
            xt = XB[:, i * 1024:(i + 1) * 1024]
            xw = dma("sync", xt, src_dram[t * 128:(t + 1) * 128, :], wait=w)
            si, rstd, negm = ln_stats(xt, twait=(xw,))
            tgt = ZS[:, (t % 2) * 1024:(t % 2) * 1024 + 1024]
            op("scalar", lambda e, xt=xt, tgt=tgt, rstd=rstd, negm=negm:
               e.activation(tgt, xt, AF.Identity, bias=negm, scale=rstd),
               wait=((sV, None), (sT, None)), inc=((sA, 1),))
            bST.rel(si, sA)
            bXB.rel(i, sA)
            for ch in range(DC):
                pc = (ch % 2) * 512
                op("tensor", lambda e, tgt=tgt, ch=ch, pc=pc:
                   e.matmul(PT[:, pc:pc + 128], tgt[:, ch * 128:(ch + 1) * 128], IDB[:], start=True, stop=True),
                   wait=((sA, None), (sV, None)), inc=((sT, 1),))
                op("vector", lambda e, ch=ch, t=t, pc=pc:
                   e.tensor_copy(dst_fm[:, ch * 2048 + t * 128: ch * 2048 + (t + 1) * 128], PT[:, pc:pc + 128]),
                   wait=((sT, None),), inc=((sV, 1),))

    def load_wslab(slab, wdram, kc, width, queue="gpsimd", wait=()):
        last = None
        for k in range(kc):
            last = dma(queue, slab[:, k * width:(k + 1) * width],
                       wdram[k * 128:(k + 1) * 128, :], wait=wait if k == 0 else ())
        return last

    def projection(src_fm, slab, kc, evac, first_wait=()):
        marks = [()] * 4
        for oc in range(DC):
            for sp in range(4):
                idx = oc * 4 + sp
                bank = [P0, P1][idx % 2]
                pa = bank[:, 0:512] if (idx % 4) < 2 else bank[:, 512:1024]
                for k in range(kc):
                    w = ()
                    if k == 0:
                        w = (first_wait + ((sA, None), (sV, None))) if idx == 0 else marks[idx % 4]
                    op("tensor", lambda e, pa=pa, k=k, oc=oc, sp=sp, slab=slab:
                       e.matmul(pa, slab[:, k * 1024 + oc * 128: k * 1024 + (oc + 1) * 128],
                                src_fm[:, k * 2048 + sp * 512: k * 2048 + (sp + 1) * 512],
                                start=(k == 0), stop=(k == kc - 1)),
                       wait=w, inc=((sT, 1),) if k == kc - 1 else ())
                evac(oc, sp, pa)
                marks[idx % 4] = ((sA, sA.n), (sV, sV.n))

    def transpose_to_tok(esl, sp, dst_fn):
        for tp in range(4):
            t = sp * 4 + tp
            pc = (tp % 2) * 512
            op("tensor", lambda e, esl=esl, tp=tp, pc=pc:
               e.matmul(PT[:, pc:pc + 128], esl[:, tp * 128:(tp + 1) * 128], IDB[:], start=True, stop=True),
               wait=((sA, None), (sV, None)), inc=((sT, 1),))
            dst_fn(t, PT[:, pc:pc + 128])

    def ffn(src_dram, dst_dram, ccid):
        cci, cco = cc_in[ccid], cc_out[ccid]
        ln0(src_dram, FMB)
        dump(f"srcfm{ccid}", FMB[:], ((sV, None),))

        whi, whw = bWH.acq()
        ksl = WSB[:, whi * 8192: whi * 8192 + 8192]
        kw = load_wslab(ksl, fkw, DC, 1024, wait=whw)
        ktv = cci[0:1, 0:KT_ELE].rearrange("a (p c) -> (a p) c", p=64)

        def evk(oc, sp, pa):
            ei, ew = bEB.acq()
            esl = EB[:, ei * 1024: ei * 1024 + 1024]
            op("scalar", lambda e, esl=esl, pa=pa: e.copy(esl[:, 0:512], pa),
               wait=((sT, None),) + ew, inc=((sA, 1),))
            op("scalar", lambda e, esl=esl: e.activation(esl[:, 512:1024], esl[:, 0:512], AF.Square),
               wait=((sT, None),), inc=((sA, 1),))
            for half in range(2):
                hh = 2 * oc + half
                _last_dma = dma("sync", ktv[0:64, hh * TOK + sp * 512: hh * TOK + (sp + 1) * 512],
                    esl[half * 64:half * 64 + 64, 0:512], wait=((sA, None),))
                op("tensor", lambda e, esl=esl, half=half:
                   e.matmul(P2[0:1, 0:512], ONE1[half * 64:half * 64 + 64, 0:1],
                            esl[half * 64:half * 64 + 64, 512:1024], start=True, stop=True),
                   wait=((sA, None), (sV, None)), inc=((sT, 1),))
                op("vector", lambda e, hh=hh, sp=sp:
                   e.reduce_max(ROWS[0:1, 2304 + hh * 4 + sp: 2305 + hh * 4 + sp], P2[0:1, 0:512], AX.X),
                   wait=((sT, None),), inc=((sV, 1),))
            bEB.rel(ei, sV)
            bEB.free_at[ei].append(_last_dma)
        projection(FMB, ksl, DC, evk, first_wait=(kw, (sV, None)))
        op("vector", lambda e: e.reduce_max(ROWS[0:1, 2432:2448],
                                            ROWS[0:1, 2304:2368].rearrange("a (h s) -> a h s", s=4), AX.X),
           inc=((sV, 1),))
        bWH.rel(whi, sT)

        whi, whw = bWH.acq()
        vsl = WSB[:, whi * 8192: whi * 8192 + 8192]
        vw = load_wslab(vsl, fvw, DC, 1024, wait=whw)
        vtv = cci[0:1, KT_ELE:KT_ELE + VT_ELE].rearrange("a (p c) -> (a p) c", p=128)

        def evv(oc, sp, pa):
            ei, ew = bEB.acq()
            esl = EB[:, ei * 1024: ei * 1024 + 512]
            op("scalar", lambda e, esl=esl, pa=pa: e.copy(esl, pa),
               wait=((sT, None),) + ew, inc=((sA, 1),))

            def sink(t, pts, oc=oc):
                zi, zw = bZV.acq()
                zv = ZS[:, 6144 + zi * 256: 6144 + zi * 256 + 256]
                op("vector", lambda e, zv=zv, pts=pts: e.tensor_copy(zv[:, 0:64], pts[:, 0:64]),
                   wait=((sT, None),) + zw, inc=((sV, 1),))
                op("vector", lambda e, zv=zv, pts=pts: e.tensor_copy(zv[:, 65:129], pts[:, 64:128]), inc=((sV, 1),))
                op("vector", lambda e, zv=zv: e.memset(zv[:, 64:65], 1.0), inc=((sV, 1),))
                op("vector", lambda e, zv=zv: e.memset(zv[:, 129:130], 1.0), inc=((sV, 1),))
                _last_dma = dma("sync", vtv[:, t * (H * 65) + (2 * oc) * 65: t * (H * 65) + (2 * oc) * 65 + 130],
                    zv[:, 0:130], wait=((sV, None),))
                bZV.free_at[zi].append(_last_dma)
            transpose_to_tok(esl, sp, sink)
            bEB.rel(ei, sT)
        projection(FMB, vsl, DC, evv, first_wait=(vw, (sV, None)))
        bWH.rel(whi, sT)

        op("gpsimd", lambda e, cci=cci, cco=cco: e.collective_compute(
            "AllGather", ALU.bypass,
            replica_groups=[[0, 1], [2, 3], [4, 5], [6, 7]],
            ins=[cci.opt()], outs=[cco.opt()]),
           wait=(*ALLD(), *ALLG()), inc=((sCC, 1),))
        ccw = (sCC, sCC.n)

        whi, whw = bWH.acq()
        qsl = WSB[:, whi * 8192: whi * 8192 + 8192]
        qw = load_wslab(qsl, fq, DC, 1024, wait=whw)

        def evq(oc, sp, pa):
            op("scalar", lambda e, oc=oc, sp=sp, pa=pa:
               e.copy(QB[:, oc * 2048 + sp * 512: oc * 2048 + (sp + 1) * 512], pa),
               wait=((sT, None),), inc=((sA, 1),))
        projection(FMB, qsl, DC, evq, first_wait=(qw, (sV, None)))
        bWH.rel(whi, sA)

        rowdma = None
        rowd = []
        for h in range(H):
            base = (h % 2) * 64
            ch = h // 2
            for sp in range(4):
                op("tensor", lambda e, h=h, sp=sp, base=base, ch=ch:
                   e.matmul(P2[0:1, 0:512], SWCOL[base:base + 64, h:h + 1],
                            QB[base:base + 64, ch * 2048 + sp * 512: ch * 2048 + (sp + 1) * 512],
                            start=True, stop=True),
                   wait=((sA, None), (sV, None)), inc=((sT, 1),))
                w = ((sT, None),) if not (sp == 0 and rowdma) else ((sT, None), rowdma)
                op("scalar", lambda e, sp=sp:
                   e.activation(ROWS[0:1, 512 + sp * 512: 512 + (sp + 1) * 512], P2[0:1, 0:512], AF.Sigmoid),
                   wait=w, inc=((sA, 1),))
            rowdma = dma("sync", rowsd[8 + h:9 + h, 0:2048], ROWS[0:1, 512:2560], wait=((sA, None),))
            rowd.append(rowdma)
        for ch in range(DC):
            scw = dma("gpsimd", SCBB[:],
                      bass.AP(rowsd.tensor, (8 + 2 * ch) * 4096, [[4096, 2], [0, 64], [1, 2048]]),
                      wait=(rowd[2 * ch], rowd[2 * ch + 1], (sV, None)))
            op("vector", lambda e, ch=ch:
               e.scalar_tensor_tensor(QB[:, ch * 2048:(ch + 1) * 2048], QB[:, ch * 2048:(ch + 1) * 2048],
                                      0.25, SCBB[:], ALU.mult, ALU.mult),
               wait=(scw, (sA, None)), inc=((sV, 1),))

        for ch in range(DC):
            op("scalar", lambda e, ch=ch: e.activation(EB[:, 0:1024], QB[:, ch * 2048: ch * 2048 + 1024], AF.Square),
               wait=((sV, None), (sT, None), *ALLD()), inc=((sA, 1),))
            op("scalar", lambda e, ch=ch: e.activation(EB[:, 1024:2048], QB[:, ch * 2048 + 1024: ch * 2048 + 2048], AF.Square),
               inc=((sA, 1),))
            for half in range(2):
                hh = 2 * ch + half
                for sp in range(4):
                    op("tensor", lambda e, half=half, sp=sp:
                       e.matmul(P2[0:1, 0:512], ONE1[half * 64:half * 64 + 64, 0:1],
                                EB[half * 64:half * 64 + 64, sp * 512:(sp + 1) * 512], start=True, stop=True),
                       wait=((sA, None), (sV, None)), inc=((sT, 1),))
                    op("vector", lambda e, hh=hh, sp=sp:
                       e.reduce_max(ROWS[0:1, 2368 + hh * 4 + sp: 2369 + hh * 4 + sp], P2[0:1, 0:512], AX.X),
                       wait=((sT, None),), inc=((sV, 1),))
        op("vector", lambda e: e.reduce_max(ROWS[0:1, 2448:2464],
                                            ROWS[0:1, 2368:2432].rearrange("a (h s) -> a h s", s=4), AX.X),
           inc=((sV, 1),))
        op("vector", lambda e: e.tensor_mul(ROWS[0:1, 2464:2480], ROWS[0:1, 2432:2448], ROWS[0:1, 2448:2464]),
           inc=((sV, 1),))
        op("scalar", lambda e: e.activation(ROWS[0:1, 2480:2496], ROWS[0:1, 2464:2480], AF.Sqrt),
           wait=((sV, None),), inc=((sA, 1),))
        op("vector", lambda e: e.tensor_scalar_mul(ROWS[0:1, 2496:2512], ROWS[0:1, 2480:2496], -1.0),
           wait=((sA, None),), inc=((sV, 1),))
        cbw = bounce_bcast(ROWS[0:1, 2496:2512], 16, SM16[:, 32:48], 1, 128, wait=((sV, None),))
        dump(f"qt{ccid}", QB[:], ((sV, None),))
        dump(f"crow{ccid}", ROWS[0:1, 2432:2512], ((sV, None),))

        QRDY = ((sV, sV.n),)
        nmark = [()]
        ESL3 = [EB[:, 0:1024], EB[:, 1024:2048], SCR[:, 2048:3072]]
        esl_free = [(), (), ()]
        esl_cnt = [0]
        for h in range(H):
            khs = WSB[:, (h % 2) * 4096:(h % 2) * 4096 + 4096]
            vhs = WSB[:, 8192 + (h % 2) * 2080: 8192 + (h % 2) * 2080 + 2080]
            kw_ = []
            vw_ = []
            hwait = (ccw,) if h < 2 else (ccw, (sT, None))
            for r in range(2):
                src = cco[r:r + 1, 0:KT_ELE].rearrange("a (p c) -> (a p) c", p=64)[0:64, h * TOK:(h + 1) * TOK]
                kw_.append(dma("sync", khs[0:64, r * TOK:(r + 1) * TOK], src, wait=hwait))
                kw_[-1] = dma("sync", khs[64:128, r * TOK:(r + 1) * TOK], src, wait=hwait)
                vsrc = cco[r:r + 1, KT_ELE:KT_ELE + VT_ELE].rearrange("a (p c) -> (a p) c", p=128)
                vsrc = vsrc.rearrange("p (t x) -> p t x", x=H * 65)[:, :, h * 65:(h + 1) * 65]
                vw_.append(dma("sync", vhs[:, r * NT * 65:(r + 1) * NT * 65].rearrange("p (t x) -> p t x", x=65),
                               vsrc, wait=hwait))
            for qs in range(2):
                pend = []

                def emit_av(p, vhs=vhs):
                    pesl, pc, pm, pei = p
                    for half in range(2):
                        w = (pm, vw_[pc // NT]) if half == 0 else ()
                        if pc == 0 and half == 0:
                            w = w + (nmark[0] if nmark[0] else ())
                        op("tensor", lambda e, esl=pesl, half=half, c=pc, vhs=vhs:
                           e.matmul(P2[0:65, half * 512:(half + 1) * 512],
                                    vhs[:, c * 65:(c + 1) * 65],
                                    esl[:, half * 512:(half + 1) * 512],
                                    start=(c == 0), stop=(c == NC2 - 1)),
                           wait=w, inc=((sT, 1),) if half == 1 else ())
                    esl_free[pei] = ((sT, sT.n),)

                for c in range(NC2):
                    si, sw = bSP.acq()
                    sbank = [P0, P1, PT][si]
                    for half in range(2):
                        w = ()
                        if half == 0:
                            w = sw + (kw_[c // NT], cbw) + QRDY
                        op("tensor", lambda e, sbank=sbank, half=half, c=c, h=h, qs=qs, khs=khs:
                           e.matmul(sbank[:, half * 512:(half + 1) * 512],
                                    khs[(h % 2) * 64:(h % 2) * 64 + 64, c * 128:(c + 1) * 128],
                                    QB[(h % 2) * 64:(h % 2) * 64 + 64,
                                       (h // 2) * 2048 + qs * 1024 + half * 512:(h // 2) * 2048 + qs * 1024 + (half + 1) * 512],
                                    start=True, stop=True),
                           wait=w, inc=((sT, 1),) if half == 1 else ())
                    ei = esl_cnt[0] % 3
                    esl_cnt[0] += 1
                    ew = esl_free[ei]
                    esl = ESL3[ei]
                    op("scalar", lambda e, esl=esl, sbank=sbank, h=h:
                       e.activation(esl, sbank[:, 0:1024], AF.Exp, bias=SM16[:, 32 + h:33 + h], scale=1.0),
                       wait=((sT, None),) + ew, inc=((sA, 1),))
                    bSP.rel(si, sA)
                    pend.append((esl, c, (sA, sA.n), ei))
                    if len(pend) > 2:
                        emit_av(pend.pop(0))
                for p in pend:
                    emit_av(p)
                OSC = SCRF[:, 0:1024]
                oi, ow = bOS.acq()
                op("vector", lambda e, OSC=OSC: e.tensor_copy(OSC[0:65, :], P2[0:65, 0:1024]),
                   wait=((sT, None),) + ow, inc=((sV, 1),))
                nmark[0] = ((sV, sV.n),)
                if h == 0 and qs == 0:
                    dump(f"osc{ccid}", OSC[:], ((sV, None),))
                ri, rw = bRW.acq()
                rrow = ROWS[0:1, 512 + ri * 1024: 512 + ri * 1024 + 1024]
                op("vector", lambda e, OSC=OSC: e.tensor_copy(XB[0:1, 2048:3072], OSC[64:65, :]),
                   wait=rw, inc=((sV, 1),))
                op("vector", lambda e, rrow=rrow: e.reciprocal(rrow, XB[0:1, 2048:3072]),
                   inc=((sV, 1),))
                rbw = bounce_bcast(rrow, 1024, RCPB[0:64, ri * 1024: ri * 1024 + 1024], 2 + ri, 64,
                                   wait=((sV, None),), wait2=((sV, None),))
                if h == 0 and qs == 0:
                    dump(f"rcpb{ccid}", RCPB[0:64, 0:1024], (rbw,))
                op("vector", lambda e, h=h, qs=qs, ri=ri, OSC=OSC:
                   e.scalar_tensor_tensor(FMB[(h % 2) * 64:(h % 2) * 64 + 64,
                                              (h // 2) * 2048 + qs * 1024:(h // 2) * 2048 + (qs + 1) * 1024],
                                          OSC[0:64, :], 1.0, RCPB[0:64, ri * 1024: ri * 1024 + 1024],
                                          ALU.mult, ALU.mult),
                   wait=(rbw,), inc=((sV, 1),))
                bRW.rel(ri, sV)
                bOS.rel(oi, sV)

        dump(f"ofm{ccid}", FMB[:], ((sV, None),))
        whi, whw = bWH.acq()
        gsl = WSB[:, whi * 8192: whi * 8192 + 8192]
        gw = load_wslab(gsl, fgw, DC, 1024, wait=whw + ((sV, None),))
        gbw = dma("sync", SM16[:, 72:80], fgbT, wait=())

        GBUF = SCR[:, 0:16384]

        def evg(oc, sp, pa):
            op("scalar", lambda e, pa=pa, oc=oc, sp=sp:
               e.activation(GBUF[:, oc * 2048 + sp * 512: oc * 2048 + (sp + 1) * 512], pa,
                            AF.Sigmoid, bias=SM16[:, 72 + oc:73 + oc]),
               wait=((sT, None), gbw), inc=((sA, 1),))
        projection(FMB, gsl, DC, evg, first_wait=(gw,))
        for oc in range(DC):
            op("vector", lambda e, oc=oc:
               e.tensor_mul(FMB[:, oc * 2048:(oc + 1) * 2048],
                            FMB[:, oc * 2048:(oc + 1) * 2048], GBUF[:, oc * 2048:(oc + 1) * 2048]),
               wait=((sT, None), (sA, None)) if oc == 0 else (), inc=((sV, 1),))
        bWH.rel(whi, sT)

        whi, whw = bWH.acq()
        osl_w = WSB[:, whi * 8192: whi * 8192 + 8192]
        oww = load_wslab(osl_w, fow, DC, 1024, wait=whw + ((sV, None),))

        def evo(oc, sp, pa):
            ei, ew = bEB.acq()
            esl = EB[:, ei * 1024: ei * 1024 + 512]
            op("scalar", lambda e, esl=esl, pa=pa: e.copy(esl, pa),
               wait=((sT, None),) + ew, inc=((sA, 1),))

            def sink(t, pts, oc=oc):
                op("vector", lambda e, t=t, oc=oc, pts=pts:
                   e.tensor_copy(QB[:, t * 1024 + oc * 128: t * 1024 + (oc + 1) * 128], pts),
                   wait=((sT, None),), inc=((sV, 1),))
            transpose_to_tok(esl, sp, sink)
            bEB.rel(ei, sT)
        projection(FMB, osl_w, DC, evo, first_wait=(oww,))
        bWH.rel(whi, sT)

        n1gw = dma("gpsimd", NGB[:, 0:1024], n1g.broadcast_to([128, D]), wait=((sV, None),))
        n1bw = dma("gpsimd", NGB[:, 1024:2048], n1b.broadcast_to([128, D]))
        for t in range(NT):
            i, w = bXB.acq()
            xt = XB[:, i * 1024:(i + 1) * 1024]
            xw = dma("sync", xt, src_dram[t * 128:(t + 1) * 128, :], wait=w + (((sV, None), (sA, None)) if t == 0 else ()))
            si0, rstd0, negm0 = ln_stats(xt, twait=(xw,))
            s1 = ZS[:, (t % 2) * 1024:(t % 2) * 1024 + 1024]
            op("scalar", lambda e, xt=xt, s1=s1, rstd0=rstd0, negm0=negm0:
               e.activation(s1, xt, AF.Identity, bias=negm0, scale=rstd0),
               wait=((sV, None), (sT, None)), inc=((sA, 1),))
            bST.rel(si0, sA)
            op("vector", lambda e, xt=xt, s1=s1, t=t:
               e.tensor_add(xt, s1, QB[:, t * 1024:(t + 1) * 1024]),
               wait=((sA, None),), inc=((sV, 1),))
            si, rstd, negm = ln_stats(xt)
            t1 = T1F
            op("vector", lambda e, xt=xt, t1=t1, rstd=rstd:
               e.scalar_tensor_tensor(t1, xt, rstd, NGB[:, 0:1024], ALU.mult, ALU.mult),
               wait=(n1gw, n1bw) if t == 0 else (), inc=((sV, 1),))
            op("vector", lambda e, xt=xt, negm=negm:
               e.scalar_tensor_tensor(xt, NGB[:, 0:1024], negm, NGB[:, 1024:2048], ALU.mult, ALU.add),
               inc=((sV, 1),))
            op("vector", lambda e, xt=xt, t1=t1, s1=s1:
               e.tensor_add(s1, t1, xt), inc=((sV, 1),))
            bST.rel(si, sV)
            bXB.rel(i, sV)
            for ch in range(DC):
                pc = (ch % 2) * 512
                op("tensor", lambda e, s1=s1, ch=ch, pc=pc:
                   e.matmul(PT[:, pc:pc + 128], s1[:, ch * 128:(ch + 1) * 128], IDB[:], start=True, stop=True),
                   wait=((sV, None),), inc=((sT, 1),))
                op("vector", lambda e, t=t, ch=ch, pc=pc:
                   e.tensor_copy(FMB[:, ch * 2048 + t * 128: ch * 2048 + (t + 1) * 128], PT[:, pc:pc + 128]),
                   wait=((sT, None),), inc=((sV, 1),))

        b2_w = dma("gpsimd", B2S[:], b2, wait=((sV, None),))
        dump(f"x1fm{ccid}", FMB[:], ((sV, None),))
        dump(f"ao{ccid}", QB[:], ((sV, None),))
        for t in range(NT):
            for k in range(DC):
                op("tensor", lambda e, t=t, k=k:
                   e.matmul(P2[:, 0:8], FMB[:, k * 2048 + t * 128: k * 2048 + (t + 1) * 128],
                            MGWS[:, k * 8:(k + 1) * 8], start=(k == 0), stop=(k == DC - 1)),
                   wait=((sV, None), *ALLG()) if k == 0 else (), inc=((sT, 1),) if k == DC - 1 else ())
            L = GSC[:, 0:8]
            wk_ = GSC[:, 8:16]; idsc = GSC[:, 16:24]
            m1 = GSC[:, 40:41]; m1n = GSC[:, 41:42]; m2 = GSC[:, 42:43]
            e2 = GSC[:, 43:44]; s_ = GSC[:, 44:45]; rcp = GSC[:, 45:46]; cmx = GSC[:, 46:47]
            sw_ = SWB[:, t:t + 1]
            op("vector", lambda e, L=L: e.tensor_add(L, P2[:, 0:8], SM16[:, 64:72]),
               wait=((sT, None),), inc=((sV, 1),))
            op("vector", lambda e, L=L, m1=m1: e.reduce_max(m1, L, AX.X),
               wait=((sA, None),), inc=((sV, 1),))
            op("vector", lambda e, m1=m1, m1n=m1n: e.tensor_scalar_mul(m1n, m1, -1.0), inc=((sV, 1),))
            op("vector", lambda e, L=L, m1=m1, wk_=wk_: e.tensor_scalar(wk_, L, m1, None, ALU.is_ge), inc=((sV, 1),))
            op("vector", lambda e, L=L, wk_=wk_: e.scalar_tensor_tensor(wk_, wk_, -1.0e30, L, ALU.mult, ALU.add), inc=((sV, 1),))
            op("vector", lambda e, m2=m2, wk_=wk_: e.reduce_max(m2, wk_, AX.X), inc=((sV, 1),))
            op("scalar", lambda e, m2=m2, m1n=m1n, e2=e2: e.activation(e2, m2, AF.Exp, bias=m1n),
               wait=((sV, None),), inc=((sA, 1),))
            op("vector", lambda e, e2=e2, s_=s_: e.tensor_scalar_add(s_, e2, 1.0),
               wait=((sA, None),), inc=((sV, 1),))
            op("vector", lambda e, s_=s_, rcp=rcp: e.reciprocal(rcp, s_), inc=((sV, 1),))
            op("vector", lambda e, e2=e2, rcp=rcp, sw_=sw_: e.scalar_tensor_tensor(sw_, e2, rcp, rcp, ALU.mult, ALU.add), inc=((sV, 1),))
            op("vector", lambda e, L=L, m2=m2, wk_=wk_: e.tensor_scalar(wk_, L, m2, None, ALU.is_ge), inc=((sV, 1),))
            op("vector", lambda e, wk_=wk_, idsc=idsc: e.scalar_tensor_tensor(idsc, wk_, 1000.0, SM16[:, 48:56], ALU.mult, ALU.add), inc=((sV, 1),))
            op("vector", lambda e, idsc=idsc, cmx=cmx: e.reduce_max(cmx, idsc, AX.X), inc=((sV, 1),))
            op("vector", lambda e, idsc=idsc, cmx=cmx, wk_=wk_: e.tensor_scalar(wk_, idsc, cmx, None, ALU.is_ge), inc=((sV, 1),))
            op("vector", lambda e, wk_=wk_, sw_=sw_: e.tensor_scalar(wk_, wk_, sw_, None, ALU.mult), inc=((sV, 1),))
            op("vector", lambda e, wk_=wk_, t=t: e.tensor_mul(FCO[:, t * 8:(t + 1) * 8], wk_, SM16[:, 56:64]), inc=((sV, 1),))
            op("vector", lambda e, t=t: e.tensor_copy(FCB[:], FCO[:, t * 8:(t + 1) * 8]), inc=((sV, 1),))
            op("tensor", lambda e, t=t: e.matmul(PT[0:8, 0:128], FCB[:], IDB[:], start=True, stop=True),
               wait=((sV, None), (sA, None)), inc=((sT, 1),))
            op("vector", lambda e, t=t: e.tensor_copy(CTB[:, t * 128:(t + 1) * 128], PT[0:8, 0:128]),
               wait=((sT, None),), inc=((sV, 1),))

        ACC = QB
        last_hh_sT = [0]
        last_w2_sT = [0]
        zmark_par = [(), ()]
        evacmarkB = [()]
        trgmark = [()]
        for e_i in range(EE):
            w1w = load_wslab(WSA, w1[e_i], DC, 2048,
                             wait=((sT, last_hh_sT[0]),) if e_i > 0 else ((sT, None),))
            w2w = load_wslab(WSB, w2[e_i], FC, 1024,
                             wait=((sT, last_w2_sT[0]),) if e_i > 0 else ((sT, None),))
            b1i, b1iw = bB1.acq()
            b1t = B1T[:, b1i * 2048:(b1i + 1) * 2048]
            b1w = dma("gpsimd", b1t, b1[e_i:e_i + 1, :].broadcast_to([128, FF]), wait=b1iw)
            lgi, lgiw = bLNG.acq()
            lg = LNGB[:, lgi * 32: lgi * 32 + 16]
            lb = LNGB[:, lgi * 32 + 16: lgi * 32 + 32]
            lgw = dma("sync", lg, lngT[e_i], wait=lgiw)
            lbw = dma("sync", lb, lnbT[e_i])
            for quad in range(4):
                ts = [quad * 4 + j for j in range(4)]
                gend = {}
                evacm = {}

                def hh_tile(j, quad=quad):
                    t = ts[j]
                    par = j % 2
                    BA, BB = (P0, P1) if par == 0 else (P2, PT)
                    aim = ()
                    if e_i == 0:
                        for half in range(2):
                            op("tensor", lambda e, t=t, half=half:
                               e.matmul(P2[:, half * 512:(half + 1) * 512], CTB[:, t * 128:(t + 1) * 128],
                                        B2S[:, half * 512:(half + 1) * 512], start=True, stop=True),
                               wait=(b2_w, (sV, None), (sA, None)) if half == 0 else (), inc=((sT, 1),))
                        for ch in range(DC):
                            pc = (ch % 2) * 512
                            op("tensor", lambda e, t=t, ch=ch, pc=pc:
                               e.matmul(PT[:, pc:pc + 128], FMB[:, ch * 2048 + t * 128: ch * 2048 + (t + 1) * 128], IDB[:], start=True, stop=True),
                               wait=((sV, None), (sA, None)), inc=((sT, 1),))
                            op("vector", lambda e, ch=ch, pc=pc:
                               e.tensor_copy(XRS[:, ch * 128:(ch + 1) * 128], PT[:, pc:pc + 128]),
                               wait=((sT, None),), inc=((sV, 1),))
                        for half in range(2):
                            op("vector", lambda e, t=t, half=half:
                               e.scalar_tensor_tensor(ACC[:, t * 1024 + half * 512: t * 1024 + (half + 1) * 512],
                                                      XRS[:, half * 512:(half + 1) * 512], SWB[:, t:t + 1],
                                                      P2[:, half * 512:(half + 1) * 512], ALU.mult, ALU.add),
                               inc=((sV, 1),))
                        aim = ((sV, sV.n),)
                    w_extra = ()
                    if j in (0, 1):
                        w_extra = (trgmark[0] if trgmark[0] else ()) + (evacmarkB[0] if evacmarkB[0] else ())
                    elif j == 3:
                        w_extra = ((sA, gend[1]), (sV, evacm[1]))
                    for k in range(DC):
                        for fs in range(4):
                            bank = [BA, BA, BB, BB][fs]
                            pa = bank[:, (fs % 2) * 512:(fs % 2) * 512 + 512]
                            w = ()
                            if k == 0 and fs == 0:
                                w = (w1w,) + aim + (zmark_par[par] if zmark_par[par] else ((sA, None), (sV, None))) + w_extra
                            op("tensor", lambda e, pa=pa, k=k, fs=fs, t=t:
                               e.matmul(pa, FMB[:, k * 2048 + t * 128: k * 2048 + (t + 1) * 128],
                                        WSA[:, k * 2048 + fs * 512: k * 2048 + (fs + 1) * 512],
                                        start=(k == 0), stop=(k == DC - 1)),
                               wait=w, inc=((sT, 1),) if (k == DC - 1 and fs == 3) else ())
                    last_hh_sT[0] = sT.n
                    op("vector", lambda e, b1t=b1t, BA=BA: e.tensor_add(BA[:, 0:1024], BA[:, 0:1024], b1t[:, 0:1024]),
                       wait=((sT, None), b1w), inc=((sV, 1),))
                    op("vector", lambda e, b1t=b1t, BB=BB: e.tensor_add(BB[:, 0:1024], BB[:, 0:1024], b1t[:, 1024:2048]),
                       inc=((sV, 1),))
                    si, rstd, negm = ln_stats4([BA[:, 0:512], BA[:, 512:1024], BB[:, 0:512], BB[:, 512:1024]])
                    zsl = ZS[:, j * 2048:(j + 1) * 2048]
                    op("scalar", lambda e, zsl=zsl, rstd=rstd, negm=negm, BA=BA:
                       e.activation(zsl[:, 0:1024], BA[:, 0:1024], AF.Identity, bias=negm, scale=rstd),
                       wait=((sV, None),), inc=((sA, 1),))
                    op("scalar", lambda e, zsl=zsl, rstd=rstd, negm=negm, BB=BB:
                       e.activation(zsl[:, 1024:2048], BB[:, 0:1024], AF.Identity, bias=negm, scale=rstd),
                       inc=((sA, 1),))
                    bST.rel(si, sA)
                    zmark_par[par] = ((sA, sA.n),)

                def blk(j, trbank, w2bank, quad=quad):
                    t = ts[j]
                    gmark = {}
                    if j == 0:
                        trw = zmark_par[1]
                        w2r = zmark_par[1]
                    elif j == 1:
                        trw = ((sV, evacm[0]),) + zmark_par[1]
                        w2r = ((sA, gend[0]),) + zmark_par[1]
                    elif j == 2:
                        trw = zmark_par[0]
                        w2r = zmark_par[0]
                    else:
                        trw = ((sV, evacm[2]),) + zmark_par[1]
                        w2r = ((sA, gend[2]),)
                    slots = [trbank[:, 0:512], trbank[:, 512:1024]]

                    def w2_mm(fc, first_w):
                        for half in range(2):
                            w = first_w if half == 0 else ()
                            op("tensor", lambda e, fc=fc, half=half, j=j, w2bank=w2bank:
                               e.matmul(w2bank[:, half * 512:(half + 1) * 512],
                                        H2T[:, 8192 + fc * 512 + j * 128: 8192 + fc * 512 + (j + 1) * 128],
                                        WSB[:, fc * 1024 + half * 512: fc * 1024 + (half + 1) * 512],
                                        start=(fc == 0), stop=(fc == FC - 1)),
                               wait=w, inc=((sT, 1),) if (fc == FC - 1 and half == 1) else ())

                    for fc in range(FC):
                        sl = slots[fc % 2]
                        w = ()
                        if fc == 0:
                            w = trw
                        elif fc >= 2:
                            w = ((sA, gmark[fc - 2]),)
                        op("tensor", lambda e, j=j, fc=fc, sl=sl:
                           e.matmul(sl[:, 0:128], ZS[:, j * 2048 + fc * 128: j * 2048 + (fc + 1) * 128], IDB[:], start=True, stop=True),
                           wait=w, inc=((sT, 1),))
                        op("scalar", lambda e, fc=fc, lg=lg, lb=lb, sl=sl, j=j:
                           e.activation(H2T[:, 8192 + fc * 512 + j * 128: 8192 + fc * 512 + (j + 1) * 128], sl[:, 0:128], AF.Gelu,
                                        bias=lb[:, fc:fc + 1], scale=lg[:, fc:fc + 1]),
                           wait=((sT, None), lgw, lbw), inc=((sA, 1),))
                        gmark[fc] = sA.n
                        if fc >= 1:
                            fw = ((sA, gmark[fc - 1]),)
                            if fc == 1:
                                fw = fw + (w2w,) + w2r
                            w2_mm(fc - 1, fw)
                    w2_mm(FC - 1, ((sA, gmark[FC - 1]),))
                    last_w2_sT[0] = sT.n
                    for half in range(2):
                        op("vector", lambda e, t=t, half=half, e_i=e_i, w2bank=w2bank:
                           e.scalar_tensor_tensor(ACC[:, t * 1024 + half * 512: t * 1024 + (half + 1) * 512],
                                                  w2bank[:, half * 512:(half + 1) * 512], FCO[:, t * 8 + e_i: t * 8 + e_i + 1],
                                                  ACC[:, t * 1024 + half * 512: t * 1024 + (half + 1) * 512],
                                                  ALU.mult, ALU.add),
                           wait=((sT, None),) if half == 0 else (), inc=((sV, 1),))
                    evacm[j] = sV.n
                    gend[j] = gmark[FC - 1]

                hh_tile(0)
                hh_tile(1)
                hh_tile(2)
                blk(0, PT, P2)
                blk(1, P2, PT)
                hh_tile(3)
                blk(2, P1, P0)
                blk(3, P0, P1)
                evacmarkB[0] = ((sV, evacm[3]),)
                trgmark[0] = ((sA, gend[3]),)
            bB1.rel(b1i, sV)
            bLNG.rel(lgi, sA)

        dump(f"acc{ccid}", QB[:], ((sV, None),))
        dump(f"fco{ccid}", FCO[:], ((sV, None),))
        dump(f"swb{ccid}", SWB[:], ((sV, None),))
        n2gw = dma("gpsimd", NGB[:, 0:1024], n2g.broadcast_to([128, D]), wait=((sV, None),))
        n2bw = dma("gpsimd", NGB[:, 1024:2048], n2b.broadcast_to([128, D]))
        for t in range(NT):
            for ch in range(DC):
                pc = (ch % 2) * 512
                op("tensor", lambda e, t=t, ch=ch, pc=pc:
                   e.matmul(PT[:, pc:pc + 128], FMB[:, ch * 2048 + t * 128: ch * 2048 + (t + 1) * 128], IDB[:], start=True, stop=True),
                   wait=((sV, None),), inc=((sT, 1),))
                op("vector", lambda e, ch=ch, pc=pc:
                   e.tensor_copy(XRS[:, ch * 128:(ch + 1) * 128], PT[:, pc:pc + 128]),
                   wait=((sT, None),), inc=((sV, 1),))
            i, w = bXB.acq()
            z2 = XB[:, i * 1024:(i + 1) * 1024]
            op("vector", lambda e, z2=z2, t=t: e.tensor_add(z2, XRS[:], ACC[:, t * 1024:(t + 1) * 1024]),
               wait=w, inc=((sV, 1),))
            si, rstd, negm = ln_stats(z2)
            t1 = T1F
            op("vector", lambda e, z2=z2, t1=t1, rstd=rstd:
               e.scalar_tensor_tensor(t1, z2, rstd, NGB[:, 0:1024], ALU.mult, ALU.mult),
               wait=(n2gw, n2bw) if t == 0 else (), inc=((sV, 1),))
            op("vector", lambda e, z2=z2, negm=negm:
               e.scalar_tensor_tensor(z2, NGB[:, 0:1024], negm, NGB[:, 1024:2048], ALU.mult, ALU.add),
               inc=((sV, 1),))
            bST.rel(si, sV)
            i2, w2_ = bXB.acq()
            xo = XB[:, i2 * 1024:(i2 + 1) * 1024]
            xw = dma("sync", xo, src_dram[t * 128:(t + 1) * 128, :], wait=w2_)
            op("vector", lambda e, z2=z2, t1=t1: e.tensor_add(t1, t1, z2), inc=((sV, 1),))
            op("vector", lambda e, t1=t1, xo=xo:
               e.scalar_tensor_tensor(xo, t1, 0.5, xo, ALU.mult, ALU.add),
               wait=(xw,), inc=((sV, 1),))
            _last_dma = dma("sync", dst_dram[t * 128:(t + 1) * 128, :], xo, wait=((sV, None),))
            bXB.rel(i, sV)
            bXB.free_at[i2].append(_last_dma)

    def attn(src_dram, dst_dram):
        cci, cco = cc_in[1], cc_out[1]
        ln0(src_dram, FMB)
        aw1 = dma("sync", KRELS[:], krel, wait=((sT, None), (sV, None), (sA, None)))
        dma("sync", QRELS[:], qrel)
        KCOL = ROWS[:, 256:288]
        BIASB = ROWS[:, 288:352]
        kclw = dma("sync", KCOL, kcol)
        op("vector", lambda e: e.tensor_scalar_mul(SM16[:, 32:48], SM16[:, 0:16], -1.0),
           wait=((sA, None),), inc=((sV, 1),))
        dma("sync", AMS.rearrange("p (a c) -> p a c", c=128), amask.rearrange("a p c -> p a c"))
        dma("sync", CMS.rearrange("p (a c) -> p a c", c=128), cmask.rearrange("a p c -> p a c"))
        ACW_T = ALLD()
        dump("krels", KRELS[:], ALLD())
        dump("qrels", QRELS[:], ALLD())

        op("vector", lambda e: e.tensor_scalar_mul(ROWS[0:1, 96:112], ROWS[0:1, 16:32], 8.0), inc=((sV, 1),))
        op("vector", lambda e: e.reciprocal(ROWS[0:1, 32:48], ROWS[0:1, 96:112]),
           inc=((sV, 1),))
        qw_ = dma("sync", ROWS[0:1, 48:64], qpa)
        op("scalar", lambda e: e.activation(ROWS[0:1, 64:80], ROWS[0:1, 48:64], AF.Sin),
           wait=(qw_,), inc=((sA, 1),))
        op("scalar", lambda e: e.activation(ROWS[0:1, 80:96], ROWS[0:1, 48:64], AF.Sin, bias=PI2C[0:1, 0:1]),
           inc=((sA, 1),))
        w1_ = dma("sync", rowsd[0:1, 0:16], ROWS[0:1, 64:80], wait=((sA, None),))
        w2_ = dma("sync", rowsd[0:1, 16:32], ROWS[0:1, 80:96], wait=((sA, None),))
        w3_ = dma("sync", rowsd[0:1, 32:48], ROWS[0:1, 32:48], wait=((sA, None),))
        wsin = dma("sync", SM16[:, 80:96], rowsd[0:1, 0:16].broadcast_to([128, 16]),
                   wait=(w1_, (sA, None), (sT, None), (sV, None)))
        wcos = dma("sync", SM16[:, 96:112], rowsd[0:1, 16:32].broadcast_to([128, 16]), wait=(w2_,))
        wrs8 = dma("sync", SM16[:, 112:128], rowsd[0:1, 32:48].broadcast_to([128, 16]), wait=(w3_,))

        wqrot = WSA
        for k in range(4):
            wqw = dma("sync", WQSTG[:, 0:1024], wq[k * 128:(k + 1) * 128, :], wait=((sV, None),))
            for h in range(H):
                wq_h = WQSTG[:, h * 64:(h + 1) * 64]
                ev = wq_h.rearrange("p (a b) -> p a b", b=2)[:, :, 0].rearrange("p a -> p a")
                od = wq_h.rearrange("p (a b) -> p a b", b=2)[:, :, 1].rearrange("p a -> p a")
                cc_ = SM16[:, 96 + h:97 + h]; ss_ = SM16[:, 80 + h:81 + h]; rr_ = SM16[:, 112 + h:113 + h]
                t1 = WQSTG[:, 1024:1056]; t2 = WQSTG[:, 1056:1088]
                ww = (wqw, wsin, wcos, wrs8, (sT, None)) if (h == 0) else ()
                op("vector", lambda e, ev=ev, cc_=cc_, t1=t1: e.tensor_scalar_mul(t1, ev, cc_), wait=ww, inc=((sV, 1),))
                op("vector", lambda e, od=od, ss_=ss_, t2=t2: e.tensor_scalar_mul(t2, od, ss_), inc=((sV, 1),))
                op("vector", lambda e, t1=t1, t2=t2: e.tensor_sub(t1, t1, t2), inc=((sV, 1),))
                op("vector", lambda e, t1=t1, rr_=rr_: e.tensor_scalar_mul(t1, t1, rr_), inc=((sV, 1),))
                op("vector", lambda e, ev=ev, ss_=ss_, t2=t2: e.tensor_scalar_mul(t2, ev, ss_), inc=((sV, 1),))
                op("vector", lambda e, od=od, cc_=cc_, od2=WQSTG[:, 1088:1120]: e.tensor_scalar_mul(od2, od, cc_), inc=((sV, 1),))
                op("vector", lambda e, t2=t2, od2=WQSTG[:, 1088:1120]: e.tensor_add(t2, t2, od2), inc=((sV, 1),))
                op("vector", lambda e, t2=t2, rr_=rr_: e.tensor_scalar_mul(t2, t2, rr_), inc=((sV, 1),))
                wr_ev = wqrot[:, k * 1024 + h * 64:(k * 1024) + (h + 1) * 64].rearrange("p (a b) -> p a b", b=2)[:, :, 0].rearrange("p a -> p a")
                wr_od = wqrot[:, k * 1024 + h * 64:(k * 1024) + (h + 1) * 64].rearrange("p (a b) -> p a b", b=2)[:, :, 1].rearrange("p a -> p a")
                op("vector", lambda e, t1=t1, wr_ev=wr_ev: e.tensor_copy(wr_ev, t1), inc=((sV, 1),))
                op("vector", lambda e, t2=t2, wr_od=wr_od: e.tensor_copy(wr_od, t2), inc=((sV, 1),))

        wkw = dma("gpsimd", WSA[:, 4096:5120].rearrange("p (c o) -> p c o", o=256),
                  wk.rearrange("(c p) o -> p c o", p=128), wait=((sT, None),))
        wvw = dma("gpsimd", WSA[:, 5120:6144].rearrange("p (c o) -> p c o", o=256),
                  wv.rearrange("(c p) o -> p c o", p=128))
        wks = WSA[:, 4096:5120]
        wvs = WSA[:, 5120:6144]
        ktv = cci[0:1, 0:KT2_ELE].rearrange("a (p c) -> (a p) c", p=64)
        vtv = cci[0:1, KT2_ELE:KT2_ELE + VT2_ELE].rearrange("a (p c) -> (a p) c", p=128)
        for oc in range(2):
            for sp in range(4):
                pa = [P0, P1][(oc * 4 + sp) % 2][:, 0:512]
                for k in range(4):
                    w = ((wkw, (sA, None), (sV, None)) if k == 0 else ())
                    op("tensor", lambda e, pa=pa, k=k, oc=oc, sp=sp:
                       e.matmul(pa, wks[:, k * 256 + oc * 128: k * 256 + (oc + 1) * 128],
                                FMB[:, (4 + k) * 2048 + sp * 512: (4 + k) * 2048 + (sp + 1) * 512],
                                start=(k == 0), stop=(k == 3)),
                       wait=w, inc=((sT, 1),) if k == 3 else ())
                ei, ew = bEB.acq()
                esl = EB[:, ei * 1024: ei * 1024 + 512]
                op("scalar", lambda e, esl=esl, pa=pa: e.copy(esl, pa),
                   wait=((sT, None),) + ew, inc=((sA, 1),))
                for half in range(2):
                    kvh = 2 * oc + half
                    _last_dma = dma("sync", ktv[0:64, kvh * TOK + sp * 512: kvh * TOK + (sp + 1) * 512],
                        esl[half * 64:half * 64 + 64, 0:512], wait=((sA, None),))
                bEB.free_at[ei].append(_last_dma)
        for oc in range(2):
            for sp in range(4):
                pa = [P0, P1][(oc * 4 + sp) % 2][:, 512:1024]
                for k in range(4):
                    w = ((wvw, (sA, None), (sV, None)) if k == 0 else ())
                    op("tensor", lambda e, pa=pa, k=k, oc=oc, sp=sp:
                       e.matmul(pa, wvs[:, k * 256 + oc * 128: k * 256 + (oc + 1) * 128],
                                FMB[:, k * 2048 + sp * 512: k * 2048 + (sp + 1) * 512],
                                start=(k == 0), stop=(k == 3)),
                       wait=w, inc=((sT, 1),) if k == 3 else ())
                ei, ew = bEB.acq()
                esl = EB[:, ei * 1024: ei * 1024 + 512]
                op("scalar", lambda e, esl=esl, pa=pa: e.copy(esl, pa),
                   wait=((sT, None),) + ew, inc=((sA, 1),))

                def sink(t, pts, oc=oc):
                    zi, zw = bZV.acq()
                    zv = ZS[:, 16384 + zi * 256: 16384 + zi * 256 + 256]
                    op("vector", lambda e, zv=zv, pts=pts: e.tensor_copy(zv[:, 0:64], pts[:, 0:64]),
                       wait=((sT, None),) + zw, inc=((sV, 1),))
                    op("vector", lambda e, zv=zv, pts=pts: e.tensor_copy(zv[:, 65:129], pts[:, 64:128]), inc=((sV, 1),))
                    op("vector", lambda e, zv=zv: e.memset(zv[:, 64:65], 1.0), inc=((sV, 1),))
                    op("vector", lambda e, zv=zv: e.memset(zv[:, 129:130], 1.0), inc=((sV, 1),))
                    _last_dma = dma("sync", vtv[:, t * (KVH * 65) + (2 * oc) * 65: t * (KVH * 65) + (2 * oc) * 65 + 130],
                        zv[:, 0:130], wait=((sV, None),))
                    bZV.free_at[zi].append(_last_dma)
                transpose_to_tok(esl, sp, sink)
                bEB.rel(ei, sT)
        op("gpsimd", lambda e: e.collective_compute(
            "AllGather", ALU.bypass,
            replica_groups=[[0, 1], [2, 3], [4, 5], [6, 7]],
            ins=[cci.opt()], outs=[cco.opt()]),
           wait=(*ALLD(), *ALLG()), inc=((sCC, 1),))
        ccw = (sCC, sCC.n)

        for oc in range(DC):
            for sp in range(4):
                idx = oc * 4 + sp
                bank = [P0, P1][idx % 2]
                pa = bank[:, 0:512] if (idx % 4) < 2 else bank[:, 512:1024]
                for k in range(4):
                    w = ((sV, None), (sA, None)) if k == 0 else ()
                    op("tensor", lambda e, pa=pa, k=k, oc=oc, sp=sp:
                       e.matmul(pa, wqrot[:, k * 1024 + oc * 128: k * 1024 + (oc + 1) * 128],
                                FMB[:, (4 + k) * 2048 + sp * 512: (4 + k) * 2048 + (sp + 1) * 512],
                                start=(k == 0), stop=(k == 3)),
                       wait=w, inc=((sT, 1),) if k == 3 else ())
                op("scalar", lambda e, pa=pa, oc=oc, sp=sp:
                   e.copy(QB[:, oc * 2048 + sp * 512: oc * 2048 + (sp + 1) * 512], pa),
                   wait=((sT, None),), inc=((sA, 1),))

        khA = WSB
        for kvh in range(KVH):
            pcb = kvh * 4096
            for r in range(2):
                src = cco[r:r + 1, 0:KT2_ELE].rearrange("a (p c) -> (a p) c", p=64)[0:64, kvh * TOK:(kvh + 1) * TOK]
                src = src.rearrange("p (lt c) -> p lt c", c=128)
                for hb in range(2):
                    dst = khA[hb * 64:hb * 64 + 64, pcb:pcb + 4096].rearrange("p (lt c) -> p lt c", c=256)
                    dma("sync", dst[:, :, r * 128:(r + 1) * 128], src, wait=(ccw, (sT, None)))
                vsrc = cco[r:r + 1, KT2_ELE:KT2_ELE + VT2_ELE].rearrange("a (p c) -> (a p) c", p=128)
                vsrc = vsrc.rearrange("p (t x) -> p t x", x=KVH * 65)[:, :, kvh * 65:(kvh + 1) * 65]
                vdst = vAt[:, 0:8320].rearrange("p (cc x) -> p cc x", x=260)[:, :, kvh * 65:(kvh + 1) * 65]
                vdst = vdst.rearrange("p (lt two) x -> p lt two x", two=2)[:, :, r, :]
                dma("sync", vdst, vsrc, wait=(ccw, (sT, None), (sA, None)))
        kvw = ALLD()

        dump("qa", QB[:], ((sA, None),))
        dump("kha", WSB[:], (*ALLD(),))
        MPW = ((sA, sA.n), (sV, sV.n))
        for t in range(NT):
            npieces = (2 * t + 2 + 7) // 8
            for h in range(H):
                kvh = h // 4
                prow = (h % 2) * 64
                pcb = kvh * 4096
                for p_i in range(npieces):
                    klo = p_i * 1024
                    khi = min((2 * t + 2) * 128, klo + 1024)
                    si, sw = bSP.acq()
                    bank = [P0, P1, PT][si]
                    a = klo
                    first = True
                    while a < khi:
                        b_ = min(a + 512, khi)
                        w = kvw + ACW_T + MPW + sw if first else ()
                        op("tensor", lambda e, bank=bank, a=a, b_=b_, klo=klo, h=h, t=t, prow=prow, pcb=pcb:
                           e.matmul(bank[:, a - klo: b_ - klo],
                                    QB[prow:prow + 64, (h // 2) * 2048 + t * 128:(h // 2) * 2048 + (t + 1) * 128],
                                    khA[prow:prow + 64, pcb + a: pcb + b_],
                                    start=True, stop=False),
                           wait=w)
                        op("tensor", lambda e, bank=bank, a=a, b_=b_, klo=klo, t=t:
                           e.matmul(bank[:, a - klo: b_ - klo],
                                    QRELS[:, t * 128:(t + 1) * 128], KRELS[:, a:b_],
                                    start=False, stop=True),
                           inc=((sT, 1),))
                        first = False
                        a = b_
                    if khi == (2 * t + 2) * 128:
                        off = (khi - klo) - 256
                        op("vector", lambda e, bank=bank, off=off:
                           e.tensor_add(bank[:, off:off + 256], bank[:, off:off + 256], AMS[:]),
                           wait=((sT, None),), inc=((sV, 1),))
                    op("vector", lambda e, bank=bank, p_i=p_i, khi=khi, klo=klo:
                       e.reduce_max(GSC[:, 24 + p_i: 25 + p_i], bank[:, 0:khi - klo], AX.X),
                       wait=((sT, None),), inc=((sV, 1),))
                    bSP.rel(si, sV)
                op("vector", lambda e, npieces=npieces, h=h, t=t:
                   e.reduce_max(MBUF[:, h * 16 + t: h * 16 + t + 1], GSC[:, 24:24 + npieces], AX.X),
                   inc=((sV, 1),))
        for h in range(H):
            op("vector", lambda e, h=h: e.reduce_max(MH16[:, h:h + 1], MBUF[:, h * 16:(h + 1) * 16], AX.X),
               inc=((sV, 1),))
        op("vector", lambda e: e.tensor_mul(MH16[:], MH16[:], SM16[:, 0:16]), inc=((sV, 1),))
        op("vector", lambda e: e.tensor_copy(MHB[:], MH16[:]), inc=((sV, 1),))
        op("tensor", lambda e: e.matmul(PT[0:16, 0:128], MHB[:], IDB[:], start=True, stop=True),
           wait=((sV, None), (sA, None)), inc=((sT, 1),))
        op("vector", lambda e: e.tensor_copy(XB[0:16, 0:128], PT[0:16, 0:128]),
           wait=((sT, None),), inc=((sV, 1),))
        op("vector", lambda e: e.reduce_max(XB[0:16, 128:129], XB[0:16, 0:128], AX.X), inc=((sV, 1),))
        wmh = dma("sync", rowsd[0:1, 100:116], XB[0:16, 128:129], wait=((sV, None),))
        wmh2 = dma("sync", ROWS[0:1, 200:216], rowsd[0:1, 100:116], wait=(wmh,))
        op("vector", lambda e: e.tensor_scalar_mul(ROWS[0:1, 216:232], ROWS[0:1, 200:216], -1.0),
           wait=(wmh2,), inc=((sV, 1),))
        nmhw = bounce_bcast(ROWS[0:1, 216:232], 16, SM16[:, 16:32], 1, 128, wait=((sV, None),))
        expm_w = {}
        for h in range(H):
            op("scalar", lambda e, h=h:
               e.activation(EXPM[:, h * 16:(h + 1) * 16], MBUF[:, h * 16:(h + 1) * 16], AF.Exp,
                            bias=SM16[:, 16 + h:17 + h], scale=SM16[:, h:h + 1]),
               wait=(nmhw, (sV, None)), inc=((sA, 1),))
            op("vector", lambda e, h=h: e.tensor_copy(MHB[:], EXPM[:, h * 16:(h + 1) * 16]),
               wait=((sA, None), (sT, None)), inc=((sV, 1),))
            op("tensor", lambda e: e.matmul(PT[0:16, 0:128], MHB[:], IDB[:], start=True, stop=True),
               wait=((sV, None), (sA, None)), inc=((sT, 1),))
            op("scalar", lambda e: e.copy(XB[0:16, 0:128], PT[0:16, 0:128]),
               wait=((sT, None),) + ((expm_w[h - 1],) if h > 0 else ()), inc=((sA, 1),))
            expm_w[h] = dma("sync",
                            rowsd[10 + (h % 8):11 + (h % 8), (h // 8) * 2048:(h // 8) * 2048 + 2048]
                            .rearrange("a (t q) -> (a t) q", q=128),
                            XB[0:16, 0:128], wait=((sA, None),))

        nmark2 = [()]
        ESL3b = [EB[:, 0:1024], EB[:, 1024:2048], SCR[:, 18432:19456]]
        eslb_free = [(), (), ()]
        eslb_cnt = [0]
        QAW = ((sA, sA.n),)
        for h in range(H):
            kvh = h // 4
            prow = (h % 2) * 64
            pcb = kvh * 4096
            bslot = BIASB[:, (h % 2) * 32:(h % 2) * 32 + 32]
            op("vector", lambda e, bslot=bslot, h=h:
               e.tensor_scalar(bslot, KCOL, SM16[:, 32 + h:33 + h], SM16[:, 16 + h:17 + h],
                               ALU.mult, ALU.add),
               wait=(kclw, nmhw, (sA, None)), inc=((sV, 1),))
            bwh = ((sV, sV.n),)
            for qs in range(2):
                cmaxv = 16 * qs + 16
                pend = []
                lastb0 = 7 if qs == 0 else 23

                def emit_av(p, h=h, qs=qs, kvh=kvh, cmaxv=cmaxv, lastb0=lastb0):
                    esl, c, spans, avm, ei = p
                    for sx, (a, b_) in enumerate(spans):
                        stop_ = (c == lastb0) if a < 512 else (c == cmaxv - 1)
                        op("tensor", lambda e, esl=esl, a=a, b_=b_, c=c, kvh=kvh, stop_=stop_:
                           e.matmul(P2[0:65, a:b_], vAt[:, c * 260 + kvh * 65: c * 260 + kvh * 65 + 65],
                                    esl[:, a:b_], start=(c == 0), stop=stop_),
                           wait=avm if sx == 0 else (), inc=((sT, 1),) if (sx == len(spans) - 1 and c == cmaxv - 1) else ())
                    eslb_free[ei] = ((sT, sT.n),)

                for c in range(cmaxv):
                    t0 = max(c // 2 - 8 * qs, 0)
                    cols0 = t0 * 128
                    base = (h // 2) * 2048 + qs * 1024
                    si, sw = bSP.acq()
                    sbank = [P0, P1, PT][si]
                    spans = []
                    if cols0 < 512:
                        spans.append((cols0, 512))
                        spans.append((512, 1024))
                    else:
                        spans.append((cols0, 1024))
                    for sx, (a, b_) in enumerate(spans):
                        w = sw + QAW if sx == 0 else ()
                        op("tensor", lambda e, sbank=sbank, a=a, b_=b_, c=c, base=base, prow=prow, pcb=pcb:
                           e.matmul(sbank[:, a:b_], khA[prow:prow + 64, pcb + c * 128: pcb + (c + 1) * 128],
                                    QB[prow:prow + 64, base + a: base + b_],
                                    start=True, stop=True),
                           wait=w, inc=((sT, 1),))
                    ei = eslb_cnt[0] % 3
                    eslb_cnt[0] += 1
                    ew = eslb_free[ei]
                    esl = ESL3b[ei]
                    op("scalar", lambda e, esl=esl, sbank=sbank, c=c, bslot=bslot, h=h, cols0=cols0:
                       e.activation(esl[:, cols0:1024], sbank[:, cols0:1024], AF.Exp,
                                    bias=bslot[:, c:c + 1], scale=SM16[:, h:h + 1]),
                       wait=((sT, None),) + bwh + ew, inc=((sA, 1),))
                    bSP.rel(si, sA)
                    bt = c // 2
                    if 8 * qs <= bt < 8 * qs + 8:
                        bc = (bt - 8 * qs) * 128
                        op("vector", lambda e, esl=esl, bc=bc, c=c:
                           e.tensor_mul(esl[:, bc:bc + 128], esl[:, bc:bc + 128],
                                        CMS[:, (c % 2) * 128:(c % 2) * 128 + 128]),
                           wait=((sA, None),), inc=((sV, 1),))
                        avm = ((sV, sV.n),)
                    else:
                        avm = ((sA, sA.n),)
                    if c == 0:
                        avm = avm + (nmark2[0] if nmark2[0] else ())
                    pend.append((esl, c, spans, avm, ei))
                    if len(pend) > 2:
                        emit_av(pend.pop(0))
                for p in pend:
                    emit_av(p)
                OSC = SCRF[:, 8192:9216]
                oi, ow = bOS.acq()
                op("vector", lambda e, OSC=OSC: e.tensor_copy(OSC[0:65, :], P2[0:65, 0:1024]),
                   wait=((sT, None),) + ow, inc=((sV, 1),))
                nmark2[0] = ((sV, sV.n),)
                ri, rw = bRW.acq()
                erow = ROWS[0:1, 512 + ri * 1024: 512 + ri * 1024 + 1024]
                wb2 = dma("sync", erow,
                          rowsd[10 + (h % 8):11 + (h % 8), (h // 8) * 2048 + qs * 1024:(h // 8) * 2048 + qs * 1024 + 1024],
                          wait=rw + (expm_w[h],))
                op("vector", lambda e, OSC=OSC: e.tensor_copy(XB[0:1, 2048:3072], OSC[64:65, :]),
                   wait=((sV, None),), inc=((sV, 1),))
                op("vector", lambda e, erow=erow: e.tensor_add(erow, erow, XB[0:1, 2048:3072]),
                   wait=(wb2,), inc=((sV, 1),))
                op("vector", lambda e, erow=erow: e.reciprocal(erow, erow), inc=((sV, 1),))
                rbw = bounce_bcast(erow, 1024, RCPB[0:64, ri * 1024: ri * 1024 + 1024], 4 + ri, 64,
                                   wait=((sV, None),), wait2=((sV, None),))
                op("vector", lambda e, h=h, qs=qs, ri=ri, OSC=OSC:
                   e.scalar_tensor_tensor(FMB[(h % 2) * 64:(h % 2) * 64 + 64,
                                              (h // 2) * 2048 + qs * 1024:(h // 2) * 2048 + (qs + 1) * 1024],
                                          OSC[0:64, :], 1.0, RCPB[0:64, ri * 1024: ri * 1024 + 1024],
                                          ALU.mult, ALU.mult),
                   wait=(rbw,), inc=((sV, 1),))
                bRW.rel(ri, sV)
                bOS.rel(oi, sV)

        dump("aofm", FMB[:], ((sV, None),))
        dump("mbuf", MBUF[:], ((sV, None),))
        wow = load_wslab(WSA[:, 8192:16384], wo, DC, 1024, wait=((sT, None),))
        wosl = WSA[:, 8192:16384]

        def evwo(oc, sp, pa):
            ei, ew = bEB.acq()
            esl = EB[:, ei * 1024: ei * 1024 + 512]
            op("scalar", lambda e, esl=esl, pa=pa: e.copy(esl, pa),
               wait=((sT, None),) + ew, inc=((sA, 1),))

            def sink(t, pts, oc=oc):
                op("vector", lambda e, t=t, oc=oc, pts=pts:
                   e.tensor_copy(QB[:, t * 1024 + oc * 128: t * 1024 + (oc + 1) * 128], pts),
                   wait=((sT, None),), inc=((sV, 1),))
            transpose_to_tok(esl, sp, sink)
            bEB.rel(ei, sT)
        projection(FMB, wosl, DC, evwo, first_wait=(wow,))

        for t in range(NT):
            i, w = bXB.acq()
            xo = XB[:, i * 1024:(i + 1) * 1024]
            xw = dma("sync", xo, src_dram[t * 128:(t + 1) * 128, :], wait=w + (((sV, None),) if t == 0 else ()))
            op("vector", lambda e, xo=xo, t=t:
               e.tensor_add(xo, xo, QB[:, t * 1024:(t + 1) * 1024]),
               wait=(xw,), inc=((sV, 1),))
            _last_dma = dma("sync", dst_dram[t * 128:(t + 1) * 128, :], xo, wait=((sV, None),))
            bXB.free_at[i].append(_last_dma)

    ALLBUFS = [bXB, bEB, bSP, bWH, bB1, bLNG, bOS, bST, bRW, bZV]
    KPHASE = int(os.environ.get("KPHASE", "3"))
    if KPHASE == 1:
        ffn(x_in, out, 0)
    elif KPHASE == 2:
        ffn(x_in, xs, 0)
        attn(xs, out)
    else:
        ffn(x_in, xs, 0)
        if DBG:
            dump("xs1", xs, (*ALLD(),))
        attn(xs, xs)
        if DBG:
            dump("xs2", xs, (*ALLD(),))
        ffn(xs, out, 2)
    op("sync", lambda e: e.nop(), wait=(*ALLD(), *ALLG(), (sCC, None)))

    with nc.Block() as blk:
        sch.flush(blk)
    stack.close()
    return nc


_NC_CACHE = None


def kernel(**inputs):
    global _NC_CACHE
    xfull = np.ascontiguousarray(np.asarray(inputs["L0_attn_x"], dtype=np.float32))

    def cvt(a):
        return np.ascontiguousarray(np.asarray(a), dtype=np.float32)

    base = dict(
        wq=cvt(inputs["L0_attn_Wq"]), wk=cvt(inputs["L0_attn_Wk"]), wv=cvt(inputs["L0_attn_Wv"]),
        wo=cvt(inputs["L0_attn_Wo"]),
        qpa=cvt(inputs["L0_attn_q_phase_angle"]).reshape(1, H),
        als=cvt(inputs["L0_attn_alibi_log_slopes"]).reshape(1, H),
        fq=cvt(inputs["L0_ffn_q_w"]), fk=cvt(inputs["L0_ffn_k_w"]), fv=cvt(inputs["L0_ffn_v_w"]),
        fow=cvt(inputs["L0_ffn_out_w"]), fgw=cvt(inputs["L0_ffn_gate_w"]),
        fgbT=np.ascontiguousarray(cvt(inputs["L0_ffn_gate_b"]).reshape(DC, 128).T),
        fswT=np.ascontiguousarray(cvt(inputs["L0_ffn_scale_w"]).T),
        n1g=cvt(inputs["L0_ffn_n1_g"]).reshape(1, D), n1b=cvt(inputs["L0_ffn_n1_b"]).reshape(1, D),
        n2g=cvt(inputs["L0_ffn_n2_g"]).reshape(1, D), n2b=cvt(inputs["L0_ffn_n2_b"]).reshape(1, D),
        mgw=cvt(inputs["L0_ffn_moe_gate_w"]), mgb=cvt(inputs["L0_ffn_moe_gate_b"]).reshape(1, EE),
        w1=cvt(inputs["L0_ffn_w1"]), b1=cvt(inputs["L0_ffn_b1"]),
        lngT=np.ascontiguousarray(cvt(inputs["L0_ffn_ln_g"]).reshape(EE, FC, 128).transpose(0, 2, 1)),
        lnbT=np.ascontiguousarray(cvt(inputs["L0_ffn_ln_b"]).reshape(EE, FC, 128).transpose(0, 2, 1)),
        w2=cvt(inputs["L0_ffn_w2"]), b2=cvt(inputs["L0_ffn_b2"]),
        rsc=cvt(inputs["L0_ffn_res_scale"]).reshape(1, EE),
        identb=np.eye(128, dtype=BF), identf=np.eye(128, dtype=np.float32),
        krel=np.stack([np.ones(T), -32.0 * (np.arange(T) // 32), np.ones(T),
                       -1.0 * (np.arange(T) % 32)]).astype(BF),
        iot8=np.arange(EE, dtype=np.float32).reshape(1, EE),
        kcol=(np.arange(NC2, dtype=np.float32)[None, :] * 128.0
              + np.arange(128, dtype=np.float32)[:, None]),
    )

    in_maps = []
    for c in range(8):
        b, half = c // 2, c % 2
        rows = np.concatenate([np.arange((2 * t + half) * 128, (2 * t + half + 1) * 128) for t in range(NT)])
        m = dict(base)
        m["x"] = np.ascontiguousarray(xfull[b][rows])
        m["qrel"] = np.stack([np.zeros(TOK), np.ones(TOK), np.zeros(TOK), np.ones(TOK)]).astype(BF)
        r_ = np.arange(128)[:, None]
        i_ = np.arange(128)[None, :]
        am = np.zeros((2, 128, 128), np.float32)
        cm = np.zeros((2, 128, 128), np.float32)
        if half == 0:
            am[0][i_ > r_] = -1e30
            am[1][:, :] = -1e30
            cm[0] = (np.arange(128)[None, :] >= np.arange(128)[:, None]).astype(np.float32)
            cm[1][:, :] = 0.0
        else:
            am[0][:, :] = 0.0
            am[1][i_ > r_] = -1e30
            cm[0][:, :] = 1.0
            cm[1] = (np.arange(128)[None, :] >= np.arange(128)[:, None]).astype(np.float32)
        m["amask"] = am
        m["cmask"] = cm.astype(BF)
        in_maps.append(m)

    if _NC_CACHE is None:
        _NC_CACHE = build_program()
    res = run_bass_kernel_spmd(_NC_CACHE, in_maps, core_ids=list(range(8)))
    import os as _os
    if int(_os.environ.get("KDBG", "0")):
        kernel.last_results = res.results
    outp = np.zeros((B, T, D), np.float32)
    for c in range(8):
        b, half = c // 2, c % 2
        rows = np.concatenate([np.arange((2 * t + half) * 128, (2 * t + half + 1) * 128) for t in range(NT)])
        outp[b][rows] = res.results[c]["out"]
    return outp


# version 18
# speedup vs baseline: 1.0957x; 1.0957x over previous
import numpy as np
import ml_dtypes
import concourse.bass as bass
import concourse.mybir as mybir
from concourse.bass_utils import run_bass_kernel_spmd

F32 = mybir.dt.float32
BF16 = mybir.dt.bfloat16
AF = mybir.ActivationFunctionType
ALU = mybir.AluOpType
AX = mybir.AxisListType
BF = ml_dtypes.bfloat16

B, T, D = 4, 4096, 1024
NT = 16
TOK = NT * 128
DC = D // 128
H, HD = 16, 64
KVH = 4
FF, FC = 2048, 16
EE = 8
NC2 = T // 128
EPS = 1e-5

ENGS = ("tensor", "vector", "scalar", "gpsimd", "sync")


class Sem:
    def __init__(self, h):
        self.h = h
        self.n = 0


class Buf:
    def __init__(self, n):
        self.free_at = [[] for _ in range(n)]
        self.i = 0
        self.n = n

    def acq(self):
        i = self.i % self.n
        self.i += 1
        w = self.free_at[i]
        self.free_at[i] = []
        return i, tuple(w)

    def rel(self, i, sem):
        self.free_at[i].append((sem, sem.n))


class Sched:
    def __init__(self):
        self.q = {e: [] for e in ENGS}
        self.count = 0
        self.selfsem = {}
        self.maxw = {e: {} for e in ENGS}

    def op(self, eng, fn, wait=(), inc=()):
        wl = []
        mw = self.maxw[eng]
        ss = self.selfsem.get(eng)
        if ss is not None and ss.n > 0:
            wl.append((ss.h, ss.n))
        for w in wait:
            s, v = w
            wl.append((s.h, s.n if v is None else v))
        wl2 = []
        for h, v in wl:
            hid = id(h)
            if v > mw.get(hid, 0):
                mw[hid] = v
                wl2.append((h, v))
        wl = wl2
        il = []
        for s, k in inc:
            s.n += k
            il.append((s.h, k))
        self.count += 1

        def thunk(e, fn=fn, wl=wl, il=il):
            for h, v in wl:
                if v > 0:
                    e.wait_ge(h, v)
            ins = fn(e)
            for h, k in il:
                ins.then_inc(h, k)
        self.q[eng].append(thunk)

    def flush(self, blk):
        for e in ENGS:
            thunks = self.q[e]
            if not thunks:
                continue

            def body(eng, thunks=thunks):
                for t in thunks:
                    t(eng)
            getattr(blk, e)(body)


import os
DBG = int(os.environ.get("KDBG", "0"))


def build_program():
    nc = bass.Bass("TRN2", target_bir_lowering=False,
                   detect_race_conditions=not int(os.environ.get("KNORACE", "0")))
    dbg_outs = {}

    def din(name, shape, dt=F32):
        return nc.dram_tensor(name, list(shape), dt, kind="ExternalInput").ap()

    x_in = din("x", [TOK, D])
    wq = din("wq", [512, D]); wk = din("wk", [512, 256]); wv = din("wv", [512, 256])
    wo = din("wo", [D, D])
    qpa = din("qpa", [1, H]); als = din("als", [1, H])
    fq = din("fq", [D, D]); fkw = din("fk", [D, D]); fvw = din("fv", [D, D])
    fow = din("fow", [D, D]); fgw = din("fgw", [D, D]); fgbT = din("fgbT", [128, DC])
    fswT = din("fswT", [HD, H])
    n1g = din("n1g", [1, D]); n1b = din("n1b", [1, D])
    n2g = din("n2g", [1, D]); n2b = din("n2b", [1, D])
    mgw = din("mgw", [D, EE]); mgb = din("mgb", [1, EE])
    w1 = din("w1", [EE, D, FF]); b1 = din("b1", [EE, FF])
    lngT = din("lngT", [EE, 128, FC]); lnbT = din("lnbT", [EE, 128, FC])
    w2 = din("w2", [EE, FF, D]); b2 = din("b2", [EE, D])
    rsc = din("rsc", [1, EE])
    identb = din("identb", [128, 128], BF16)
    identf = din("identf", [128, 128], F32)
    qrel = din("qrel", [4, TOK], BF16)
    krel = din("krel", [4, T], BF16)
    amask = din("amask", [2, 128, 128], F32)
    cmask = din("cmask", [2, 128, 128], BF16)
    iot8 = din("iot8", [1, EE])
    kcol = din("kcol", [128, NC2])

    out = nc.dram_tensor("out", [TOK, D], F32, kind="ExternalOutput").ap()
    xs = nc.dram_tensor("xs", [TOK, D], F32).ap()
    rowsd = nc.dram_tensor("rowsd", [24, 4096], F32).ap()

    KT_ELE = 64 * (H * TOK)
    VT_ELE = 128 * (NT * H * 65)
    CC1N = KT_ELE + VT_ELE
    KT2_ELE = 64 * (KVH * TOK)
    VT2_ELE = 128 * (NT * KVH * 65)
    CC2N = KT2_ELE + VT2_ELE
    cc_in = [nc.dram_tensor(f"cci{i}", [1, n], BF16).ap()
             for i, n in ((0, CC1N), (1, CC2N), (2, CC1N))]
    cc_out = [nc.dram_tensor(f"cco{i}", [2, n], BF16).ap()
              for i, n in ((0, CC1N), (1, CC2N), (2, CC1N))]

    sch = Sched()
    import contextlib
    stack = contextlib.ExitStack()
    sb = lambda name, shape, dt: stack.enter_context(nc.sbuf_tensor(name, list(shape), dt))
    ps = lambda name, shape, dt: stack.enter_context(nc.psum_tensor(name, list(shape), dt))

    WSA = sb("WSA", [128, 16384], BF16)
    WSB = sb("WSB", [128, 16384], BF16)
    FMB = sb("FMB", [128, 16384], BF16)
    QB = sb("QB", [128, 16384], BF16)
    SCR = sb("SCR", [128, 19456], BF16)
    XB = sb("XB", [128, 3072], F32)
    EB = sb("EB", [128, 2048], BF16)
    STT_ = sb("STAT", [128, 128], F32)
    ROWS = sb("ROWS", [128, 2560], F32)
    NGB = sb("NGB", [128, 2048], BF16)
    LNGB = sb("LNGB", [128, 64], F32)
    IDB = sb("IDB", [128, 128], BF16)
    SM16 = sb("SM16", [128, 128], F32)
    GSC = sb("GSC", [128, 48], F32)
    FCO = sb("FCO", [128, NT * 8], F32)
    SWB = sb("SWB", [128, NT], F32)
    CTB = sb("CTB", [8, NT * 128], BF16)
    FCB = sb("FCB", [128, 8], BF16)
    B2S = sb("B2S", [8, 1024], BF16)
    MGWS = sb("MGWS", [128, 64], BF16)
    ONE1 = sb("ONE1", [128, 1], BF16)
    EPSC = sb("EPSC", [128, 1], F32)
    PI2C = sb("PI2C", [128, 1], F32)
    SWCOL = sb("SWCOL", [128, 16], BF16)
    SCRF = SCR.bitcast(F32)
    ZS = SCR
    H2T = SCR
    XRS = SCR[:, 16384:17408]
    B1T = SCR[:, 17408:19456]
    vAt = SCR
    KRELS = SCR[0:4, 8320:12416]
    QRELS = SCR[0:4, 12416:14464]
    AMS = SCRF[:, 7232:7488]
    CMS = SCR[:, 14976:15232]
    MBUF = SCRF[:, 7616:7872]
    MH16 = SCRF[:, 7872:7888]
    MHB = SCR[:, 15776:15792]
    EXPM = SCRF[:, 7904:8160]
    WQSTG = SCRF[:, 0:2048]
    RCPB = XB
    SCBB = SCR[:, 0:2048]
    T1F = SCRF[:, 1024:2048]

    P0 = ps("P0", [128, 1024], F32)
    P1 = ps("P1", [128, 1024], F32)
    P2 = ps("P2", [128, 1024], F32)
    PT = ps("PT", [128, 1024], F32)

    mk = lambda name: Sem(stack.enter_context(nc.semaphore(name)))
    sT = mk("sT"); sV = mk("sV"); sA = mk("sA"); sCC = mk("sCC")
    sB = mk("sB")
    sch.selfsem = {"vector": sV, "scalar": sA}
    SYNCP = [mk(f"sq{i}") for i in range(8)]
    GPP = [mk(f"gq{i}") for i in range(4)]
    _qrr = {"sync": [0], "gpsimd": [0]}

    def ALLD():
        return tuple((s, s.n) for s in SYNCP)

    def ALLG():
        return tuple((s, s.n) for s in GPP)

    def phase_barrier(bufs):
        W = ALLD() + ALLG() + ((sT, sT.n), (sV, sV.n), (sA, sA.n), (sCC, sCC.n))
        for eng in ENGS:
            op(eng, lambda e: e.nop(), wait=W, inc=((sB, 1),))
        gate = sB.n
        allsems = [sT, sV, sA, sCC] + SYNCP + GPP
        def clear_all(e, hs=[s.h for s in allsems]):
            ins = None
            for h in hs:
                ins = e.sem_clear(h)
            return ins
        op("gpsimd", clear_all, wait=((sB, gate),), inc=((sB, 1),))
        gate2 = sB.n
        for s in allsems:
            s.n = 0
        for eng in ENGS:
            op(eng, lambda e: e.nop(), wait=((sB, gate2),))
        for b in bufs:
            b.free_at = [[] for _ in range(b.n)]
            b.i = 0

    bXB = Buf(3)
    bEB = Buf(2)
    bSP = Buf(3)
    bWH = Buf(2)
    bB1 = Buf(1)
    bLNG = Buf(2)
    bOS = Buf(1)
    bST = Buf(4)
    bRW = Buf(2)
    bZV = Buf(2)

    op = sch.op

    def dma(queue, outap, inap, wait=(), slow=False):
        pool = SYNCP if queue == "sync" else GPP
        rr = _qrr[queue]
        s = pool[rr[0] % len(pool)]
        rr[0] += 1
        op(queue, lambda e, o=outap, i=inap, slow=slow: e.dma_start(out=o, in_=i, allow_slow_non_contiguous=slow),
           wait=tuple(wait) + ((s, s.n),), inc=((s, 16),))
        return (s, s.n)

    def dmaT(queue, outap, inap, wait=()):
        pool = SYNCP if queue == "sync" else GPP
        rr = _qrr[queue]
        s = pool[rr[0] % len(pool)]
        rr[0] += 1
        op(queue, lambda e, o=outap, i=inap: e.dma_start_transpose(out=o, in_=i),
           wait=tuple(wait) + ((s, s.n),), inc=((s, 16),))
        return (s, s.n)

    def bounce_bcast(row_src, ncols, dst_ap, drow_idx, nrows, wait=(), wait2=()):
        w1_ = dma("sync", rowsd[drow_idx:drow_idx + 1, 0:ncols], row_src, wait=wait)
        bc = rowsd[drow_idx:drow_idx + 1, 0:ncols].broadcast_to([nrows, ncols])
        return dma("sync", dst_ap, bc, wait=(w1_,) + tuple(wait2))

    def dump(name, ap, wait):
        if not DBG:
            return
        if name not in dbg_outs:
            dbg_outs[name] = nc.dram_tensor("dbg_" + name, list(ap.shape), ap.dtype, kind="ExternalOutput").ap()
        dma("sync", dbg_outs[name], ap, wait=wait)

    dma("sync", IDB[:], identb)
    dma("sync", SM16[:, 48:56], iot8.broadcast_to([128, EE]))
    dma("sync", SM16[:, 56:64], rsc.broadcast_to([128, EE]))
    dma("sync", SM16[:, 64:72], mgb.broadcast_to([128, EE]))
    dma("gpsimd", SWCOL[0:64, :], fswT)
    dma("gpsimd", SWCOL[64:128, :], fswT)
    for k in range(DC):
        dma("gpsimd", MGWS[:, k * 8:(k + 1) * 8], mgw[k * 128:(k + 1) * 128, :])
    op("vector", lambda e: e.memset(ONE1[:], 1.0), inc=((sV, 1),))
    op("vector", lambda e: e.memset(EPSC[:], EPS), inc=((sV, 1),))
    op("vector", lambda e: e.memset(PI2C[:], 1.5707963267948966), inc=((sV, 1),))
    w_ = dma("sync", ROWS[0:1, 0:16], als)
    op("scalar", lambda e: e.activation(ROWS[0:1, 16:32], ROWS[0:1, 0:16], AF.Exp),
       wait=(w_,), inc=((sA, 1),))
    bounce_bcast(ROWS[0:1, 16:32], 16, SM16[:, 0:16], 0, 128, wait=((sA, None),))
    CW = ALLD() + ALLG() + ((sV, sV.n), (sA, sA.n))
    for eng in ENGS:
        op(eng, lambda e: e.nop(), wait=CW)

    def ln_stats(src_ap, twait=()):
        si, sw = bST.acq()
        st_ = STT_[:, si * 32:(si + 1) * 32]
        op("vector", lambda e, st_=st_: e.bn_stats(st_[:, 0:6], src_ap[:, 0:512]),
           wait=tuple(twait) + sw, inc=((sV, 1),))
        op("vector", lambda e, st_=st_: e.bn_stats(st_[:, 6:12], src_ap[:, 512:1024]), inc=((sV, 1),))
        op("vector", lambda e, st_=st_: e.bn_aggr(st_[:, 12:14], st_[:, 0:12].rearrange("p (n s) -> p n s", s=3)),
           inc=((sV, 1),))
        op("scalar", lambda e, st_=st_: e.activation(st_[:, 16:17], st_[:, 13:14], AF.Sqrt, bias=EPSC[:, 0:1]),
           wait=((sV, None),), inc=((sA, 1),))
        op("vector", lambda e, st_=st_: e.reciprocal(st_[:, 14:15], st_[:, 16:17]),
           wait=((sA, None),), inc=((sV, 1),))
        op("vector", lambda e, st_=st_: e.scalar_tensor_tensor(st_[:, 15:16], st_[:, 12:13], -1.0, st_[:, 14:15], ALU.mult, ALU.mult),
           inc=((sV, 1),))
        return si, st_[:, 14:15], st_[:, 15:16]

    def ln_stats4(srcs, twait=()):
        si, sw = bST.acq()
        st_ = STT_[:, si * 32:(si + 1) * 32]
        for ci, s_ in enumerate(srcs):
            op("vector", lambda e, st_=st_, ci=ci, s_=s_: e.bn_stats(st_[:, ci * 6:(ci + 1) * 6], s_),
               wait=(tuple(twait) + sw) if ci == 0 else (), inc=((sV, 1),))
        op("vector", lambda e, st_=st_: e.bn_aggr(st_[:, 24:26], st_[:, 0:24].rearrange("p (n s) -> p n s", s=3)),
           inc=((sV, 1),))
        op("scalar", lambda e, st_=st_: e.activation(st_[:, 28:29], st_[:, 25:26], AF.Sqrt, bias=EPSC[:, 0:1]),
           wait=((sV, None),), inc=((sA, 1),))
        op("vector", lambda e, st_=st_: e.reciprocal(st_[:, 26:27], st_[:, 28:29]),
           wait=((sA, None),), inc=((sV, 1),))
        op("vector", lambda e, st_=st_: e.scalar_tensor_tensor(st_[:, 27:28], st_[:, 24:25], -1.0, st_[:, 26:27], ALU.mult, ALU.mult),
           inc=((sV, 1),))
        return si, st_[:, 26:27], st_[:, 27:28]

    def ln0(src_dram, dst_fm):
        for t in range(NT):
            i, w = bXB.acq()
            xt = XB[:, i * 1024:(i + 1) * 1024]
            xw = dma("sync", xt, src_dram[t * 128:(t + 1) * 128, :], wait=w)
            si, rstd, negm = ln_stats(xt, twait=(xw,))
            tgt = ZS[:, (t % 2) * 1024:(t % 2) * 1024 + 1024]
            op("scalar", lambda e, xt=xt, tgt=tgt, rstd=rstd, negm=negm:
               e.activation(tgt, xt, AF.Identity, bias=negm, scale=rstd),
               wait=((sV, None), (sT, None)), inc=((sA, 1),))
            bST.rel(si, sA)
            bXB.rel(i, sA)
            for ch in range(DC):
                pc = (ch % 2) * 512
                op("tensor", lambda e, tgt=tgt, ch=ch, pc=pc:
                   e.matmul(PT[:, pc:pc + 128], tgt[:, ch * 128:(ch + 1) * 128], IDB[:], start=True, stop=True),
                   wait=((sA, None), (sV, None)), inc=((sT, 1),))
                op("vector", lambda e, ch=ch, t=t, pc=pc:
                   e.tensor_copy(dst_fm[:, ch * 2048 + t * 128: ch * 2048 + (t + 1) * 128], PT[:, pc:pc + 128]),
                   wait=((sT, None),), inc=((sV, 1),))

    def load_wslab(slab, wdram, kc, width, queue="gpsimd", wait=()):
        last = None
        for k in range(kc):
            last = dma(queue, slab[:, k * width:(k + 1) * width],
                       wdram[k * 128:(k + 1) * 128, :], wait=wait if k == 0 else ())
        return last

    def projection(src_fm, slab, kc, evac, first_wait=()):
        marks = [()] * 4
        for oc in range(DC):
            for sp in range(4):
                idx = oc * 4 + sp
                bank = [P0, P1][idx % 2]
                pa = bank[:, 0:512] if (idx % 4) < 2 else bank[:, 512:1024]
                for k in range(kc):
                    w = ()
                    if k == 0:
                        w = (first_wait + ((sA, None), (sV, None))) if idx == 0 else marks[idx % 4]
                    op("tensor", lambda e, pa=pa, k=k, oc=oc, sp=sp, slab=slab:
                       e.matmul(pa, slab[:, k * 1024 + oc * 128: k * 1024 + (oc + 1) * 128],
                                src_fm[:, k * 2048 + sp * 512: k * 2048 + (sp + 1) * 512],
                                start=(k == 0), stop=(k == kc - 1)),
                       wait=w, inc=((sT, 1),) if k == kc - 1 else ())
                evac(oc, sp, pa)
                marks[idx % 4] = ((sA, sA.n), (sV, sV.n))

    def transpose_to_tok(esl, sp, dst_fn):
        for tp in range(4):
            t = sp * 4 + tp
            pc = (tp % 2) * 512
            op("tensor", lambda e, esl=esl, tp=tp, pc=pc:
               e.matmul(PT[:, pc:pc + 128], esl[:, tp * 128:(tp + 1) * 128], IDB[:], start=True, stop=True),
               wait=((sA, None), (sV, None)), inc=((sT, 1),))
            dst_fn(t, PT[:, pc:pc + 128])

    def ffn(src_dram, dst_dram, ccid):
        cci, cco = cc_in[ccid], cc_out[ccid]
        ln0(src_dram, FMB)
        dump(f"srcfm{ccid}", FMB[:], ((sV, None),))

        whi, whw = bWH.acq()
        ksl = WSB[:, whi * 8192: whi * 8192 + 8192]
        kw = load_wslab(ksl, fkw, DC, 1024, wait=whw)
        ktv = cci[0:1, 0:KT_ELE].rearrange("a (p c) -> (a p) c", p=64)

        def evk(oc, sp, pa):
            ei, ew = bEB.acq()
            esl = EB[:, ei * 1024: ei * 1024 + 1024]
            op("scalar", lambda e, esl=esl, pa=pa: e.copy(esl[:, 0:512], pa),
               wait=((sT, None),) + ew, inc=((sA, 1),))
            op("scalar", lambda e, esl=esl: e.activation(esl[:, 512:1024], esl[:, 0:512], AF.Square),
               wait=((sT, None),), inc=((sA, 1),))
            for half in range(2):
                hh = 2 * oc + half
                _last_dma = dma("sync", ktv[0:64, hh * TOK + sp * 512: hh * TOK + (sp + 1) * 512],
                    esl[half * 64:half * 64 + 64, 0:512], wait=((sA, None),))
                op("tensor", lambda e, esl=esl, half=half:
                   e.matmul(P2[0:1, 0:512], ONE1[half * 64:half * 64 + 64, 0:1],
                            esl[half * 64:half * 64 + 64, 512:1024], start=True, stop=True),
                   wait=((sA, None), (sV, None)), inc=((sT, 1),))
                op("vector", lambda e, hh=hh, sp=sp:
                   e.reduce_max(ROWS[0:1, 2304 + hh * 4 + sp: 2305 + hh * 4 + sp], P2[0:1, 0:512], AX.X),
                   wait=((sT, None),), inc=((sV, 1),))
            bEB.rel(ei, sV)
            bEB.free_at[ei].append(_last_dma)
        projection(FMB, ksl, DC, evk, first_wait=(kw, (sV, None)))
        op("vector", lambda e: e.reduce_max(ROWS[0:1, 2432:2448],
                                            ROWS[0:1, 2304:2368].rearrange("a (h s) -> a h s", s=4), AX.X),
           inc=((sV, 1),))
        bWH.rel(whi, sT)

        whi, whw = bWH.acq()
        vsl = WSB[:, whi * 8192: whi * 8192 + 8192]
        vw = load_wslab(vsl, fvw, DC, 1024, wait=whw)
        vtv = cci[0:1, KT_ELE:KT_ELE + VT_ELE].rearrange("a (p c) -> (a p) c", p=128)

        def evv(oc, sp, pa):
            ei, ew = bEB.acq()
            esl = EB[:, ei * 1024: ei * 1024 + 512]
            op("scalar", lambda e, esl=esl, pa=pa: e.copy(esl, pa),
               wait=((sT, None),) + ew, inc=((sA, 1),))

            def sink(t, pts, oc=oc):
                zi, zw = bZV.acq()
                zv = ZS[:, 6144 + zi * 256: 6144 + zi * 256 + 256]
                op("vector", lambda e, zv=zv, pts=pts: e.tensor_copy(zv[:, 0:64], pts[:, 0:64]),
                   wait=((sT, None),) + zw, inc=((sV, 1),))
                op("vector", lambda e, zv=zv, pts=pts: e.tensor_copy(zv[:, 65:129], pts[:, 64:128]), inc=((sV, 1),))
                op("vector", lambda e, zv=zv: e.memset(zv[:, 64:65], 1.0), inc=((sV, 1),))
                op("vector", lambda e, zv=zv: e.memset(zv[:, 129:130], 1.0), inc=((sV, 1),))
                _last_dma = dma("sync", vtv[:, t * (H * 65) + (2 * oc) * 65: t * (H * 65) + (2 * oc) * 65 + 130],
                    zv[:, 0:130], wait=((sV, None),))
                bZV.free_at[zi].append(_last_dma)
            transpose_to_tok(esl, sp, sink)
            bEB.rel(ei, sT)
        projection(FMB, vsl, DC, evv, first_wait=(vw, (sV, None)))
        bWH.rel(whi, sT)

        op("gpsimd", lambda e, cci=cci, cco=cco: e.collective_compute(
            "AllGather", ALU.bypass,
            replica_groups=[[0, 1], [2, 3], [4, 5], [6, 7]],
            ins=[cci.opt()], outs=[cco.opt()]),
           wait=(*ALLD(), *ALLG()), inc=((sCC, 1),))
        ccw = (sCC, sCC.n)

        whi, whw = bWH.acq()
        qsl = WSB[:, whi * 8192: whi * 8192 + 8192]
        qw = load_wslab(qsl, fq, DC, 1024, wait=whw)

        def evq(oc, sp, pa):
            op("scalar", lambda e, oc=oc, sp=sp, pa=pa:
               e.copy(QB[:, oc * 2048 + sp * 512: oc * 2048 + (sp + 1) * 512], pa),
               wait=((sT, None),), inc=((sA, 1),))
        projection(FMB, qsl, DC, evq, first_wait=(qw, (sV, None)))
        bWH.rel(whi, sA)

        rowdma = None
        rowd = []
        for h in range(H):
            base = (h % 2) * 64
            ch = h // 2
            for sp in range(4):
                op("tensor", lambda e, h=h, sp=sp, base=base, ch=ch:
                   e.matmul(P2[0:1, 0:512], SWCOL[base:base + 64, h:h + 1],
                            QB[base:base + 64, ch * 2048 + sp * 512: ch * 2048 + (sp + 1) * 512],
                            start=True, stop=True),
                   wait=((sA, None), (sV, None)), inc=((sT, 1),))
                w = ((sT, None),) if not (sp == 0 and rowdma) else ((sT, None), rowdma)
                op("scalar", lambda e, sp=sp:
                   e.activation(ROWS[0:1, 512 + sp * 512: 512 + (sp + 1) * 512], P2[0:1, 0:512], AF.Sigmoid),
                   wait=w, inc=((sA, 1),))
            rowdma = dma("sync", rowsd[8 + h:9 + h, 0:2048], ROWS[0:1, 512:2560], wait=((sA, None),))
            rowd.append(rowdma)
        for ch in range(DC):
            scw = dma("gpsimd", SCBB[:],
                      bass.AP(rowsd.tensor, (8 + 2 * ch) * 4096, [[4096, 2], [0, 64], [1, 2048]]),
                      wait=(rowd[2 * ch], rowd[2 * ch + 1], (sV, None)))
            op("vector", lambda e, ch=ch:
               e.scalar_tensor_tensor(QB[:, ch * 2048:(ch + 1) * 2048], QB[:, ch * 2048:(ch + 1) * 2048],
                                      0.25, SCBB[:], ALU.mult, ALU.mult),
               wait=(scw, (sA, None)), inc=((sV, 1),))

        for ch in range(DC):
            op("scalar", lambda e, ch=ch: e.activation(EB[:, 0:1024], QB[:, ch * 2048: ch * 2048 + 1024], AF.Square),
               wait=((sV, None), (sT, None), *ALLD()), inc=((sA, 1),))
            op("scalar", lambda e, ch=ch: e.activation(EB[:, 1024:2048], QB[:, ch * 2048 + 1024: ch * 2048 + 2048], AF.Square),
               inc=((sA, 1),))
            for half in range(2):
                hh = 2 * ch + half
                for sp in range(4):
                    op("tensor", lambda e, half=half, sp=sp:
                       e.matmul(P2[0:1, 0:512], ONE1[half * 64:half * 64 + 64, 0:1],
                                EB[half * 64:half * 64 + 64, sp * 512:(sp + 1) * 512], start=True, stop=True),
                       wait=((sA, None), (sV, None)), inc=((sT, 1),))
                    op("vector", lambda e, hh=hh, sp=sp:
                       e.reduce_max(ROWS[0:1, 2368 + hh * 4 + sp: 2369 + hh * 4 + sp], P2[0:1, 0:512], AX.X),
                       wait=((sT, None),), inc=((sV, 1),))
        op("vector", lambda e: e.reduce_max(ROWS[0:1, 2448:2464],
                                            ROWS[0:1, 2368:2432].rearrange("a (h s) -> a h s", s=4), AX.X),
           inc=((sV, 1),))
        op("vector", lambda e: e.tensor_mul(ROWS[0:1, 2464:2480], ROWS[0:1, 2432:2448], ROWS[0:1, 2448:2464]),
           inc=((sV, 1),))
        op("scalar", lambda e: e.activation(ROWS[0:1, 2480:2496], ROWS[0:1, 2464:2480], AF.Sqrt),
           wait=((sV, None),), inc=((sA, 1),))
        op("vector", lambda e: e.tensor_scalar_mul(ROWS[0:1, 2496:2512], ROWS[0:1, 2480:2496], -1.0),
           wait=((sA, None),), inc=((sV, 1),))
        cbw = bounce_bcast(ROWS[0:1, 2496:2512], 16, SM16[:, 32:48], 1, 128, wait=((sV, None),))
        dump(f"qt{ccid}", QB[:], ((sV, None),))
        dump(f"crow{ccid}", ROWS[0:1, 2432:2512], ((sV, None),))

        QRDY = ((sV, sV.n),)
        nmark = [()]
        ESL3 = [EB[:, 0:1024], EB[:, 1024:2048], SCR[:, 2048:3072]]
        esl_free = [(), (), ()]
        esl_cnt = [0]
        for h in range(H):
            khs = WSB[:, (h % 2) * 4096:(h % 2) * 4096 + 4096]
            vhs = WSB[:, 8192 + (h % 2) * 2080: 8192 + (h % 2) * 2080 + 2080]
            kw_ = []
            vw_ = []
            hwait = (ccw,) if h < 2 else (ccw, (sT, None))
            for r in range(2):
                src = cco[r:r + 1, 0:KT_ELE].rearrange("a (p c) -> (a p) c", p=64)[0:64, h * TOK:(h + 1) * TOK]
                kw_.append(dma("sync", khs[0:64, r * TOK:(r + 1) * TOK], src, wait=hwait))
                kw_[-1] = dma("sync", khs[64:128, r * TOK:(r + 1) * TOK], src, wait=hwait)
                vsrc = cco[r:r + 1, KT_ELE:KT_ELE + VT_ELE].rearrange("a (p c) -> (a p) c", p=128)
                vsrc = vsrc.rearrange("p (t x) -> p t x", x=H * 65)[:, :, h * 65:(h + 1) * 65]
                vw_.append(dma("sync", vhs[:, r * NT * 65:(r + 1) * NT * 65].rearrange("p (t x) -> p t x", x=65),
                               vsrc, wait=hwait))
            for qs in range(2):
                pend = []

                def emit_av(p, vhs=vhs):
                    pesl, pc, pm, pei = p
                    for half in range(2):
                        w = (pm, vw_[pc // NT]) if half == 0 else ()
                        if pc == 0 and half == 0:
                            w = w + (nmark[0] if nmark[0] else ())
                        op("tensor", lambda e, esl=pesl, half=half, c=pc, vhs=vhs:
                           e.matmul(P2[0:65, half * 512:(half + 1) * 512],
                                    vhs[:, c * 65:(c + 1) * 65],
                                    esl[:, half * 512:(half + 1) * 512],
                                    start=(c == 0), stop=(c == NC2 - 1)),
                           wait=w, inc=((sT, 1),) if half == 1 else ())
                    esl_free[pei] = ((sT, sT.n),)

                for c in range(NC2):
                    si, sw = bSP.acq()
                    sbank = [P0, P1, PT][si]
                    for half in range(2):
                        w = ()
                        if half == 0:
                            w = sw + (kw_[c // NT], cbw) + QRDY
                        op("tensor", lambda e, sbank=sbank, half=half, c=c, h=h, qs=qs, khs=khs:
                           e.matmul(sbank[:, half * 512:(half + 1) * 512],
                                    khs[(h % 2) * 64:(h % 2) * 64 + 64, c * 128:(c + 1) * 128],
                                    QB[(h % 2) * 64:(h % 2) * 64 + 64,
                                       (h // 2) * 2048 + qs * 1024 + half * 512:(h // 2) * 2048 + qs * 1024 + (half + 1) * 512],
                                    start=True, stop=True),
                           wait=w, inc=((sT, 1),) if half == 1 else ())
                    ei = esl_cnt[0] % 3
                    esl_cnt[0] += 1
                    ew = esl_free[ei]
                    esl = ESL3[ei]
                    op("scalar", lambda e, esl=esl, sbank=sbank, h=h:
                       e.activation(esl, sbank[:, 0:1024], AF.Exp, bias=SM16[:, 32 + h:33 + h], scale=1.0),
                       wait=((sT, None),) + ew, inc=((sA, 1),))
                    bSP.rel(si, sA)
                    pend.append((esl, c, (sA, sA.n), ei))
                    if len(pend) > 2:
                        emit_av(pend.pop(0))
                for p in pend:
                    emit_av(p)
                OSC = SCRF[:, 0:1024]
                oi, ow = bOS.acq()
                op("vector", lambda e, OSC=OSC: e.tensor_copy(OSC[0:65, :], P2[0:65, 0:1024]),
                   wait=((sT, None),) + ow, inc=((sV, 1),))
                nmark[0] = ((sV, sV.n),)
                if h == 0 and qs == 0:
                    dump(f"osc{ccid}", OSC[:], ((sV, None),))
                ri, rw = bRW.acq()
                rrow = ROWS[0:1, 512 + ri * 1024: 512 + ri * 1024 + 1024]
                op("vector", lambda e, OSC=OSC: e.tensor_copy(XB[0:1, 2048:3072], OSC[64:65, :]),
                   wait=rw, inc=((sV, 1),))
                op("vector", lambda e, rrow=rrow: e.reciprocal(rrow, XB[0:1, 2048:3072]),
                   inc=((sV, 1),))
                rbw = bounce_bcast(rrow, 1024, RCPB[0:64, ri * 1024: ri * 1024 + 1024], 2 + ri, 64,
                                   wait=((sV, None),), wait2=((sV, None),))
                if h == 0 and qs == 0:
                    dump(f"rcpb{ccid}", RCPB[0:64, 0:1024], (rbw,))
                op("vector", lambda e, h=h, qs=qs, ri=ri, OSC=OSC:
                   e.scalar_tensor_tensor(FMB[(h % 2) * 64:(h % 2) * 64 + 64,
                                              (h // 2) * 2048 + qs * 1024:(h // 2) * 2048 + (qs + 1) * 1024],
                                          OSC[0:64, :], 1.0, RCPB[0:64, ri * 1024: ri * 1024 + 1024],
                                          ALU.mult, ALU.mult),
                   wait=(rbw,), inc=((sV, 1),))
                bRW.rel(ri, sV)
                bOS.rel(oi, sV)

        dump(f"ofm{ccid}", FMB[:], ((sV, None),))
        whi, whw = bWH.acq()
        gsl = WSB[:, whi * 8192: whi * 8192 + 8192]
        gw = load_wslab(gsl, fgw, DC, 1024, wait=whw + ((sV, None),))
        gbw = dma("sync", SM16[:, 72:80], fgbT, wait=())

        GBUF = SCR[:, 0:16384]

        def evg(oc, sp, pa):
            op("scalar", lambda e, pa=pa, oc=oc, sp=sp:
               e.activation(GBUF[:, oc * 2048 + sp * 512: oc * 2048 + (sp + 1) * 512], pa,
                            AF.Sigmoid, bias=SM16[:, 72 + oc:73 + oc]),
               wait=((sT, None), gbw), inc=((sA, 1),))
        projection(FMB, gsl, DC, evg, first_wait=(gw,))
        for oc in range(DC):
            op("vector", lambda e, oc=oc:
               e.tensor_mul(FMB[:, oc * 2048:(oc + 1) * 2048],
                            FMB[:, oc * 2048:(oc + 1) * 2048], GBUF[:, oc * 2048:(oc + 1) * 2048]),
               wait=((sT, None), (sA, None)) if oc == 0 else (), inc=((sV, 1),))
        bWH.rel(whi, sT)

        whi, whw = bWH.acq()
        osl_w = WSB[:, whi * 8192: whi * 8192 + 8192]
        oww = load_wslab(osl_w, fow, DC, 1024, wait=whw + ((sV, None),))

        def evo(oc, sp, pa):
            ei, ew = bEB.acq()
            esl = EB[:, ei * 1024: ei * 1024 + 512]
            op("scalar", lambda e, esl=esl, pa=pa: e.copy(esl, pa),
               wait=((sT, None),) + ew, inc=((sA, 1),))

            def sink(t, pts, oc=oc):
                op("vector", lambda e, t=t, oc=oc, pts=pts:
                   e.tensor_copy(QB[:, t * 1024 + oc * 128: t * 1024 + (oc + 1) * 128], pts),
                   wait=((sT, None),), inc=((sV, 1),))
            transpose_to_tok(esl, sp, sink)
            bEB.rel(ei, sT)
        projection(FMB, osl_w, DC, evo, first_wait=(oww,))
        bWH.rel(whi, sT)

        n1gw = dma("gpsimd", NGB[:, 0:1024], n1g.broadcast_to([128, D]), wait=((sV, None),))
        n1bw = dma("gpsimd", NGB[:, 1024:2048], n1b.broadcast_to([128, D]))
        for t in range(NT):
            i, w = bXB.acq()
            xt = XB[:, i * 1024:(i + 1) * 1024]
            xw = dma("sync", xt, src_dram[t * 128:(t + 1) * 128, :], wait=w + (((sV, None), (sA, None)) if t == 0 else ()))
            si0, rstd0, negm0 = ln_stats(xt, twait=(xw,))
            s1 = ZS[:, (t % 2) * 1024:(t % 2) * 1024 + 1024]
            op("scalar", lambda e, xt=xt, s1=s1, rstd0=rstd0, negm0=negm0:
               e.activation(s1, xt, AF.Identity, bias=negm0, scale=rstd0),
               wait=((sV, None), (sT, None)), inc=((sA, 1),))
            bST.rel(si0, sA)
            op("vector", lambda e, xt=xt, s1=s1, t=t:
               e.tensor_add(xt, s1, QB[:, t * 1024:(t + 1) * 1024]),
               wait=((sA, None),), inc=((sV, 1),))
            si, rstd, negm = ln_stats(xt)
            t1 = T1F
            op("vector", lambda e, xt=xt, t1=t1, rstd=rstd:
               e.scalar_tensor_tensor(t1, xt, rstd, NGB[:, 0:1024], ALU.mult, ALU.mult),
               wait=(n1gw, n1bw) if t == 0 else (), inc=((sV, 1),))
            op("vector", lambda e, xt=xt, negm=negm:
               e.scalar_tensor_tensor(xt, NGB[:, 0:1024], negm, NGB[:, 1024:2048], ALU.mult, ALU.add),
               inc=((sV, 1),))
            op("vector", lambda e, xt=xt, t1=t1, s1=s1:
               e.tensor_add(s1, t1, xt), inc=((sV, 1),))
            bST.rel(si, sV)
            bXB.rel(i, sV)
            for ch in range(DC):
                pc = (ch % 2) * 512
                op("tensor", lambda e, s1=s1, ch=ch, pc=pc:
                   e.matmul(PT[:, pc:pc + 128], s1[:, ch * 128:(ch + 1) * 128], IDB[:], start=True, stop=True),
                   wait=((sV, None),), inc=((sT, 1),))
                op("vector", lambda e, t=t, ch=ch, pc=pc:
                   e.tensor_copy(FMB[:, ch * 2048 + t * 128: ch * 2048 + (t + 1) * 128], PT[:, pc:pc + 128]),
                   wait=((sT, None),), inc=((sV, 1),))

        b2_w = dma("gpsimd", B2S[:], b2, wait=((sV, None),))
        dump(f"x1fm{ccid}", FMB[:], ((sV, None),))
        dump(f"ao{ccid}", QB[:], ((sV, None),))
        for t in range(NT):
            for k in range(DC):
                op("tensor", lambda e, t=t, k=k:
                   e.matmul(P2[:, 0:8], FMB[:, k * 2048 + t * 128: k * 2048 + (t + 1) * 128],
                            MGWS[:, k * 8:(k + 1) * 8], start=(k == 0), stop=(k == DC - 1)),
                   wait=((sV, None), *ALLG()) if k == 0 else (), inc=((sT, 1),) if k == DC - 1 else ())
            L = GSC[:, 0:8]
            wk_ = GSC[:, 8:16]; idsc = GSC[:, 16:24]
            m1 = GSC[:, 40:41]; m1n = GSC[:, 41:42]; m2 = GSC[:, 42:43]
            e2 = GSC[:, 43:44]; s_ = GSC[:, 44:45]; rcp = GSC[:, 45:46]; cmx = GSC[:, 46:47]
            sw_ = SWB[:, t:t + 1]
            op("vector", lambda e, L=L: e.tensor_add(L, P2[:, 0:8], SM16[:, 64:72]),
               wait=((sT, None),), inc=((sV, 1),))
            op("vector", lambda e, L=L, m1=m1: e.reduce_max(m1, L, AX.X),
               wait=((sA, None),), inc=((sV, 1),))
            op("vector", lambda e, m1=m1, m1n=m1n: e.tensor_scalar_mul(m1n, m1, -1.0), inc=((sV, 1),))
            op("vector", lambda e, L=L, m1=m1, wk_=wk_: e.tensor_scalar(wk_, L, m1, None, ALU.is_ge), inc=((sV, 1),))
            op("vector", lambda e, L=L, wk_=wk_: e.scalar_tensor_tensor(wk_, wk_, -1.0e30, L, ALU.mult, ALU.add), inc=((sV, 1),))
            op("vector", lambda e, m2=m2, wk_=wk_: e.reduce_max(m2, wk_, AX.X), inc=((sV, 1),))
            op("scalar", lambda e, m2=m2, m1n=m1n, e2=e2: e.activation(e2, m2, AF.Exp, bias=m1n),
               wait=((sV, None),), inc=((sA, 1),))
            op("vector", lambda e, e2=e2, s_=s_: e.tensor_scalar_add(s_, e2, 1.0),
               wait=((sA, None),), inc=((sV, 1),))
            op("vector", lambda e, s_=s_, rcp=rcp: e.reciprocal(rcp, s_), inc=((sV, 1),))
            op("vector", lambda e, e2=e2, rcp=rcp, sw_=sw_: e.scalar_tensor_tensor(sw_, e2, rcp, rcp, ALU.mult, ALU.add), inc=((sV, 1),))
            op("vector", lambda e, L=L, m2=m2, wk_=wk_: e.tensor_scalar(wk_, L, m2, None, ALU.is_ge), inc=((sV, 1),))
            op("vector", lambda e, wk_=wk_, idsc=idsc: e.scalar_tensor_tensor(idsc, wk_, 1000.0, SM16[:, 48:56], ALU.mult, ALU.add), inc=((sV, 1),))
            op("vector", lambda e, idsc=idsc, cmx=cmx: e.reduce_max(cmx, idsc, AX.X), inc=((sV, 1),))
            op("vector", lambda e, idsc=idsc, cmx=cmx, wk_=wk_: e.tensor_scalar(wk_, idsc, cmx, None, ALU.is_ge), inc=((sV, 1),))
            op("vector", lambda e, wk_=wk_, sw_=sw_: e.tensor_scalar(wk_, wk_, sw_, None, ALU.mult), inc=((sV, 1),))
            op("vector", lambda e, wk_=wk_, t=t: e.tensor_mul(FCO[:, t * 8:(t + 1) * 8], wk_, SM16[:, 56:64]), inc=((sV, 1),))
            op("vector", lambda e, t=t: e.tensor_copy(FCB[:], FCO[:, t * 8:(t + 1) * 8]), inc=((sV, 1),))
            op("tensor", lambda e, t=t: e.matmul(PT[0:8, 0:128], FCB[:], IDB[:], start=True, stop=True),
               wait=((sV, None), (sA, None)), inc=((sT, 1),))
            op("vector", lambda e, t=t: e.tensor_copy(CTB[:, t * 128:(t + 1) * 128], PT[0:8, 0:128]),
               wait=((sT, None),), inc=((sV, 1),))

        ACC = QB
        last_hh_sT = [0]
        last_w2_sT = [0]
        zmark_par = [(), ()]
        evacmarkB = [()]
        trgmark = [()]
        for e_i in range(EE):
            w1w = load_wslab(WSA, w1[e_i], DC, 2048,
                             wait=((sT, last_hh_sT[0]),) if e_i > 0 else ((sT, None),))
            w2w = load_wslab(WSB, w2[e_i], FC, 1024,
                             wait=((sT, last_w2_sT[0]),) if e_i > 0 else ((sT, None),))
            b1i, b1iw = bB1.acq()
            b1t = B1T[:, b1i * 2048:(b1i + 1) * 2048]
            b1w = dma("gpsimd", b1t, b1[e_i:e_i + 1, :].broadcast_to([128, FF]), wait=b1iw)
            lgi, lgiw = bLNG.acq()
            lg = LNGB[:, lgi * 32: lgi * 32 + 16]
            lb = LNGB[:, lgi * 32 + 16: lgi * 32 + 32]
            lgw = dma("sync", lg, lngT[e_i], wait=lgiw)
            lbw = dma("sync", lb, lnbT[e_i])
            for quad in range(4):
                ts = [quad * 4 + j for j in range(4)]
                for j, t in enumerate(ts):
                    par = j % 2
                    BA, BB = (P0, P1) if par == 0 else (P2, PT)
                    aim = ()
                    if e_i == 0:
                        for half in range(2):
                            op("tensor", lambda e, t=t, half=half:
                               e.matmul(P2[:, half * 512:(half + 1) * 512], CTB[:, t * 128:(t + 1) * 128],
                                        B2S[:, half * 512:(half + 1) * 512], start=True, stop=True),
                               wait=(b2_w, (sV, None), (sA, None)) if half == 0 else (), inc=((sT, 1),))
                        for ch in range(DC):
                            pc = (ch % 2) * 512
                            op("tensor", lambda e, t=t, ch=ch, pc=pc:
                               e.matmul(PT[:, pc:pc + 128], FMB[:, ch * 2048 + t * 128: ch * 2048 + (t + 1) * 128], IDB[:], start=True, stop=True),
                               wait=((sV, None), (sA, None)), inc=((sT, 1),))
                            op("vector", lambda e, ch=ch, pc=pc:
                               e.tensor_copy(XRS[:, ch * 128:(ch + 1) * 128], PT[:, pc:pc + 128]),
                               wait=((sT, None),), inc=((sV, 1),))
                        for half in range(2):
                            op("vector", lambda e, t=t, half=half:
                               e.scalar_tensor_tensor(ACC[:, t * 1024 + half * 512: t * 1024 + (half + 1) * 512],
                                                      XRS[:, half * 512:(half + 1) * 512], SWB[:, t:t + 1],
                                                      P2[:, half * 512:(half + 1) * 512], ALU.mult, ALU.add),
                               inc=((sV, 1),))
                        aim = ((sV, sV.n),)
                    for k in range(DC):
                        for fs in range(4):
                            bank = [BA, BA, BB, BB][fs]
                            pa = bank[:, (fs % 2) * 512:(fs % 2) * 512 + 512]
                            w = ()
                            if k == 0 and fs == 0:
                                w = (w1w,) + aim + (zmark_par[par] if zmark_par[par] else ((sA, None), (sV, None)))
                                if par == 1 and evacmarkB[0]:
                                    w = w + evacmarkB[0]
                                if par == 0 and j == 0 and trgmark[0]:
                                    w = w + trgmark[0]
                            op("tensor", lambda e, pa=pa, k=k, fs=fs, t=t:
                               e.matmul(pa, FMB[:, k * 2048 + t * 128: k * 2048 + (t + 1) * 128],
                                        WSA[:, k * 2048 + fs * 512: k * 2048 + (fs + 1) * 512],
                                        start=(k == 0), stop=(k == DC - 1)),
                               wait=w, inc=((sT, 1),) if (k == DC - 1 and fs == 3) else ())
                    last_hh_sT[0] = sT.n
                    op("vector", lambda e, b1t=b1t, BA=BA: e.tensor_add(BA[:, 0:1024], BA[:, 0:1024], b1t[:, 0:1024]),
                       wait=((sT, None), b1w), inc=((sV, 1),))
                    op("vector", lambda e, b1t=b1t, BB=BB: e.tensor_add(BB[:, 0:1024], BB[:, 0:1024], b1t[:, 1024:2048]),
                       inc=((sV, 1),))
                    si, rstd, negm = ln_stats4([BA[:, 0:512], BA[:, 512:1024], BB[:, 0:512], BB[:, 512:1024]])
                    zsl = ZS[:, j * 2048:(j + 1) * 2048]
                    op("scalar", lambda e, zsl=zsl, rstd=rstd, negm=negm, BA=BA:
                       e.activation(zsl[:, 0:1024], BA[:, 0:1024], AF.Identity, bias=negm, scale=rstd),
                       wait=((sV, None),), inc=((sA, 1),))
                    op("scalar", lambda e, zsl=zsl, rstd=rstd, negm=negm, BB=BB:
                       e.activation(zsl[:, 1024:2048], BB[:, 0:1024], AF.Identity, bias=negm, scale=rstd),
                       inc=((sA, 1),))
                    bST.rel(si, sA)
                    zmark_par[par] = ((sA, sA.n),)
                trslots = [P0[:, 0:512], P0[:, 512:1024], P1[:, 0:512], P1[:, 512:1024]]
                gmark = {}

                def w2_fc(fc, jg, bank, first_w):
                    for half in range(2):
                        w = first_w if half == 0 else ()
                        op("tensor", lambda e, fc=fc, half=half, jg=jg, bank=bank:
                           e.matmul(bank[:, half * 512:(half + 1) * 512],
                                    H2T[:, 8192 + fc * 512 + jg * 128: 8192 + fc * 512 + (jg + 1) * 128],
                                    WSB[:, fc * 1024 + half * 512: fc * 1024 + (half + 1) * 512],
                                    start=(fc == 0), stop=(fc == FC - 1)),
                           wait=w, inc=((sT, 1),) if (fc == FC - 1 and half == 1) else ())

                for fc in range(FC):
                    sl = trslots[fc % 4]
                    for j in range(4):
                        w = ()
                        if j == 0:
                            if fc == 0:
                                w = zmark_par[1]
                            if fc >= 4:
                                w = w + ((sA, gmark[fc - 4]),)
                        op("tensor", lambda e, j=j, fc=fc, sl=sl:
                           e.matmul(sl[:, j * 128:(j + 1) * 128], ZS[:, j * 2048 + fc * 128: j * 2048 + (fc + 1) * 128], IDB[:], start=True, stop=True),
                           wait=w, inc=((sT, 1),) if j == 3 else ())
                    op("scalar", lambda e, fc=fc, lg=lg, lb=lb, sl=sl:
                       e.activation(H2T[:, 8192 + fc * 512: 8192 + fc * 512 + 512], sl, AF.Gelu,
                                    bias=lb[:, fc:fc + 1], scale=lg[:, fc:fc + 1]),
                       wait=((sT, None), lgw, lbw), inc=((sA, 1),))
                    gmark[fc] = sA.n
                    if fc >= 1:
                        fw = ((sA, gmark[fc - 1]),)
                        if fc == 1:
                            fw = fw + (w2w,) + zmark_par[1]
                        w2_fc(fc - 1, 0, P2, fw)
                        w2_fc(fc - 1, 1, PT, ())
                w2_fc(FC - 1, 0, P2, ((sA, gmark[FC - 1]),))
                w2_fc(FC - 1, 1, PT, ())
                for jg, bank in ((0, P2), (1, PT)):
                    t = ts[jg]
                    for half in range(2):
                        op("vector", lambda e, t=t, half=half, e_i=e_i, bank=bank:
                           e.scalar_tensor_tensor(ACC[:, t * 1024 + half * 512: t * 1024 + (half + 1) * 512],
                                                  bank[:, half * 512:(half + 1) * 512], FCO[:, t * 8 + e_i: t * 8 + e_i + 1],
                                                  ACC[:, t * 1024 + half * 512: t * 1024 + (half + 1) * 512],
                                                  ALU.mult, ALU.add),
                           wait=((sT, None),) if (jg == 0 and half == 0) else (), inc=((sV, 1),))
                evacA = ((sV, sV.n),)
                for fc in range(FC):
                    fw = evacA + ((sA, gmark[FC - 1]),) if fc == 0 else ()
                    w2_fc(fc, 2, P2, fw)
                    w2_fc(fc, 3, PT, ())
                last_w2_sT[0] = sT.n
                for jg, bank in ((2, P2), (3, PT)):
                    t = ts[jg]
                    for half in range(2):
                        op("vector", lambda e, t=t, half=half, e_i=e_i, bank=bank:
                           e.scalar_tensor_tensor(ACC[:, t * 1024 + half * 512: t * 1024 + (half + 1) * 512],
                                                  bank[:, half * 512:(half + 1) * 512], FCO[:, t * 8 + e_i: t * 8 + e_i + 1],
                                                  ACC[:, t * 1024 + half * 512: t * 1024 + (half + 1) * 512],
                                                  ALU.mult, ALU.add),
                           wait=((sT, None),) if (jg == 2 and half == 0) else (), inc=((sV, 1),))
                evacmarkB[0] = ((sV, sV.n),)
                trgmark[0] = ((sA, gmark[FC - 1]),)
            bB1.rel(b1i, sV)
            bLNG.rel(lgi, sA)

        dump(f"acc{ccid}", QB[:], ((sV, None),))
        dump(f"fco{ccid}", FCO[:], ((sV, None),))
        dump(f"swb{ccid}", SWB[:], ((sV, None),))
        n2gw = dma("gpsimd", NGB[:, 0:1024], n2g.broadcast_to([128, D]), wait=((sV, None),))
        n2bw = dma("gpsimd", NGB[:, 1024:2048], n2b.broadcast_to([128, D]))
        for t in range(NT):
            for ch in range(DC):
                pc = (ch % 2) * 512
                op("tensor", lambda e, t=t, ch=ch, pc=pc:
                   e.matmul(PT[:, pc:pc + 128], FMB[:, ch * 2048 + t * 128: ch * 2048 + (t + 1) * 128], IDB[:], start=True, stop=True),
                   wait=((sV, None),), inc=((sT, 1),))
                op("vector", lambda e, ch=ch, pc=pc:
                   e.tensor_copy(XRS[:, ch * 128:(ch + 1) * 128], PT[:, pc:pc + 128]),
                   wait=((sT, None),), inc=((sV, 1),))
            i, w = bXB.acq()
            z2 = XB[:, i * 1024:(i + 1) * 1024]
            op("vector", lambda e, z2=z2, t=t: e.tensor_add(z2, XRS[:], ACC[:, t * 1024:(t + 1) * 1024]),
               wait=w, inc=((sV, 1),))
            si, rstd, negm = ln_stats(z2)
            t1 = T1F
            op("vector", lambda e, z2=z2, t1=t1, rstd=rstd:
               e.scalar_tensor_tensor(t1, z2, rstd, NGB[:, 0:1024], ALU.mult, ALU.mult),
               wait=(n2gw, n2bw) if t == 0 else (), inc=((sV, 1),))
            op("vector", lambda e, z2=z2, negm=negm:
               e.scalar_tensor_tensor(z2, NGB[:, 0:1024], negm, NGB[:, 1024:2048], ALU.mult, ALU.add),
               inc=((sV, 1),))
            bST.rel(si, sV)
            i2, w2_ = bXB.acq()
            xo = XB[:, i2 * 1024:(i2 + 1) * 1024]
            xw = dma("sync", xo, src_dram[t * 128:(t + 1) * 128, :], wait=w2_)
            op("vector", lambda e, z2=z2, t1=t1: e.tensor_add(t1, t1, z2), inc=((sV, 1),))
            op("vector", lambda e, t1=t1, xo=xo:
               e.scalar_tensor_tensor(xo, t1, 0.5, xo, ALU.mult, ALU.add),
               wait=(xw,), inc=((sV, 1),))
            _last_dma = dma("sync", dst_dram[t * 128:(t + 1) * 128, :], xo, wait=((sV, None),))
            bXB.rel(i, sV)
            bXB.free_at[i2].append(_last_dma)

    def attn(src_dram, dst_dram):
        cci, cco = cc_in[1], cc_out[1]
        ln0(src_dram, FMB)
        aw1 = dma("sync", KRELS[:], krel, wait=((sT, None), (sV, None), (sA, None)))
        dma("sync", QRELS[:], qrel)
        KCOL = ROWS[:, 256:288]
        BIASB = ROWS[:, 288:352]
        kclw = dma("sync", KCOL, kcol)
        op("vector", lambda e: e.tensor_scalar_mul(SM16[:, 32:48], SM16[:, 0:16], -1.0),
           wait=((sA, None),), inc=((sV, 1),))
        dma("sync", AMS.rearrange("p (a c) -> p a c", c=128), amask.rearrange("a p c -> p a c"))
        dma("sync", CMS.rearrange("p (a c) -> p a c", c=128), cmask.rearrange("a p c -> p a c"))
        ACW_T = ALLD()
        dump("krels", KRELS[:], ALLD())
        dump("qrels", QRELS[:], ALLD())

        op("vector", lambda e: e.tensor_scalar_mul(ROWS[0:1, 96:112], ROWS[0:1, 16:32], 8.0), inc=((sV, 1),))
        op("vector", lambda e: e.reciprocal(ROWS[0:1, 32:48], ROWS[0:1, 96:112]),
           inc=((sV, 1),))
        qw_ = dma("sync", ROWS[0:1, 48:64], qpa)
        op("scalar", lambda e: e.activation(ROWS[0:1, 64:80], ROWS[0:1, 48:64], AF.Sin),
           wait=(qw_,), inc=((sA, 1),))
        op("scalar", lambda e: e.activation(ROWS[0:1, 80:96], ROWS[0:1, 48:64], AF.Sin, bias=PI2C[0:1, 0:1]),
           inc=((sA, 1),))
        w1_ = dma("sync", rowsd[0:1, 0:16], ROWS[0:1, 64:80], wait=((sA, None),))
        w2_ = dma("sync", rowsd[0:1, 16:32], ROWS[0:1, 80:96], wait=((sA, None),))
        w3_ = dma("sync", rowsd[0:1, 32:48], ROWS[0:1, 32:48], wait=((sA, None),))
        wsin = dma("sync", SM16[:, 80:96], rowsd[0:1, 0:16].broadcast_to([128, 16]),
                   wait=(w1_, (sA, None), (sT, None), (sV, None)))
        wcos = dma("sync", SM16[:, 96:112], rowsd[0:1, 16:32].broadcast_to([128, 16]), wait=(w2_,))
        wrs8 = dma("sync", SM16[:, 112:128], rowsd[0:1, 32:48].broadcast_to([128, 16]), wait=(w3_,))

        wqrot = WSA
        for k in range(4):
            wqw = dma("sync", WQSTG[:, 0:1024], wq[k * 128:(k + 1) * 128, :], wait=((sV, None),))
            for h in range(H):
                wq_h = WQSTG[:, h * 64:(h + 1) * 64]
                ev = wq_h.rearrange("p (a b) -> p a b", b=2)[:, :, 0].rearrange("p a -> p a")
                od = wq_h.rearrange("p (a b) -> p a b", b=2)[:, :, 1].rearrange("p a -> p a")
                cc_ = SM16[:, 96 + h:97 + h]; ss_ = SM16[:, 80 + h:81 + h]; rr_ = SM16[:, 112 + h:113 + h]
                t1 = WQSTG[:, 1024:1056]; t2 = WQSTG[:, 1056:1088]
                ww = (wqw, wsin, wcos, wrs8, (sT, None)) if (h == 0) else ()
                op("vector", lambda e, ev=ev, cc_=cc_, t1=t1: e.tensor_scalar_mul(t1, ev, cc_), wait=ww, inc=((sV, 1),))
                op("vector", lambda e, od=od, ss_=ss_, t2=t2: e.tensor_scalar_mul(t2, od, ss_), inc=((sV, 1),))
                op("vector", lambda e, t1=t1, t2=t2: e.tensor_sub(t1, t1, t2), inc=((sV, 1),))
                op("vector", lambda e, t1=t1, rr_=rr_: e.tensor_scalar_mul(t1, t1, rr_), inc=((sV, 1),))
                op("vector", lambda e, ev=ev, ss_=ss_, t2=t2: e.tensor_scalar_mul(t2, ev, ss_), inc=((sV, 1),))
                op("vector", lambda e, od=od, cc_=cc_, od2=WQSTG[:, 1088:1120]: e.tensor_scalar_mul(od2, od, cc_), inc=((sV, 1),))
                op("vector", lambda e, t2=t2, od2=WQSTG[:, 1088:1120]: e.tensor_add(t2, t2, od2), inc=((sV, 1),))
                op("vector", lambda e, t2=t2, rr_=rr_: e.tensor_scalar_mul(t2, t2, rr_), inc=((sV, 1),))
                wr_ev = wqrot[:, k * 1024 + h * 64:(k * 1024) + (h + 1) * 64].rearrange("p (a b) -> p a b", b=2)[:, :, 0].rearrange("p a -> p a")
                wr_od = wqrot[:, k * 1024 + h * 64:(k * 1024) + (h + 1) * 64].rearrange("p (a b) -> p a b", b=2)[:, :, 1].rearrange("p a -> p a")
                op("vector", lambda e, t1=t1, wr_ev=wr_ev: e.tensor_copy(wr_ev, t1), inc=((sV, 1),))
                op("vector", lambda e, t2=t2, wr_od=wr_od: e.tensor_copy(wr_od, t2), inc=((sV, 1),))

        wkw = dma("gpsimd", WSA[:, 4096:5120].rearrange("p (c o) -> p c o", o=256),
                  wk.rearrange("(c p) o -> p c o", p=128), wait=((sT, None),))
        wvw = dma("gpsimd", WSA[:, 5120:6144].rearrange("p (c o) -> p c o", o=256),
                  wv.rearrange("(c p) o -> p c o", p=128))
        wks = WSA[:, 4096:5120]
        wvs = WSA[:, 5120:6144]
        ktv = cci[0:1, 0:KT2_ELE].rearrange("a (p c) -> (a p) c", p=64)
        vtv = cci[0:1, KT2_ELE:KT2_ELE + VT2_ELE].rearrange("a (p c) -> (a p) c", p=128)
        for oc in range(2):
            for sp in range(4):
                pa = [P0, P1][(oc * 4 + sp) % 2][:, 0:512]
                for k in range(4):
                    w = ((wkw, (sA, None), (sV, None)) if k == 0 else ())
                    op("tensor", lambda e, pa=pa, k=k, oc=oc, sp=sp:
                       e.matmul(pa, wks[:, k * 256 + oc * 128: k * 256 + (oc + 1) * 128],
                                FMB[:, (4 + k) * 2048 + sp * 512: (4 + k) * 2048 + (sp + 1) * 512],
                                start=(k == 0), stop=(k == 3)),
                       wait=w, inc=((sT, 1),) if k == 3 else ())
                ei, ew = bEB.acq()
                esl = EB[:, ei * 1024: ei * 1024 + 512]
                op("scalar", lambda e, esl=esl, pa=pa: e.copy(esl, pa),
                   wait=((sT, None),) + ew, inc=((sA, 1),))
                for half in range(2):
                    kvh = 2 * oc + half
                    _last_dma = dma("sync", ktv[0:64, kvh * TOK + sp * 512: kvh * TOK + (sp + 1) * 512],
                        esl[half * 64:half * 64 + 64, 0:512], wait=((sA, None),))
                bEB.free_at[ei].append(_last_dma)
        for oc in range(2):
            for sp in range(4):
                pa = [P0, P1][(oc * 4 + sp) % 2][:, 512:1024]
                for k in range(4):
                    w = ((wvw, (sA, None), (sV, None)) if k == 0 else ())
                    op("tensor", lambda e, pa=pa, k=k, oc=oc, sp=sp:
                       e.matmul(pa, wvs[:, k * 256 + oc * 128: k * 256 + (oc + 1) * 128],
                                FMB[:, k * 2048 + sp * 512: k * 2048 + (sp + 1) * 512],
                                start=(k == 0), stop=(k == 3)),
                       wait=w, inc=((sT, 1),) if k == 3 else ())
                ei, ew = bEB.acq()
                esl = EB[:, ei * 1024: ei * 1024 + 512]
                op("scalar", lambda e, esl=esl, pa=pa: e.copy(esl, pa),
                   wait=((sT, None),) + ew, inc=((sA, 1),))

                def sink(t, pts, oc=oc):
                    zi, zw = bZV.acq()
                    zv = ZS[:, 16384 + zi * 256: 16384 + zi * 256 + 256]
                    op("vector", lambda e, zv=zv, pts=pts: e.tensor_copy(zv[:, 0:64], pts[:, 0:64]),
                       wait=((sT, None),) + zw, inc=((sV, 1),))
                    op("vector", lambda e, zv=zv, pts=pts: e.tensor_copy(zv[:, 65:129], pts[:, 64:128]), inc=((sV, 1),))
                    op("vector", lambda e, zv=zv: e.memset(zv[:, 64:65], 1.0), inc=((sV, 1),))
                    op("vector", lambda e, zv=zv: e.memset(zv[:, 129:130], 1.0), inc=((sV, 1),))
                    _last_dma = dma("sync", vtv[:, t * (KVH * 65) + (2 * oc) * 65: t * (KVH * 65) + (2 * oc) * 65 + 130],
                        zv[:, 0:130], wait=((sV, None),))
                    bZV.free_at[zi].append(_last_dma)
                transpose_to_tok(esl, sp, sink)
                bEB.rel(ei, sT)
        op("gpsimd", lambda e: e.collective_compute(
            "AllGather", ALU.bypass,
            replica_groups=[[0, 1], [2, 3], [4, 5], [6, 7]],
            ins=[cci.opt()], outs=[cco.opt()]),
           wait=(*ALLD(), *ALLG()), inc=((sCC, 1),))
        ccw = (sCC, sCC.n)

        for oc in range(DC):
            for sp in range(4):
                idx = oc * 4 + sp
                bank = [P0, P1][idx % 2]
                pa = bank[:, 0:512] if (idx % 4) < 2 else bank[:, 512:1024]
                for k in range(4):
                    w = ((sV, None), (sA, None)) if k == 0 else ()
                    op("tensor", lambda e, pa=pa, k=k, oc=oc, sp=sp:
                       e.matmul(pa, wqrot[:, k * 1024 + oc * 128: k * 1024 + (oc + 1) * 128],
                                FMB[:, (4 + k) * 2048 + sp * 512: (4 + k) * 2048 + (sp + 1) * 512],
                                start=(k == 0), stop=(k == 3)),
                       wait=w, inc=((sT, 1),) if k == 3 else ())
                op("scalar", lambda e, pa=pa, oc=oc, sp=sp:
                   e.copy(QB[:, oc * 2048 + sp * 512: oc * 2048 + (sp + 1) * 512], pa),
                   wait=((sT, None),), inc=((sA, 1),))

        khA = WSB
        for kvh in range(KVH):
            pcb = kvh * 4096
            for r in range(2):
                src = cco[r:r + 1, 0:KT2_ELE].rearrange("a (p c) -> (a p) c", p=64)[0:64, kvh * TOK:(kvh + 1) * TOK]
                src = src.rearrange("p (lt c) -> p lt c", c=128)
                for hb in range(2):
                    dst = khA[hb * 64:hb * 64 + 64, pcb:pcb + 4096].rearrange("p (lt c) -> p lt c", c=256)
                    dma("sync", dst[:, :, r * 128:(r + 1) * 128], src, wait=(ccw, (sT, None)))
                vsrc = cco[r:r + 1, KT2_ELE:KT2_ELE + VT2_ELE].rearrange("a (p c) -> (a p) c", p=128)
                vsrc = vsrc.rearrange("p (t x) -> p t x", x=KVH * 65)[:, :, kvh * 65:(kvh + 1) * 65]
                vdst = vAt[:, 0:8320].rearrange("p (cc x) -> p cc x", x=260)[:, :, kvh * 65:(kvh + 1) * 65]
                vdst = vdst.rearrange("p (lt two) x -> p lt two x", two=2)[:, :, r, :]
                dma("sync", vdst, vsrc, wait=(ccw, (sT, None), (sA, None)))
        kvw = ALLD()

        dump("qa", QB[:], ((sA, None),))
        dump("kha", WSB[:], (*ALLD(),))
        MPW = ((sA, sA.n), (sV, sV.n))
        for t in range(NT):
            npieces = (2 * t + 2 + 7) // 8
            for h in range(H):
                kvh = h // 4
                prow = (h % 2) * 64
                pcb = kvh * 4096
                for p_i in range(npieces):
                    klo = p_i * 1024
                    khi = min((2 * t + 2) * 128, klo + 1024)
                    si, sw = bSP.acq()
                    bank = [P0, P1, PT][si]
                    a = klo
                    first = True
                    while a < khi:
                        b_ = min(a + 512, khi)
                        w = kvw + ACW_T + MPW + sw if first else ()
                        op("tensor", lambda e, bank=bank, a=a, b_=b_, klo=klo, h=h, t=t, prow=prow, pcb=pcb:
                           e.matmul(bank[:, a - klo: b_ - klo],
                                    QB[prow:prow + 64, (h // 2) * 2048 + t * 128:(h // 2) * 2048 + (t + 1) * 128],
                                    khA[prow:prow + 64, pcb + a: pcb + b_],
                                    start=True, stop=False),
                           wait=w)
                        op("tensor", lambda e, bank=bank, a=a, b_=b_, klo=klo, t=t:
                           e.matmul(bank[:, a - klo: b_ - klo],
                                    QRELS[:, t * 128:(t + 1) * 128], KRELS[:, a:b_],
                                    start=False, stop=True),
                           inc=((sT, 1),))
                        first = False
                        a = b_
                    if khi == (2 * t + 2) * 128:
                        off = (khi - klo) - 256
                        op("vector", lambda e, bank=bank, off=off:
                           e.tensor_add(bank[:, off:off + 256], bank[:, off:off + 256], AMS[:]),
                           wait=((sT, None),), inc=((sV, 1),))
                    op("vector", lambda e, bank=bank, p_i=p_i, khi=khi, klo=klo:
                       e.reduce_max(GSC[:, 24 + p_i: 25 + p_i], bank[:, 0:khi - klo], AX.X),
                       wait=((sT, None),), inc=((sV, 1),))
                    bSP.rel(si, sV)
                op("vector", lambda e, npieces=npieces, h=h, t=t:
                   e.reduce_max(MBUF[:, h * 16 + t: h * 16 + t + 1], GSC[:, 24:24 + npieces], AX.X),
                   inc=((sV, 1),))
        for h in range(H):
            op("vector", lambda e, h=h: e.reduce_max(MH16[:, h:h + 1], MBUF[:, h * 16:(h + 1) * 16], AX.X),
               inc=((sV, 1),))
        op("vector", lambda e: e.tensor_mul(MH16[:], MH16[:], SM16[:, 0:16]), inc=((sV, 1),))
        op("vector", lambda e: e.tensor_copy(MHB[:], MH16[:]), inc=((sV, 1),))
        op("tensor", lambda e: e.matmul(PT[0:16, 0:128], MHB[:], IDB[:], start=True, stop=True),
           wait=((sV, None), (sA, None)), inc=((sT, 1),))
        op("vector", lambda e: e.tensor_copy(XB[0:16, 0:128], PT[0:16, 0:128]),
           wait=((sT, None),), inc=((sV, 1),))
        op("vector", lambda e: e.reduce_max(XB[0:16, 128:129], XB[0:16, 0:128], AX.X), inc=((sV, 1),))
        wmh = dma("sync", rowsd[0:1, 100:116], XB[0:16, 128:129], wait=((sV, None),))
        wmh2 = dma("sync", ROWS[0:1, 200:216], rowsd[0:1, 100:116], wait=(wmh,))
        op("vector", lambda e: e.tensor_scalar_mul(ROWS[0:1, 216:232], ROWS[0:1, 200:216], -1.0),
           wait=(wmh2,), inc=((sV, 1),))
        nmhw = bounce_bcast(ROWS[0:1, 216:232], 16, SM16[:, 16:32], 1, 128, wait=((sV, None),))
        expm_w = {}
        for h in range(H):
            op("scalar", lambda e, h=h:
               e.activation(EXPM[:, h * 16:(h + 1) * 16], MBUF[:, h * 16:(h + 1) * 16], AF.Exp,
                            bias=SM16[:, 16 + h:17 + h], scale=SM16[:, h:h + 1]),
               wait=(nmhw, (sV, None)), inc=((sA, 1),))
            op("vector", lambda e, h=h: e.tensor_copy(MHB[:], EXPM[:, h * 16:(h + 1) * 16]),
               wait=((sA, None), (sT, None)), inc=((sV, 1),))
            op("tensor", lambda e: e.matmul(PT[0:16, 0:128], MHB[:], IDB[:], start=True, stop=True),
               wait=((sV, None), (sA, None)), inc=((sT, 1),))
            op("scalar", lambda e: e.copy(XB[0:16, 0:128], PT[0:16, 0:128]),
               wait=((sT, None),) + ((expm_w[h - 1],) if h > 0 else ()), inc=((sA, 1),))
            expm_w[h] = dma("sync",
                            rowsd[10 + (h % 8):11 + (h % 8), (h // 8) * 2048:(h // 8) * 2048 + 2048]
                            .rearrange("a (t q) -> (a t) q", q=128),
                            XB[0:16, 0:128], wait=((sA, None),))

        nmark2 = [()]
        ESL3b = [EB[:, 0:1024], EB[:, 1024:2048], SCR[:, 18432:19456]]
        eslb_free = [(), (), ()]
        eslb_cnt = [0]
        QAW = ((sA, sA.n),)
        for h in range(H):
            kvh = h // 4
            prow = (h % 2) * 64
            pcb = kvh * 4096
            bslot = BIASB[:, (h % 2) * 32:(h % 2) * 32 + 32]
            op("vector", lambda e, bslot=bslot, h=h:
               e.tensor_scalar(bslot, KCOL, SM16[:, 32 + h:33 + h], SM16[:, 16 + h:17 + h],
                               ALU.mult, ALU.add),
               wait=(kclw, nmhw, (sA, None)), inc=((sV, 1),))
            bwh = ((sV, sV.n),)
            for qs in range(2):
                cmaxv = 16 * qs + 16
                pend = []
                lastb0 = 7 if qs == 0 else 23

                def emit_av(p, h=h, qs=qs, kvh=kvh, cmaxv=cmaxv, lastb0=lastb0):
                    esl, c, spans, avm, ei = p
                    for sx, (a, b_) in enumerate(spans):
                        stop_ = (c == lastb0) if a < 512 else (c == cmaxv - 1)
                        op("tensor", lambda e, esl=esl, a=a, b_=b_, c=c, kvh=kvh, stop_=stop_:
                           e.matmul(P2[0:65, a:b_], vAt[:, c * 260 + kvh * 65: c * 260 + kvh * 65 + 65],
                                    esl[:, a:b_], start=(c == 0), stop=stop_),
                           wait=avm if sx == 0 else (), inc=((sT, 1),) if (sx == len(spans) - 1 and c == cmaxv - 1) else ())
                    eslb_free[ei] = ((sT, sT.n),)

                for c in range(cmaxv):
                    t0 = max(c // 2 - 8 * qs, 0)
                    cols0 = t0 * 128
                    base = (h // 2) * 2048 + qs * 1024
                    si, sw = bSP.acq()
                    sbank = [P0, P1, PT][si]
                    spans = []
                    if cols0 < 512:
                        spans.append((cols0, 512))
                        spans.append((512, 1024))
                    else:
                        spans.append((cols0, 1024))
                    for sx, (a, b_) in enumerate(spans):
                        w = sw + QAW if sx == 0 else ()
                        op("tensor", lambda e, sbank=sbank, a=a, b_=b_, c=c, base=base, prow=prow, pcb=pcb:
                           e.matmul(sbank[:, a:b_], khA[prow:prow + 64, pcb + c * 128: pcb + (c + 1) * 128],
                                    QB[prow:prow + 64, base + a: base + b_],
                                    start=True, stop=True),
                           wait=w, inc=((sT, 1),))
                    ei = eslb_cnt[0] % 3
                    eslb_cnt[0] += 1
                    ew = eslb_free[ei]
                    esl = ESL3b[ei]
                    op("scalar", lambda e, esl=esl, sbank=sbank, c=c, bslot=bslot, h=h, cols0=cols0:
                       e.activation(esl[:, cols0:1024], sbank[:, cols0:1024], AF.Exp,
                                    bias=bslot[:, c:c + 1], scale=SM16[:, h:h + 1]),
                       wait=((sT, None),) + bwh + ew, inc=((sA, 1),))
                    bSP.rel(si, sA)
                    bt = c // 2
                    if 8 * qs <= bt < 8 * qs + 8:
                        bc = (bt - 8 * qs) * 128
                        op("vector", lambda e, esl=esl, bc=bc, c=c:
                           e.tensor_mul(esl[:, bc:bc + 128], esl[:, bc:bc + 128],
                                        CMS[:, (c % 2) * 128:(c % 2) * 128 + 128]),
                           wait=((sA, None),), inc=((sV, 1),))
                        avm = ((sV, sV.n),)
                    else:
                        avm = ((sA, sA.n),)
                    if c == 0:
                        avm = avm + (nmark2[0] if nmark2[0] else ())
                    pend.append((esl, c, spans, avm, ei))
                    if len(pend) > 2:
                        emit_av(pend.pop(0))
                for p in pend:
                    emit_av(p)
                OSC = SCRF[:, 8192:9216]
                oi, ow = bOS.acq()
                op("vector", lambda e, OSC=OSC: e.tensor_copy(OSC[0:65, :], P2[0:65, 0:1024]),
                   wait=((sT, None),) + ow, inc=((sV, 1),))
                nmark2[0] = ((sV, sV.n),)
                ri, rw = bRW.acq()
                erow = ROWS[0:1, 512 + ri * 1024: 512 + ri * 1024 + 1024]
                wb2 = dma("sync", erow,
                          rowsd[10 + (h % 8):11 + (h % 8), (h // 8) * 2048 + qs * 1024:(h // 8) * 2048 + qs * 1024 + 1024],
                          wait=rw + (expm_w[h],))
                op("vector", lambda e, OSC=OSC: e.tensor_copy(XB[0:1, 2048:3072], OSC[64:65, :]),
                   wait=((sV, None),), inc=((sV, 1),))
                op("vector", lambda e, erow=erow: e.tensor_add(erow, erow, XB[0:1, 2048:3072]),
                   wait=(wb2,), inc=((sV, 1),))
                op("vector", lambda e, erow=erow: e.reciprocal(erow, erow), inc=((sV, 1),))
                rbw = bounce_bcast(erow, 1024, RCPB[0:64, ri * 1024: ri * 1024 + 1024], 4 + ri, 64,
                                   wait=((sV, None),), wait2=((sV, None),))
                op("vector", lambda e, h=h, qs=qs, ri=ri, OSC=OSC:
                   e.scalar_tensor_tensor(FMB[(h % 2) * 64:(h % 2) * 64 + 64,
                                              (h // 2) * 2048 + qs * 1024:(h // 2) * 2048 + (qs + 1) * 1024],
                                          OSC[0:64, :], 1.0, RCPB[0:64, ri * 1024: ri * 1024 + 1024],
                                          ALU.mult, ALU.mult),
                   wait=(rbw,), inc=((sV, 1),))
                bRW.rel(ri, sV)
                bOS.rel(oi, sV)

        dump("aofm", FMB[:], ((sV, None),))
        dump("mbuf", MBUF[:], ((sV, None),))
        wow = load_wslab(WSA[:, 8192:16384], wo, DC, 1024, wait=((sT, None),))
        wosl = WSA[:, 8192:16384]

        def evwo(oc, sp, pa):
            ei, ew = bEB.acq()
            esl = EB[:, ei * 1024: ei * 1024 + 512]
            op("scalar", lambda e, esl=esl, pa=pa: e.copy(esl, pa),
               wait=((sT, None),) + ew, inc=((sA, 1),))

            def sink(t, pts, oc=oc):
                op("vector", lambda e, t=t, oc=oc, pts=pts:
                   e.tensor_copy(QB[:, t * 1024 + oc * 128: t * 1024 + (oc + 1) * 128], pts),
                   wait=((sT, None),), inc=((sV, 1),))
            transpose_to_tok(esl, sp, sink)
            bEB.rel(ei, sT)
        projection(FMB, wosl, DC, evwo, first_wait=(wow,))

        for t in range(NT):
            i, w = bXB.acq()
            xo = XB[:, i * 1024:(i + 1) * 1024]
            xw = dma("sync", xo, src_dram[t * 128:(t + 1) * 128, :], wait=w + (((sV, None),) if t == 0 else ()))
            op("vector", lambda e, xo=xo, t=t:
               e.tensor_add(xo, xo, QB[:, t * 1024:(t + 1) * 1024]),
               wait=(xw,), inc=((sV, 1),))
            _last_dma = dma("sync", dst_dram[t * 128:(t + 1) * 128, :], xo, wait=((sV, None),))
            bXB.free_at[i].append(_last_dma)

    ALLBUFS = [bXB, bEB, bSP, bWH, bB1, bLNG, bOS, bST, bRW, bZV]
    KPHASE = int(os.environ.get("KPHASE", "3"))
    if KPHASE == 1:
        ffn(x_in, out, 0)
    elif KPHASE == 2:
        ffn(x_in, xs, 0)
        attn(xs, out)
    else:
        ffn(x_in, xs, 0)
        if DBG:
            dump("xs1", xs, (*ALLD(),))
        attn(xs, xs)
        if DBG:
            dump("xs2", xs, (*ALLD(),))
        ffn(xs, out, 2)
    op("sync", lambda e: e.nop(), wait=(*ALLD(), *ALLG(), (sCC, None)))

    with nc.Block() as blk:
        sch.flush(blk)
    stack.close()
    return nc


_NC_CACHE = None


def kernel(**inputs):
    global _NC_CACHE
    xfull = np.ascontiguousarray(np.asarray(inputs["L0_attn_x"], dtype=np.float32))

    def cvt(a):
        return np.ascontiguousarray(np.asarray(a), dtype=np.float32)

    base = dict(
        wq=cvt(inputs["L0_attn_Wq"]), wk=cvt(inputs["L0_attn_Wk"]), wv=cvt(inputs["L0_attn_Wv"]),
        wo=cvt(inputs["L0_attn_Wo"]),
        qpa=cvt(inputs["L0_attn_q_phase_angle"]).reshape(1, H),
        als=cvt(inputs["L0_attn_alibi_log_slopes"]).reshape(1, H),
        fq=cvt(inputs["L0_ffn_q_w"]), fk=cvt(inputs["L0_ffn_k_w"]), fv=cvt(inputs["L0_ffn_v_w"]),
        fow=cvt(inputs["L0_ffn_out_w"]), fgw=cvt(inputs["L0_ffn_gate_w"]),
        fgbT=np.ascontiguousarray(cvt(inputs["L0_ffn_gate_b"]).reshape(DC, 128).T),
        fswT=np.ascontiguousarray(cvt(inputs["L0_ffn_scale_w"]).T),
        n1g=cvt(inputs["L0_ffn_n1_g"]).reshape(1, D), n1b=cvt(inputs["L0_ffn_n1_b"]).reshape(1, D),
        n2g=cvt(inputs["L0_ffn_n2_g"]).reshape(1, D), n2b=cvt(inputs["L0_ffn_n2_b"]).reshape(1, D),
        mgw=cvt(inputs["L0_ffn_moe_gate_w"]), mgb=cvt(inputs["L0_ffn_moe_gate_b"]).reshape(1, EE),
        w1=cvt(inputs["L0_ffn_w1"]), b1=cvt(inputs["L0_ffn_b1"]),
        lngT=np.ascontiguousarray(cvt(inputs["L0_ffn_ln_g"]).reshape(EE, FC, 128).transpose(0, 2, 1)),
        lnbT=np.ascontiguousarray(cvt(inputs["L0_ffn_ln_b"]).reshape(EE, FC, 128).transpose(0, 2, 1)),
        w2=cvt(inputs["L0_ffn_w2"]), b2=cvt(inputs["L0_ffn_b2"]),
        rsc=cvt(inputs["L0_ffn_res_scale"]).reshape(1, EE),
        identb=np.eye(128, dtype=BF), identf=np.eye(128, dtype=np.float32),
        krel=np.stack([np.ones(T), -32.0 * (np.arange(T) // 32), np.ones(T),
                       -1.0 * (np.arange(T) % 32)]).astype(BF),
        iot8=np.arange(EE, dtype=np.float32).reshape(1, EE),
        kcol=(np.arange(NC2, dtype=np.float32)[None, :] * 128.0
              + np.arange(128, dtype=np.float32)[:, None]),
    )

    in_maps = []
    for c in range(8):
        b, half = c // 2, c % 2
        rows = np.concatenate([np.arange((2 * t + half) * 128, (2 * t + half + 1) * 128) for t in range(NT)])
        m = dict(base)
        m["x"] = np.ascontiguousarray(xfull[b][rows])
        m["qrel"] = np.stack([np.zeros(TOK), np.ones(TOK), np.zeros(TOK), np.ones(TOK)]).astype(BF)
        r_ = np.arange(128)[:, None]
        i_ = np.arange(128)[None, :]
        am = np.zeros((2, 128, 128), np.float32)
        cm = np.zeros((2, 128, 128), np.float32)
        if half == 0:
            am[0][i_ > r_] = -1e30
            am[1][:, :] = -1e30
            cm[0] = (np.arange(128)[None, :] >= np.arange(128)[:, None]).astype(np.float32)
            cm[1][:, :] = 0.0
        else:
            am[0][:, :] = 0.0
            am[1][i_ > r_] = -1e30
            cm[0][:, :] = 1.0
            cm[1] = (np.arange(128)[None, :] >= np.arange(128)[:, None]).astype(np.float32)
        m["amask"] = am
        m["cmask"] = cm.astype(BF)
        in_maps.append(m)

    if _NC_CACHE is None:
        _NC_CACHE = build_program()
    res = run_bass_kernel_spmd(_NC_CACHE, in_maps, core_ids=list(range(8)))
    import os as _os
    if int(_os.environ.get("KDBG", "0")):
        kernel.last_results = res.results
    outp = np.zeros((B, T, D), np.float32)
    for c in range(8):
        b, half = c // 2, c % 2
        rows = np.concatenate([np.arange((2 * t + half) * 128, (2 * t + half + 1) * 128) for t in range(NT)])
        outp[b][rows] = res.results[c]["out"]
    return outp


# version 19 (final)
# speedup vs baseline: 1.1128x; 1.0156x over previous
import numpy as np
import ml_dtypes
import concourse.bass as bass
import concourse.mybir as mybir
from concourse.bass_utils import run_bass_kernel_spmd

F32 = mybir.dt.float32
BF16 = mybir.dt.bfloat16
AF = mybir.ActivationFunctionType
ALU = mybir.AluOpType
AX = mybir.AxisListType
BF = ml_dtypes.bfloat16

B, T, D = 4, 4096, 1024
NT = 16
TOK = NT * 128
DC = D // 128
H, HD = 16, 64
KVH = 4
FF, FC = 2048, 16
EE = 8
NC2 = T // 128
EPS = 1e-5

ENGS = ("tensor", "vector", "scalar", "gpsimd", "sync")


class Sem:
    def __init__(self, h):
        self.h = h
        self.n = 0


class Buf:
    def __init__(self, n):
        self.free_at = [[] for _ in range(n)]
        self.i = 0
        self.n = n

    def acq(self):
        i = self.i % self.n
        self.i += 1
        w = self.free_at[i]
        self.free_at[i] = []
        return i, tuple(w)

    def rel(self, i, sem):
        self.free_at[i].append((sem, sem.n))


class Sched:
    def __init__(self):
        self.q = {e: [] for e in ENGS}
        self.count = 0
        self.selfsem = {}
        self.maxw = {e: {} for e in ENGS}

    def op(self, eng, fn, wait=(), inc=()):
        wl = []
        mw = self.maxw[eng]
        ss = self.selfsem.get(eng)
        if ss is not None and ss.n > 0:
            wl.append((ss.h, ss.n))
        for w in wait:
            s, v = w
            wl.append((s.h, s.n if v is None else v))
        wl2 = []
        for h, v in wl:
            hid = id(h)
            if v > mw.get(hid, 0):
                mw[hid] = v
                wl2.append((h, v))
        wl = wl2
        il = []
        for s, k in inc:
            s.n += k
            il.append((s.h, k))
        self.count += 1

        def thunk(e, fn=fn, wl=wl, il=il):
            for h, v in wl:
                if v > 0:
                    e.wait_ge(h, v)
            ins = fn(e)
            for h, k in il:
                ins.then_inc(h, k)
        self.q[eng].append(thunk)

    def flush(self, blk):
        for e in ENGS:
            thunks = self.q[e]
            if not thunks:
                continue

            def body(eng, thunks=thunks):
                for t in thunks:
                    t(eng)
            getattr(blk, e)(body)


import os
DBG = int(os.environ.get("KDBG", "0"))


def build_program():
    nc = bass.Bass("TRN2", target_bir_lowering=False,
                   detect_race_conditions=not int(os.environ.get("KNORACE", "0")))
    dbg_outs = {}

    def din(name, shape, dt=F32):
        return nc.dram_tensor(name, list(shape), dt, kind="ExternalInput").ap()

    x_in = din("x", [TOK, D])
    wq = din("wq", [512, D]); wk = din("wk", [512, 256]); wv = din("wv", [512, 256])
    wo = din("wo", [D, D])
    qpa = din("qpa", [1, H]); als = din("als", [1, H])
    fq = din("fq", [D, D]); fkw = din("fk", [D, D]); fvw = din("fv", [D, D])
    fow = din("fow", [D, D]); fgw = din("fgw", [D, D]); fgbT = din("fgbT", [128, DC])
    fswT = din("fswT", [HD, H])
    n1g = din("n1g", [1, D]); n1b = din("n1b", [1, D])
    n2g = din("n2g", [1, D]); n2b = din("n2b", [1, D])
    mgw = din("mgw", [D, EE]); mgb = din("mgb", [1, EE])
    w1 = din("w1", [EE, D, FF]); b1 = din("b1", [EE, FF])
    lngT = din("lngT", [EE, 128, FC]); lnbT = din("lnbT", [EE, 128, FC])
    w2 = din("w2", [EE, FF, D]); b2 = din("b2", [EE, D])
    rsc = din("rsc", [1, EE])
    identb = din("identb", [128, 128], BF16)
    identf = din("identf", [128, 128], F32)
    qrel = din("qrel", [4, TOK], BF16)
    krel = din("krel", [4, T], BF16)
    amask = din("amask", [2, 128, 128], F32)
    cmask = din("cmask", [2, 128, 128], BF16)
    iot8 = din("iot8", [1, EE])
    kcol = din("kcol", [128, NC2])

    out = nc.dram_tensor("out", [TOK, D], F32, kind="ExternalOutput").ap()
    xs = nc.dram_tensor("xs", [TOK, D], F32).ap()
    rowsd = nc.dram_tensor("rowsd", [24, 4096], F32).ap()

    KT_ELE = 64 * (H * TOK)
    VT_ELE = 128 * (NT * H * 65)
    CC1N = KT_ELE + VT_ELE
    KT2_ELE = 64 * (KVH * TOK)
    VT2_ELE = 128 * (NT * KVH * 65)
    CC2N = KT2_ELE + VT2_ELE
    cc_in = [nc.dram_tensor(f"cci{i}", [1, n], BF16).ap()
             for i, n in ((0, CC1N), (1, CC2N), (2, CC1N))]
    cc_out = [nc.dram_tensor(f"cco{i}", [2, n], BF16).ap()
              for i, n in ((0, CC1N), (1, CC2N), (2, CC1N))]

    sch = Sched()
    import contextlib
    stack = contextlib.ExitStack()
    sb = lambda name, shape, dt: stack.enter_context(nc.sbuf_tensor(name, list(shape), dt))
    ps = lambda name, shape, dt: stack.enter_context(nc.psum_tensor(name, list(shape), dt))

    WSA = sb("WSA", [128, 16384], BF16)
    WSB = sb("WSB", [128, 16384], BF16)
    FMB = sb("FMB", [128, 16384], BF16)
    QB = sb("QB", [128, 16384], BF16)
    SCR = sb("SCR", [128, 19456], BF16)
    XB = sb("XB", [128, 3072], F32)
    EB = sb("EB", [128, 2048], BF16)
    STT_ = sb("STAT", [128, 128], F32)
    ROWS = sb("ROWS", [128, 2560], F32)
    NGB = sb("NGB", [128, 2048], BF16)
    LNGB = sb("LNGB", [128, 64], F32)
    IDB = sb("IDB", [128, 128], BF16)
    SM16 = sb("SM16", [128, 128], F32)
    GSC = sb("GSC", [128, 48], F32)
    FCO = sb("FCO", [128, NT * 8], F32)
    SWB = sb("SWB", [128, NT], F32)
    CTB = sb("CTB", [8, NT * 128], BF16)
    FCB = sb("FCB", [128, 8], BF16)
    B2S = sb("B2S", [8, 1024], BF16)
    MGWS = sb("MGWS", [128, 64], BF16)
    ONE1 = sb("ONE1", [128, 1], BF16)
    EPSC = sb("EPSC", [128, 1], F32)
    PI2C = sb("PI2C", [128, 1], F32)
    SWCOL = sb("SWCOL", [128, 16], BF16)
    SCRF = SCR.bitcast(F32)
    ZS = SCR
    H2T = SCR
    XRS = SCR[:, 16384:17408]
    B1T = SCR[:, 17408:19456]
    vAt = SCR
    KRELS = SCR[0:4, 8320:12416]
    QRELS = SCR[0:4, 12416:14464]
    AMS = SCRF[:, 7232:7488]
    CMS = SCR[:, 14976:15232]
    MBUF = SCRF[:, 7616:7872]
    MH16 = SCRF[:, 7872:7888]
    MHB = SCR[:, 15776:15792]
    EXPM = SCRF[:, 7904:8160]
    WQSTG = SCRF[:, 0:2048]
    RCPB = XB
    SCBB = SCR[:, 0:2048]
    T1F = SCRF[:, 1024:2048]

    P0 = ps("P0", [128, 1024], F32)
    P1 = ps("P1", [128, 1024], F32)
    P2 = ps("P2", [128, 1024], F32)
    PT = ps("PT", [128, 1024], F32)

    mk = lambda name: Sem(stack.enter_context(nc.semaphore(name)))
    sT = mk("sT"); sV = mk("sV"); sA = mk("sA"); sCC = mk("sCC")
    sB = mk("sB")
    sch.selfsem = {"vector": sV, "scalar": sA}
    SYNCP = [mk(f"sq{i}") for i in range(8)]
    GPP = [mk(f"gq{i}") for i in range(4)]
    _qrr = {"sync": [0], "gpsimd": [0]}

    def ALLD():
        return tuple((s, s.n) for s in SYNCP)

    def ALLG():
        return tuple((s, s.n) for s in GPP)

    def phase_barrier(bufs):
        W = ALLD() + ALLG() + ((sT, sT.n), (sV, sV.n), (sA, sA.n), (sCC, sCC.n))
        for eng in ENGS:
            op(eng, lambda e: e.nop(), wait=W, inc=((sB, 1),))
        gate = sB.n
        allsems = [sT, sV, sA, sCC] + SYNCP + GPP
        def clear_all(e, hs=[s.h for s in allsems]):
            ins = None
            for h in hs:
                ins = e.sem_clear(h)
            return ins
        op("gpsimd", clear_all, wait=((sB, gate),), inc=((sB, 1),))
        gate2 = sB.n
        for s in allsems:
            s.n = 0
        for eng in ENGS:
            op(eng, lambda e: e.nop(), wait=((sB, gate2),))
        for b in bufs:
            b.free_at = [[] for _ in range(b.n)]
            b.i = 0

    bXB = Buf(3)
    bEB = Buf(2)
    bSP = Buf(3)
    bWH = Buf(2)
    bB1 = Buf(1)
    bLNG = Buf(2)
    bOS = Buf(1)
    bST = Buf(4)
    bRW = Buf(2)
    bZV = Buf(2)

    op = sch.op

    def dma(queue, outap, inap, wait=(), slow=False):
        pool = SYNCP if queue == "sync" else GPP
        rr = _qrr[queue]
        s = pool[rr[0] % len(pool)]
        rr[0] += 1
        op(queue, lambda e, o=outap, i=inap, slow=slow: e.dma_start(out=o, in_=i, allow_slow_non_contiguous=slow),
           wait=tuple(wait) + ((s, s.n),), inc=((s, 16),))
        return (s, s.n)

    def dmaT(queue, outap, inap, wait=()):
        pool = SYNCP if queue == "sync" else GPP
        rr = _qrr[queue]
        s = pool[rr[0] % len(pool)]
        rr[0] += 1
        op(queue, lambda e, o=outap, i=inap: e.dma_start_transpose(out=o, in_=i),
           wait=tuple(wait) + ((s, s.n),), inc=((s, 16),))
        return (s, s.n)

    def bounce_bcast(row_src, ncols, dst_ap, drow_idx, nrows, wait=(), wait2=()):
        w1_ = dma("sync", rowsd[drow_idx:drow_idx + 1, 0:ncols], row_src, wait=wait)
        bc = rowsd[drow_idx:drow_idx + 1, 0:ncols].broadcast_to([nrows, ncols])
        return dma("sync", dst_ap, bc, wait=(w1_,) + tuple(wait2))

    def dump(name, ap, wait):
        if not DBG:
            return
        if name not in dbg_outs:
            dbg_outs[name] = nc.dram_tensor("dbg_" + name, list(ap.shape), ap.dtype, kind="ExternalOutput").ap()
        dma("sync", dbg_outs[name], ap, wait=wait)

    dma("sync", IDB[:], identb)
    dma("sync", SM16[:, 48:56], iot8.broadcast_to([128, EE]))
    dma("sync", SM16[:, 56:64], rsc.broadcast_to([128, EE]))
    dma("sync", SM16[:, 64:72], mgb.broadcast_to([128, EE]))
    dma("gpsimd", SWCOL[0:64, :], fswT)
    dma("gpsimd", SWCOL[64:128, :], fswT)
    for k in range(DC):
        dma("gpsimd", MGWS[:, k * 8:(k + 1) * 8], mgw[k * 128:(k + 1) * 128, :])
    op("vector", lambda e: e.memset(ONE1[:], 1.0), inc=((sV, 1),))
    op("vector", lambda e: e.memset(EPSC[:], EPS), inc=((sV, 1),))
    op("vector", lambda e: e.memset(PI2C[:], 1.5707963267948966), inc=((sV, 1),))
    w_ = dma("sync", ROWS[0:1, 0:16], als)
    op("scalar", lambda e: e.activation(ROWS[0:1, 16:32], ROWS[0:1, 0:16], AF.Exp),
       wait=(w_,), inc=((sA, 1),))
    bounce_bcast(ROWS[0:1, 16:32], 16, SM16[:, 0:16], 0, 128, wait=((sA, None),))
    CW = ALLD() + ALLG() + ((sV, sV.n), (sA, sA.n))
    for eng in ENGS:
        op(eng, lambda e: e.nop(), wait=CW)

    def ln_stats(src_ap, twait=()):
        si, sw = bST.acq()
        st_ = STT_[:, si * 32:(si + 1) * 32]
        op("vector", lambda e, st_=st_: e.bn_stats(st_[:, 0:6], src_ap[:, 0:512]),
           wait=tuple(twait) + sw, inc=((sV, 1),))
        op("vector", lambda e, st_=st_: e.bn_stats(st_[:, 6:12], src_ap[:, 512:1024]), inc=((sV, 1),))
        op("vector", lambda e, st_=st_: e.bn_aggr(st_[:, 12:14], st_[:, 0:12].rearrange("p (n s) -> p n s", s=3)),
           inc=((sV, 1),))
        op("scalar", lambda e, st_=st_: e.activation(st_[:, 16:17], st_[:, 13:14], AF.Sqrt, bias=EPSC[:, 0:1]),
           wait=((sV, None),), inc=((sA, 1),))
        op("vector", lambda e, st_=st_: e.reciprocal(st_[:, 14:15], st_[:, 16:17]),
           wait=((sA, None),), inc=((sV, 1),))
        op("vector", lambda e, st_=st_: e.scalar_tensor_tensor(st_[:, 15:16], st_[:, 12:13], -1.0, st_[:, 14:15], ALU.mult, ALU.mult),
           inc=((sV, 1),))
        return si, st_[:, 14:15], st_[:, 15:16]

    def ln_stats4(srcs, twait=()):
        si, sw = bST.acq()
        st_ = STT_[:, si * 32:(si + 1) * 32]
        for ci, s_ in enumerate(srcs):
            op("vector", lambda e, st_=st_, ci=ci, s_=s_: e.bn_stats(st_[:, ci * 6:(ci + 1) * 6], s_),
               wait=(tuple(twait) + sw) if ci == 0 else (), inc=((sV, 1),))
        op("vector", lambda e, st_=st_: e.bn_aggr(st_[:, 24:26], st_[:, 0:24].rearrange("p (n s) -> p n s", s=3)),
           inc=((sV, 1),))
        op("scalar", lambda e, st_=st_: e.activation(st_[:, 28:29], st_[:, 25:26], AF.Sqrt, bias=EPSC[:, 0:1]),
           wait=((sV, None),), inc=((sA, 1),))
        op("vector", lambda e, st_=st_: e.reciprocal(st_[:, 26:27], st_[:, 28:29]),
           wait=((sA, None),), inc=((sV, 1),))
        op("vector", lambda e, st_=st_: e.scalar_tensor_tensor(st_[:, 27:28], st_[:, 24:25], -1.0, st_[:, 26:27], ALU.mult, ALU.mult),
           inc=((sV, 1),))
        return si, st_[:, 26:27], st_[:, 27:28]

    def ln0(src_dram, dst_fm):
        for t in range(NT):
            i, w = bXB.acq()
            xt = XB[:, i * 1024:(i + 1) * 1024]
            xw = dma("sync", xt, src_dram[t * 128:(t + 1) * 128, :], wait=w)
            si, rstd, negm = ln_stats(xt, twait=(xw,))
            tgt = ZS[:, (t % 2) * 1024:(t % 2) * 1024 + 1024]
            op("scalar", lambda e, xt=xt, tgt=tgt, rstd=rstd, negm=negm:
               e.activation(tgt, xt, AF.Identity, bias=negm, scale=rstd),
               wait=((sV, None), (sT, None)), inc=((sA, 1),))
            bST.rel(si, sA)
            bXB.rel(i, sA)
            cpm = {}
            for ch in range(DC):
                pc = (ch % 2) * 512
                w = ((sA, None), (sV, None)) if ch < 2 else ((sV, cpm[ch % 2]),)
                op("tensor", lambda e, tgt=tgt, ch=ch, pc=pc:
                   e.matmul(PT[:, pc:pc + 128], tgt[:, ch * 128:(ch + 1) * 128], IDB[:], start=True, stop=True),
                   wait=w, inc=((sT, 1),))
                op("vector", lambda e, ch=ch, t=t, pc=pc:
                   e.tensor_copy(dst_fm[:, ch * 2048 + t * 128: ch * 2048 + (t + 1) * 128], PT[:, pc:pc + 128]),
                   wait=((sT, None),), inc=((sV, 1),))
                cpm[ch % 2] = sV.n

    def load_wslab(slab, wdram, kc, width, queue="gpsimd", wait=()):
        last = None
        for k in range(kc):
            last = dma(queue, slab[:, k * width:(k + 1) * width],
                       wdram[k * 128:(k + 1) * 128, :], wait=wait if k == 0 else ())
        return last

    def projection(src_fm, slab, kc, evac, first_wait=()):
        marks = [()] * 4
        for oc in range(DC):
            for sp in range(4):
                idx = oc * 4 + sp
                bank = [P0, P1][idx % 2]
                pa = bank[:, 0:512] if (idx % 4) < 2 else bank[:, 512:1024]
                for k in range(kc):
                    w = ()
                    if k == 0:
                        w = (first_wait + ((sA, None), (sV, None))) if idx == 0 else marks[idx % 4]
                    op("tensor", lambda e, pa=pa, k=k, oc=oc, sp=sp, slab=slab:
                       e.matmul(pa, slab[:, k * 1024 + oc * 128: k * 1024 + (oc + 1) * 128],
                                src_fm[:, k * 2048 + sp * 512: k * 2048 + (sp + 1) * 512],
                                start=(k == 0), stop=(k == kc - 1)),
                       wait=w, inc=((sT, 1),) if k == kc - 1 else ())
                evac(oc, sp, pa)
                marks[idx % 4] = ((sA, sA.n), (sV, sV.n))

    def transpose_to_tok(esl, sp, dst_fn):
        for tp in range(4):
            t = sp * 4 + tp
            pc = (tp % 2) * 512
            op("tensor", lambda e, esl=esl, tp=tp, pc=pc:
               e.matmul(PT[:, pc:pc + 128], esl[:, tp * 128:(tp + 1) * 128], IDB[:], start=True, stop=True),
               wait=((sA, None), (sV, None)), inc=((sT, 1),))
            dst_fn(t, PT[:, pc:pc + 128])

    def ffn(src_dram, dst_dram, ccid):
        cci, cco = cc_in[ccid], cc_out[ccid]
        ln0(src_dram, FMB)
        dump(f"srcfm{ccid}", FMB[:], ((sV, None),))

        whi, whw = bWH.acq()
        ksl = WSB[:, whi * 8192: whi * 8192 + 8192]
        kw = load_wslab(ksl, fkw, DC, 1024, wait=whw)
        ktv = cci[0:1, 0:KT_ELE].rearrange("a (p c) -> (a p) c", p=64)

        def evk(oc, sp, pa):
            ei, ew = bEB.acq()
            esl = EB[:, ei * 1024: ei * 1024 + 1024]
            op("scalar", lambda e, esl=esl, pa=pa: e.copy(esl[:, 0:512], pa),
               wait=((sT, None),) + ew, inc=((sA, 1),))
            op("scalar", lambda e, esl=esl: e.activation(esl[:, 512:1024], esl[:, 0:512], AF.Square),
               wait=((sT, None),), inc=((sA, 1),))
            for half in range(2):
                hh = 2 * oc + half
                _last_dma = dma("sync", ktv[0:64, hh * TOK + sp * 512: hh * TOK + (sp + 1) * 512],
                    esl[half * 64:half * 64 + 64, 0:512], wait=((sA, None),))
                op("tensor", lambda e, esl=esl, half=half:
                   e.matmul(P2[0:1, 0:512], ONE1[half * 64:half * 64 + 64, 0:1],
                            esl[half * 64:half * 64 + 64, 512:1024], start=True, stop=True),
                   wait=((sA, None), (sV, None)), inc=((sT, 1),))
                op("vector", lambda e, hh=hh, sp=sp:
                   e.reduce_max(ROWS[0:1, 2304 + hh * 4 + sp: 2305 + hh * 4 + sp], P2[0:1, 0:512], AX.X),
                   wait=((sT, None),), inc=((sV, 1),))
            bEB.rel(ei, sV)
            bEB.free_at[ei].append(_last_dma)
        projection(FMB, ksl, DC, evk, first_wait=(kw, (sV, None)))
        op("vector", lambda e: e.reduce_max(ROWS[0:1, 2432:2448],
                                            ROWS[0:1, 2304:2368].rearrange("a (h s) -> a h s", s=4), AX.X),
           inc=((sV, 1),))
        bWH.rel(whi, sT)

        whi, whw = bWH.acq()
        vsl = WSB[:, whi * 8192: whi * 8192 + 8192]
        vw = load_wslab(vsl, fvw, DC, 1024, wait=whw)
        vtv = cci[0:1, KT_ELE:KT_ELE + VT_ELE].rearrange("a (p c) -> (a p) c", p=128)

        def evv(oc, sp, pa):
            ei, ew = bEB.acq()
            esl = EB[:, ei * 1024: ei * 1024 + 512]
            op("scalar", lambda e, esl=esl, pa=pa: e.copy(esl, pa),
               wait=((sT, None),) + ew, inc=((sA, 1),))

            def sink(t, pts, oc=oc):
                zi, zw = bZV.acq()
                zv = ZS[:, 6144 + zi * 256: 6144 + zi * 256 + 256]
                op("vector", lambda e, zv=zv, pts=pts: e.tensor_copy(zv[:, 0:64], pts[:, 0:64]),
                   wait=((sT, None),) + zw, inc=((sV, 1),))
                op("vector", lambda e, zv=zv, pts=pts: e.tensor_copy(zv[:, 65:129], pts[:, 64:128]), inc=((sV, 1),))
                op("vector", lambda e, zv=zv: e.memset(zv[:, 64:65], 1.0), inc=((sV, 1),))
                op("vector", lambda e, zv=zv: e.memset(zv[:, 129:130], 1.0), inc=((sV, 1),))
                _last_dma = dma("sync", vtv[:, t * (H * 65) + (2 * oc) * 65: t * (H * 65) + (2 * oc) * 65 + 130],
                    zv[:, 0:130], wait=((sV, None),))
                bZV.free_at[zi].append(_last_dma)
            transpose_to_tok(esl, sp, sink)
            bEB.rel(ei, sT)
        projection(FMB, vsl, DC, evv, first_wait=(vw, (sV, None)))
        bWH.rel(whi, sT)

        op("gpsimd", lambda e, cci=cci, cco=cco: e.collective_compute(
            "AllGather", ALU.bypass,
            replica_groups=[[0, 1], [2, 3], [4, 5], [6, 7]],
            ins=[cci.opt()], outs=[cco.opt()]),
           wait=(*ALLD(), *ALLG()), inc=((sCC, 1),))
        ccw = (sCC, sCC.n)

        whi, whw = bWH.acq()
        qsl = WSB[:, whi * 8192: whi * 8192 + 8192]
        qw = load_wslab(qsl, fq, DC, 1024, wait=whw)

        def evq(oc, sp, pa):
            op("scalar", lambda e, oc=oc, sp=sp, pa=pa:
               e.copy(QB[:, oc * 2048 + sp * 512: oc * 2048 + (sp + 1) * 512], pa),
               wait=((sT, None),), inc=((sA, 1),))
        projection(FMB, qsl, DC, evq, first_wait=(qw, (sV, None)))
        bWH.rel(whi, sA)

        rowdma = None
        rowd = []
        for h in range(H):
            base = (h % 2) * 64
            ch = h // 2
            for sp in range(4):
                op("tensor", lambda e, h=h, sp=sp, base=base, ch=ch:
                   e.matmul(P2[0:1, 0:512], SWCOL[base:base + 64, h:h + 1],
                            QB[base:base + 64, ch * 2048 + sp * 512: ch * 2048 + (sp + 1) * 512],
                            start=True, stop=True),
                   wait=((sA, None), (sV, None)), inc=((sT, 1),))
                w = ((sT, None),) if not (sp == 0 and rowdma) else ((sT, None), rowdma)
                op("scalar", lambda e, sp=sp:
                   e.activation(ROWS[0:1, 512 + sp * 512: 512 + (sp + 1) * 512], P2[0:1, 0:512], AF.Sigmoid),
                   wait=w, inc=((sA, 1),))
            rowdma = dma("sync", rowsd[8 + h:9 + h, 0:2048], ROWS[0:1, 512:2560], wait=((sA, None),))
            rowd.append(rowdma)
        for ch in range(DC):
            scw = dma("gpsimd", SCBB[:],
                      bass.AP(rowsd.tensor, (8 + 2 * ch) * 4096, [[4096, 2], [0, 64], [1, 2048]]),
                      wait=(rowd[2 * ch], rowd[2 * ch + 1], (sV, None)))
            op("vector", lambda e, ch=ch:
               e.scalar_tensor_tensor(QB[:, ch * 2048:(ch + 1) * 2048], QB[:, ch * 2048:(ch + 1) * 2048],
                                      0.25, SCBB[:], ALU.mult, ALU.mult),
               wait=(scw, (sA, None)), inc=((sV, 1),))

        for ch in range(DC):
            op("scalar", lambda e, ch=ch: e.activation(EB[:, 0:1024], QB[:, ch * 2048: ch * 2048 + 1024], AF.Square),
               wait=((sV, None), (sT, None), *ALLD()), inc=((sA, 1),))
            op("scalar", lambda e, ch=ch: e.activation(EB[:, 1024:2048], QB[:, ch * 2048 + 1024: ch * 2048 + 2048], AF.Square),
               inc=((sA, 1),))
            for half in range(2):
                hh = 2 * ch + half
                for sp in range(4):
                    op("tensor", lambda e, half=half, sp=sp:
                       e.matmul(P2[0:1, 0:512], ONE1[half * 64:half * 64 + 64, 0:1],
                                EB[half * 64:half * 64 + 64, sp * 512:(sp + 1) * 512], start=True, stop=True),
                       wait=((sA, None), (sV, None)), inc=((sT, 1),))
                    op("vector", lambda e, hh=hh, sp=sp:
                       e.reduce_max(ROWS[0:1, 2368 + hh * 4 + sp: 2369 + hh * 4 + sp], P2[0:1, 0:512], AX.X),
                       wait=((sT, None),), inc=((sV, 1),))
        op("vector", lambda e: e.reduce_max(ROWS[0:1, 2448:2464],
                                            ROWS[0:1, 2368:2432].rearrange("a (h s) -> a h s", s=4), AX.X),
           inc=((sV, 1),))
        op("vector", lambda e: e.tensor_mul(ROWS[0:1, 2464:2480], ROWS[0:1, 2432:2448], ROWS[0:1, 2448:2464]),
           inc=((sV, 1),))
        op("scalar", lambda e: e.activation(ROWS[0:1, 2480:2496], ROWS[0:1, 2464:2480], AF.Sqrt),
           wait=((sV, None),), inc=((sA, 1),))
        op("vector", lambda e: e.tensor_scalar_mul(ROWS[0:1, 2496:2512], ROWS[0:1, 2480:2496], -1.0),
           wait=((sA, None),), inc=((sV, 1),))
        cbw = bounce_bcast(ROWS[0:1, 2496:2512], 16, SM16[:, 32:48], 1, 128, wait=((sV, None),))
        dump(f"qt{ccid}", QB[:], ((sV, None),))
        dump(f"crow{ccid}", ROWS[0:1, 2432:2512], ((sV, None),))

        QRDY = ((sV, sV.n),)
        nmark = [()]
        ESL3 = [EB[:, 0:1024], EB[:, 1024:2048], SCR[:, 2048:3072]]
        esl_free = [(), (), ()]
        esl_cnt = [0]
        for h in range(H):
            khs = WSB[:, (h % 2) * 4096:(h % 2) * 4096 + 4096]
            vhs = WSB[:, 8192 + (h % 2) * 2080: 8192 + (h % 2) * 2080 + 2080]
            kw_ = []
            vw_ = []
            hwait = (ccw,) if h < 2 else (ccw, (sT, None))
            for r in range(2):
                src = cco[r:r + 1, 0:KT_ELE].rearrange("a (p c) -> (a p) c", p=64)[0:64, h * TOK:(h + 1) * TOK]
                kw_.append(dma("sync", khs[0:64, r * TOK:(r + 1) * TOK], src, wait=hwait))
                kw_[-1] = dma("sync", khs[64:128, r * TOK:(r + 1) * TOK], src, wait=hwait)
                vsrc = cco[r:r + 1, KT_ELE:KT_ELE + VT_ELE].rearrange("a (p c) -> (a p) c", p=128)
                vsrc = vsrc.rearrange("p (t x) -> p t x", x=H * 65)[:, :, h * 65:(h + 1) * 65]
                vw_.append(dma("sync", vhs[:, r * NT * 65:(r + 1) * NT * 65].rearrange("p (t x) -> p t x", x=65),
                               vsrc, wait=hwait))
            for qs in range(2):
                pend = []

                def emit_av(p, vhs=vhs):
                    pesl, pc, pm, pei = p
                    for half in range(2):
                        w = (pm, vw_[pc // NT]) if half == 0 else ()
                        if pc == 0 and half == 0:
                            w = w + (nmark[0] if nmark[0] else ())
                        op("tensor", lambda e, esl=pesl, half=half, c=pc, vhs=vhs:
                           e.matmul(P2[0:65, half * 512:(half + 1) * 512],
                                    vhs[:, c * 65:(c + 1) * 65],
                                    esl[:, half * 512:(half + 1) * 512],
                                    start=(c == 0), stop=(c == NC2 - 1)),
                           wait=w, inc=((sT, 1),) if half == 1 else ())
                    esl_free[pei] = ((sT, sT.n),)

                for c in range(NC2):
                    si, sw = bSP.acq()
                    sbank = [P0, P1, PT][si]
                    for half in range(2):
                        w = ()
                        if half == 0:
                            w = sw + (kw_[c // NT], cbw) + QRDY
                        op("tensor", lambda e, sbank=sbank, half=half, c=c, h=h, qs=qs, khs=khs:
                           e.matmul(sbank[:, half * 512:(half + 1) * 512],
                                    khs[(h % 2) * 64:(h % 2) * 64 + 64, c * 128:(c + 1) * 128],
                                    QB[(h % 2) * 64:(h % 2) * 64 + 64,
                                       (h // 2) * 2048 + qs * 1024 + half * 512:(h // 2) * 2048 + qs * 1024 + (half + 1) * 512],
                                    start=True, stop=True),
                           wait=w, inc=((sT, 1),) if half == 1 else ())
                    ei = esl_cnt[0] % 3
                    esl_cnt[0] += 1
                    ew = esl_free[ei]
                    esl = ESL3[ei]
                    op("scalar", lambda e, esl=esl, sbank=sbank, h=h:
                       e.activation(esl, sbank[:, 0:1024], AF.Exp, bias=SM16[:, 32 + h:33 + h], scale=1.0),
                       wait=((sT, None),) + ew, inc=((sA, 1),))
                    bSP.rel(si, sA)
                    pend.append((esl, c, (sA, sA.n), ei))
                    if len(pend) > 2:
                        emit_av(pend.pop(0))
                for p in pend:
                    emit_av(p)
                OSC = SCRF[:, 0:1024]
                oi, ow = bOS.acq()
                op("vector", lambda e, OSC=OSC: e.tensor_copy(OSC[0:65, :], P2[0:65, 0:1024]),
                   wait=((sT, None),) + ow, inc=((sV, 1),))
                nmark[0] = ((sV, sV.n),)
                if h == 0 and qs == 0:
                    dump(f"osc{ccid}", OSC[:], ((sV, None),))
                ri, rw = bRW.acq()
                rrow = ROWS[0:1, 512 + ri * 1024: 512 + ri * 1024 + 1024]
                op("vector", lambda e, OSC=OSC: e.tensor_copy(XB[0:1, 2048:3072], OSC[64:65, :]),
                   wait=rw, inc=((sV, 1),))
                op("vector", lambda e, rrow=rrow: e.reciprocal(rrow, XB[0:1, 2048:3072]),
                   inc=((sV, 1),))
                rbw = bounce_bcast(rrow, 1024, RCPB[0:64, ri * 1024: ri * 1024 + 1024], 2 + ri, 64,
                                   wait=((sV, None),), wait2=((sV, None),))
                if h == 0 and qs == 0:
                    dump(f"rcpb{ccid}", RCPB[0:64, 0:1024], (rbw,))
                op("vector", lambda e, h=h, qs=qs, ri=ri, OSC=OSC:
                   e.scalar_tensor_tensor(FMB[(h % 2) * 64:(h % 2) * 64 + 64,
                                              (h // 2) * 2048 + qs * 1024:(h // 2) * 2048 + (qs + 1) * 1024],
                                          OSC[0:64, :], 1.0, RCPB[0:64, ri * 1024: ri * 1024 + 1024],
                                          ALU.mult, ALU.mult),
                   wait=(rbw,), inc=((sV, 1),))
                bRW.rel(ri, sV)
                bOS.rel(oi, sV)

        dump(f"ofm{ccid}", FMB[:], ((sV, None),))
        whi, whw = bWH.acq()
        gsl = WSB[:, whi * 8192: whi * 8192 + 8192]
        gw = load_wslab(gsl, fgw, DC, 1024, wait=whw + ((sV, None),))
        gbw = dma("sync", SM16[:, 72:80], fgbT, wait=())

        GBUF = SCR[:, 0:16384]

        def evg(oc, sp, pa):
            op("scalar", lambda e, pa=pa, oc=oc, sp=sp:
               e.activation(GBUF[:, oc * 2048 + sp * 512: oc * 2048 + (sp + 1) * 512], pa,
                            AF.Sigmoid, bias=SM16[:, 72 + oc:73 + oc]),
               wait=((sT, None), gbw), inc=((sA, 1),))
        projection(FMB, gsl, DC, evg, first_wait=(gw,))
        for oc in range(DC):
            op("vector", lambda e, oc=oc:
               e.tensor_mul(FMB[:, oc * 2048:(oc + 1) * 2048],
                            FMB[:, oc * 2048:(oc + 1) * 2048], GBUF[:, oc * 2048:(oc + 1) * 2048]),
               wait=((sT, None), (sA, None)) if oc == 0 else (), inc=((sV, 1),))
        bWH.rel(whi, sT)

        whi, whw = bWH.acq()
        osl_w = WSB[:, whi * 8192: whi * 8192 + 8192]
        oww = load_wslab(osl_w, fow, DC, 1024, wait=whw + ((sV, None),))

        def evo(oc, sp, pa):
            ei, ew = bEB.acq()
            esl = EB[:, ei * 1024: ei * 1024 + 512]
            op("scalar", lambda e, esl=esl, pa=pa: e.copy(esl, pa),
               wait=((sT, None),) + ew, inc=((sA, 1),))

            def sink(t, pts, oc=oc):
                op("vector", lambda e, t=t, oc=oc, pts=pts:
                   e.tensor_copy(QB[:, t * 1024 + oc * 128: t * 1024 + (oc + 1) * 128], pts),
                   wait=((sT, None),), inc=((sV, 1),))
            transpose_to_tok(esl, sp, sink)
            bEB.rel(ei, sT)
        projection(FMB, osl_w, DC, evo, first_wait=(oww,))
        bWH.rel(whi, sT)

        n1gw = dma("gpsimd", NGB[:, 0:1024], n1g.broadcast_to([128, D]), wait=((sV, None),))
        n1bw = dma("gpsimd", NGB[:, 1024:2048], n1b.broadcast_to([128, D]))
        for t in range(NT):
            i, w = bXB.acq()
            xt = XB[:, i * 1024:(i + 1) * 1024]
            xw = dma("sync", xt, src_dram[t * 128:(t + 1) * 128, :], wait=w + (((sV, None), (sA, None)) if t == 0 else ()))
            si0, rstd0, negm0 = ln_stats(xt, twait=(xw,))
            s1 = ZS[:, (t % 2) * 1024:(t % 2) * 1024 + 1024]
            op("scalar", lambda e, xt=xt, s1=s1, rstd0=rstd0, negm0=negm0:
               e.activation(s1, xt, AF.Identity, bias=negm0, scale=rstd0),
               wait=((sV, None), (sT, None)), inc=((sA, 1),))
            bST.rel(si0, sA)
            op("vector", lambda e, xt=xt, s1=s1, t=t:
               e.tensor_add(xt, s1, QB[:, t * 1024:(t + 1) * 1024]),
               wait=((sA, None),), inc=((sV, 1),))
            si, rstd, negm = ln_stats(xt)
            t1 = T1F
            op("vector", lambda e, xt=xt, t1=t1, rstd=rstd:
               e.scalar_tensor_tensor(t1, xt, rstd, NGB[:, 0:1024], ALU.mult, ALU.mult),
               wait=(n1gw, n1bw) if t == 0 else (), inc=((sV, 1),))
            op("vector", lambda e, xt=xt, negm=negm:
               e.scalar_tensor_tensor(xt, NGB[:, 0:1024], negm, NGB[:, 1024:2048], ALU.mult, ALU.add),
               inc=((sV, 1),))
            op("vector", lambda e, xt=xt, t1=t1, s1=s1:
               e.tensor_add(s1, t1, xt), inc=((sV, 1),))
            bST.rel(si, sV)
            bXB.rel(i, sV)
            cpm = {}
            for ch in range(DC):
                pc = (ch % 2) * 512
                w = ((sV, None),) if ch < 2 else ((sV, cpm[ch % 2]),)
                op("tensor", lambda e, s1=s1, ch=ch, pc=pc:
                   e.matmul(PT[:, pc:pc + 128], s1[:, ch * 128:(ch + 1) * 128], IDB[:], start=True, stop=True),
                   wait=w, inc=((sT, 1),))
                op("vector", lambda e, t=t, ch=ch, pc=pc:
                   e.tensor_copy(FMB[:, ch * 2048 + t * 128: ch * 2048 + (t + 1) * 128], PT[:, pc:pc + 128]),
                   wait=((sT, None),), inc=((sV, 1),))
                cpm[ch % 2] = sV.n

        b2_w = dma("gpsimd", B2S[:], b2, wait=((sV, None),))
        dump(f"x1fm{ccid}", FMB[:], ((sV, None),))
        dump(f"ao{ccid}", QB[:], ((sV, None),))
        for t in range(NT):
            for k in range(DC):
                op("tensor", lambda e, t=t, k=k:
                   e.matmul(P2[:, 0:8], FMB[:, k * 2048 + t * 128: k * 2048 + (t + 1) * 128],
                            MGWS[:, k * 8:(k + 1) * 8], start=(k == 0), stop=(k == DC - 1)),
                   wait=((sV, None), *ALLG()) if k == 0 else (), inc=((sT, 1),) if k == DC - 1 else ())
            L = GSC[:, 0:8]
            wk_ = GSC[:, 8:16]; idsc = GSC[:, 16:24]
            m1 = GSC[:, 40:41]; m1n = GSC[:, 41:42]; m2 = GSC[:, 42:43]
            e2 = GSC[:, 43:44]; s_ = GSC[:, 44:45]; rcp = GSC[:, 45:46]; cmx = GSC[:, 46:47]
            sw_ = SWB[:, t:t + 1]
            op("vector", lambda e, L=L: e.tensor_add(L, P2[:, 0:8], SM16[:, 64:72]),
               wait=((sT, None),), inc=((sV, 1),))
            op("vector", lambda e, L=L, m1=m1: e.reduce_max(m1, L, AX.X),
               wait=((sA, None),), inc=((sV, 1),))
            op("vector", lambda e, m1=m1, m1n=m1n: e.tensor_scalar_mul(m1n, m1, -1.0), inc=((sV, 1),))
            op("vector", lambda e, L=L, m1=m1, wk_=wk_: e.tensor_scalar(wk_, L, m1, None, ALU.is_ge), inc=((sV, 1),))
            op("vector", lambda e, L=L, wk_=wk_: e.scalar_tensor_tensor(wk_, wk_, -1.0e30, L, ALU.mult, ALU.add), inc=((sV, 1),))
            op("vector", lambda e, m2=m2, wk_=wk_: e.reduce_max(m2, wk_, AX.X), inc=((sV, 1),))
            op("scalar", lambda e, m2=m2, m1n=m1n, e2=e2: e.activation(e2, m2, AF.Exp, bias=m1n),
               wait=((sV, None),), inc=((sA, 1),))
            op("vector", lambda e, e2=e2, s_=s_: e.tensor_scalar_add(s_, e2, 1.0),
               wait=((sA, None),), inc=((sV, 1),))
            op("vector", lambda e, s_=s_, rcp=rcp: e.reciprocal(rcp, s_), inc=((sV, 1),))
            op("vector", lambda e, e2=e2, rcp=rcp, sw_=sw_: e.scalar_tensor_tensor(sw_, e2, rcp, rcp, ALU.mult, ALU.add), inc=((sV, 1),))
            op("vector", lambda e, L=L, m2=m2, wk_=wk_: e.tensor_scalar(wk_, L, m2, None, ALU.is_ge), inc=((sV, 1),))
            op("vector", lambda e, wk_=wk_, idsc=idsc: e.scalar_tensor_tensor(idsc, wk_, 1000.0, SM16[:, 48:56], ALU.mult, ALU.add), inc=((sV, 1),))
            op("vector", lambda e, idsc=idsc, cmx=cmx: e.reduce_max(cmx, idsc, AX.X), inc=((sV, 1),))
            op("vector", lambda e, idsc=idsc, cmx=cmx, wk_=wk_: e.tensor_scalar(wk_, idsc, cmx, None, ALU.is_ge), inc=((sV, 1),))
            op("vector", lambda e, wk_=wk_, sw_=sw_: e.tensor_scalar(wk_, wk_, sw_, None, ALU.mult), inc=((sV, 1),))
            op("vector", lambda e, wk_=wk_, t=t: e.tensor_mul(FCO[:, t * 8:(t + 1) * 8], wk_, SM16[:, 56:64]), inc=((sV, 1),))
            op("vector", lambda e, t=t: e.tensor_copy(FCB[:], FCO[:, t * 8:(t + 1) * 8]), inc=((sV, 1),))
            op("tensor", lambda e, t=t: e.matmul(PT[0:8, 0:128], FCB[:], IDB[:], start=True, stop=True),
               wait=((sV, None), (sA, None)), inc=((sT, 1),))
            op("vector", lambda e, t=t: e.tensor_copy(CTB[:, t * 128:(t + 1) * 128], PT[0:8, 0:128]),
               wait=((sT, None),), inc=((sV, 1),))

        ACC = QB
        last_hh_sT = [0]
        last_w2_sT = [0]
        zmark_par = [(), ()]
        evacmarkB = [()]
        trgmark = [()]
        for e_i in range(EE):
            w1w = load_wslab(WSA, w1[e_i], DC, 2048,
                             wait=((sT, last_hh_sT[0]),) if e_i > 0 else ((sT, None),))
            w2w = load_wslab(WSB, w2[e_i], FC, 1024,
                             wait=((sT, last_w2_sT[0]),) if e_i > 0 else ((sT, None),))
            b1i, b1iw = bB1.acq()
            b1t = B1T[:, b1i * 2048:(b1i + 1) * 2048]
            b1w = dma("gpsimd", b1t, b1[e_i:e_i + 1, :].broadcast_to([128, FF]), wait=b1iw)
            lgi, lgiw = bLNG.acq()
            lg = LNGB[:, lgi * 32: lgi * 32 + 16]
            lb = LNGB[:, lgi * 32 + 16: lgi * 32 + 32]
            lgw = dma("sync", lg, lngT[e_i], wait=lgiw)
            lbw = dma("sync", lb, lnbT[e_i])
            for quad in range(4):
                ts = [quad * 4 + j for j in range(4)]
                for j, t in enumerate(ts):
                    par = j % 2
                    BA, BB = (P0, P1) if par == 0 else (P2, PT)
                    aim = ()
                    if e_i == 0:
                        for half in range(2):
                            op("tensor", lambda e, t=t, half=half:
                               e.matmul(P2[:, half * 512:(half + 1) * 512], CTB[:, t * 128:(t + 1) * 128],
                                        B2S[:, half * 512:(half + 1) * 512], start=True, stop=True),
                               wait=(b2_w, (sV, None), (sA, None)) if half == 0 else (), inc=((sT, 1),))
                        for ch in range(DC):
                            pc = (ch % 2) * 512
                            op("tensor", lambda e, t=t, ch=ch, pc=pc:
                               e.matmul(PT[:, pc:pc + 128], FMB[:, ch * 2048 + t * 128: ch * 2048 + (t + 1) * 128], IDB[:], start=True, stop=True),
                               wait=((sV, None), (sA, None)), inc=((sT, 1),))
                            op("vector", lambda e, ch=ch, pc=pc:
                               e.tensor_copy(XRS[:, ch * 128:(ch + 1) * 128], PT[:, pc:pc + 128]),
                               wait=((sT, None),), inc=((sV, 1),))
                        for half in range(2):
                            op("vector", lambda e, t=t, half=half:
                               e.scalar_tensor_tensor(ACC[:, t * 1024 + half * 512: t * 1024 + (half + 1) * 512],
                                                      XRS[:, half * 512:(half + 1) * 512], SWB[:, t:t + 1],
                                                      P2[:, half * 512:(half + 1) * 512], ALU.mult, ALU.add),
                               inc=((sV, 1),))
                        aim = ((sV, sV.n),)
                    for k in range(DC):
                        for fs in range(4):
                            bank = [BA, BA, BB, BB][fs]
                            pa = bank[:, (fs % 2) * 512:(fs % 2) * 512 + 512]
                            w = ()
                            if k == 0 and fs == 0:
                                w = (w1w,) + aim + (zmark_par[par] if zmark_par[par] else ((sA, None), (sV, None)))
                                if par == 1 and evacmarkB[0]:
                                    w = w + evacmarkB[0]
                                if par == 0 and j == 0 and trgmark[0]:
                                    w = w + trgmark[0]
                            op("tensor", lambda e, pa=pa, k=k, fs=fs, t=t:
                               e.matmul(pa, FMB[:, k * 2048 + t * 128: k * 2048 + (t + 1) * 128],
                                        WSA[:, k * 2048 + fs * 512: k * 2048 + (fs + 1) * 512],
                                        start=(k == 0), stop=(k == DC - 1)),
                               wait=w, inc=((sT, 1),) if (k == DC - 1 and fs == 3) else ())
                    last_hh_sT[0] = sT.n
                    op("vector", lambda e, b1t=b1t, BA=BA: e.tensor_add(BA[:, 0:1024], BA[:, 0:1024], b1t[:, 0:1024]),
                       wait=((sT, None), b1w), inc=((sV, 1),))
                    op("vector", lambda e, b1t=b1t, BB=BB: e.tensor_add(BB[:, 0:1024], BB[:, 0:1024], b1t[:, 1024:2048]),
                       inc=((sV, 1),))
                    si, rstd, negm = ln_stats4([BA[:, 0:512], BA[:, 512:1024], BB[:, 0:512], BB[:, 512:1024]])
                    zsl = ZS[:, j * 2048:(j + 1) * 2048]
                    op("scalar", lambda e, zsl=zsl, rstd=rstd, negm=negm, BA=BA:
                       e.activation(zsl[:, 0:1024], BA[:, 0:1024], AF.Identity, bias=negm, scale=rstd),
                       wait=((sV, None),), inc=((sA, 1),))
                    op("scalar", lambda e, zsl=zsl, rstd=rstd, negm=negm, BB=BB:
                       e.activation(zsl[:, 1024:2048], BB[:, 0:1024], AF.Identity, bias=negm, scale=rstd),
                       inc=((sA, 1),))
                    bST.rel(si, sA)
                    zmark_par[par] = ((sA, sA.n),)
                trslots = [P0[:, 0:512], P0[:, 512:1024], P1[:, 0:512], P1[:, 512:1024]]
                gmark = {}

                def w2_fc(fc, jg, bank, first_w):
                    for half in range(2):
                        w = first_w if half == 0 else ()
                        op("tensor", lambda e, fc=fc, half=half, jg=jg, bank=bank:
                           e.matmul(bank[:, half * 512:(half + 1) * 512],
                                    H2T[:, 8192 + fc * 512 + jg * 128: 8192 + fc * 512 + (jg + 1) * 128],
                                    WSB[:, fc * 1024 + half * 512: fc * 1024 + (half + 1) * 512],
                                    start=(fc == 0), stop=(fc == FC - 1)),
                           wait=w, inc=((sT, 1),) if (fc == FC - 1 and half == 1) else ())

                for fc in range(FC):
                    sl = trslots[fc % 4]
                    for j in range(4):
                        w = ()
                        if j == 0:
                            if fc == 0:
                                w = zmark_par[1]
                            if fc >= 4:
                                w = w + ((sA, gmark[fc - 4]),)
                        op("tensor", lambda e, j=j, fc=fc, sl=sl:
                           e.matmul(sl[:, j * 128:(j + 1) * 128], ZS[:, j * 2048 + fc * 128: j * 2048 + (fc + 1) * 128], IDB[:], start=True, stop=True),
                           wait=w, inc=((sT, 1),) if j == 3 else ())
                    op("scalar", lambda e, fc=fc, lg=lg, lb=lb, sl=sl:
                       e.activation(H2T[:, 8192 + fc * 512: 8192 + fc * 512 + 512], sl, AF.Gelu,
                                    bias=lb[:, fc:fc + 1], scale=lg[:, fc:fc + 1]),
                       wait=((sT, None), lgw, lbw), inc=((sA, 1),))
                    gmark[fc] = sA.n
                    if fc >= 1:
                        fw = ((sA, gmark[fc - 1]),)
                        if fc == 1:
                            fw = fw + (w2w,) + zmark_par[1]
                        w2_fc(fc - 1, 0, P2, fw)
                        w2_fc(fc - 1, 1, PT, ())
                w2_fc(FC - 1, 0, P2, ((sA, gmark[FC - 1]),))
                w2_fc(FC - 1, 1, PT, ())
                for jg, bank in ((0, P2), (1, PT)):
                    t = ts[jg]
                    for half in range(2):
                        op("vector", lambda e, t=t, half=half, e_i=e_i, bank=bank:
                           e.scalar_tensor_tensor(ACC[:, t * 1024 + half * 512: t * 1024 + (half + 1) * 512],
                                                  bank[:, half * 512:(half + 1) * 512], FCO[:, t * 8 + e_i: t * 8 + e_i + 1],
                                                  ACC[:, t * 1024 + half * 512: t * 1024 + (half + 1) * 512],
                                                  ALU.mult, ALU.add),
                           wait=((sT, None),) if (jg == 0 and half == 0) else (), inc=((sV, 1),))
                evacA = ((sV, sV.n),)
                for fc in range(FC):
                    fw = evacA + ((sA, gmark[FC - 1]),) if fc == 0 else ()
                    w2_fc(fc, 2, P2, fw)
                    w2_fc(fc, 3, PT, ())
                last_w2_sT[0] = sT.n
                for jg, bank in ((2, P2), (3, PT)):
                    t = ts[jg]
                    for half in range(2):
                        op("vector", lambda e, t=t, half=half, e_i=e_i, bank=bank:
                           e.scalar_tensor_tensor(ACC[:, t * 1024 + half * 512: t * 1024 + (half + 1) * 512],
                                                  bank[:, half * 512:(half + 1) * 512], FCO[:, t * 8 + e_i: t * 8 + e_i + 1],
                                                  ACC[:, t * 1024 + half * 512: t * 1024 + (half + 1) * 512],
                                                  ALU.mult, ALU.add),
                           wait=((sT, None),) if (jg == 2 and half == 0) else (), inc=((sV, 1),))
                evacmarkB[0] = ((sV, sV.n),)
                trgmark[0] = ((sA, gmark[FC - 1]),)
            bB1.rel(b1i, sV)
            bLNG.rel(lgi, sA)

        dump(f"acc{ccid}", QB[:], ((sV, None),))
        dump(f"fco{ccid}", FCO[:], ((sV, None),))
        dump(f"swb{ccid}", SWB[:], ((sV, None),))
        n2gw = dma("gpsimd", NGB[:, 0:1024], n2g.broadcast_to([128, D]), wait=((sV, None),))
        n2bw = dma("gpsimd", NGB[:, 1024:2048], n2b.broadcast_to([128, D]))
        cpm2 = {}
        for t in range(NT):
            for ch in range(DC):
                pc = (ch % 2) * 512
                if t == 0 and ch < 2:
                    w = ((sV, None),)
                else:
                    w = ((sV, cpm2[ch % 2]),) if (ch % 2) in cpm2 else ((sV, None),)
                op("tensor", lambda e, t=t, ch=ch, pc=pc:
                   e.matmul(PT[:, pc:pc + 128], FMB[:, ch * 2048 + t * 128: ch * 2048 + (t + 1) * 128], IDB[:], start=True, stop=True),
                   wait=w, inc=((sT, 1),))
                op("vector", lambda e, ch=ch, pc=pc:
                   e.tensor_copy(XRS[:, ch * 128:(ch + 1) * 128], PT[:, pc:pc + 128]),
                   wait=((sT, None),), inc=((sV, 1),))
                cpm2[ch % 2] = sV.n
            i, w = bXB.acq()
            z2 = XB[:, i * 1024:(i + 1) * 1024]
            op("vector", lambda e, z2=z2, t=t: e.tensor_add(z2, XRS[:], ACC[:, t * 1024:(t + 1) * 1024]),
               wait=w, inc=((sV, 1),))
            si, rstd, negm = ln_stats(z2)
            t1 = T1F
            op("vector", lambda e, z2=z2, t1=t1, rstd=rstd:
               e.scalar_tensor_tensor(t1, z2, rstd, NGB[:, 0:1024], ALU.mult, ALU.mult),
               wait=(n2gw, n2bw) if t == 0 else (), inc=((sV, 1),))
            op("vector", lambda e, z2=z2, negm=negm:
               e.scalar_tensor_tensor(z2, NGB[:, 0:1024], negm, NGB[:, 1024:2048], ALU.mult, ALU.add),
               inc=((sV, 1),))
            bST.rel(si, sV)
            i2, w2_ = bXB.acq()
            xo = XB[:, i2 * 1024:(i2 + 1) * 1024]
            xw = dma("sync", xo, src_dram[t * 128:(t + 1) * 128, :], wait=w2_)
            op("vector", lambda e, z2=z2, t1=t1: e.tensor_add(t1, t1, z2), inc=((sV, 1),))
            op("vector", lambda e, t1=t1, xo=xo:
               e.scalar_tensor_tensor(xo, t1, 0.5, xo, ALU.mult, ALU.add),
               wait=(xw,), inc=((sV, 1),))
            _last_dma = dma("sync", dst_dram[t * 128:(t + 1) * 128, :], xo, wait=((sV, None),))
            bXB.rel(i, sV)
            bXB.free_at[i2].append(_last_dma)

    def attn(src_dram, dst_dram):
        cci, cco = cc_in[1], cc_out[1]
        ln0(src_dram, FMB)
        aw1 = dma("sync", KRELS[:], krel, wait=((sT, None), (sV, None), (sA, None)))
        dma("sync", QRELS[:], qrel)
        KCOL = ROWS[:, 256:288]
        BIASB = ROWS[:, 288:352]
        kclw = dma("sync", KCOL, kcol)
        op("vector", lambda e: e.tensor_scalar_mul(SM16[:, 32:48], SM16[:, 0:16], -1.0),
           wait=((sA, None),), inc=((sV, 1),))
        dma("sync", AMS.rearrange("p (a c) -> p a c", c=128), amask.rearrange("a p c -> p a c"))
        dma("sync", CMS.rearrange("p (a c) -> p a c", c=128), cmask.rearrange("a p c -> p a c"))
        ACW_T = ALLD()
        dump("krels", KRELS[:], ALLD())
        dump("qrels", QRELS[:], ALLD())

        wkw = dma("gpsimd", WSA[:, 4096:5120].rearrange("p (c o) -> p c o", o=256),
                  wk.rearrange("(c p) o -> p c o", p=128), wait=((sT, None),))
        wvw = dma("gpsimd", WSA[:, 5120:6144].rearrange("p (c o) -> p c o", o=256),
                  wv.rearrange("(c p) o -> p c o", p=128))
        wks = WSA[:, 4096:5120]
        wvs = WSA[:, 5120:6144]
        ktv = cci[0:1, 0:KT2_ELE].rearrange("a (p c) -> (a p) c", p=64)
        vtv = cci[0:1, KT2_ELE:KT2_ELE + VT2_ELE].rearrange("a (p c) -> (a p) c", p=128)
        for oc in range(2):
            for sp in range(4):
                pa = [P0, P1][(oc * 4 + sp) % 2][:, 0:512]
                for k in range(4):
                    w = ((wkw, (sA, None), (sV, None)) if k == 0 else ())
                    op("tensor", lambda e, pa=pa, k=k, oc=oc, sp=sp:
                       e.matmul(pa, wks[:, k * 256 + oc * 128: k * 256 + (oc + 1) * 128],
                                FMB[:, (4 + k) * 2048 + sp * 512: (4 + k) * 2048 + (sp + 1) * 512],
                                start=(k == 0), stop=(k == 3)),
                       wait=w, inc=((sT, 1),) if k == 3 else ())
                ei, ew = bEB.acq()
                esl = EB[:, ei * 1024: ei * 1024 + 512]
                op("scalar", lambda e, esl=esl, pa=pa: e.copy(esl, pa),
                   wait=((sT, None),) + ew, inc=((sA, 1),))
                for half in range(2):
                    kvh = 2 * oc + half
                    _last_dma = dma("sync", ktv[0:64, kvh * TOK + sp * 512: kvh * TOK + (sp + 1) * 512],
                        esl[half * 64:half * 64 + 64, 0:512], wait=((sA, None),))
                bEB.free_at[ei].append(_last_dma)
        for oc in range(2):
            for sp in range(4):
                pa = [P0, P1][(oc * 4 + sp) % 2][:, 512:1024]
                for k in range(4):
                    w = ((wvw, (sA, None), (sV, None)) if k == 0 else ())
                    op("tensor", lambda e, pa=pa, k=k, oc=oc, sp=sp:
                       e.matmul(pa, wvs[:, k * 256 + oc * 128: k * 256 + (oc + 1) * 128],
                                FMB[:, k * 2048 + sp * 512: k * 2048 + (sp + 1) * 512],
                                start=(k == 0), stop=(k == 3)),
                       wait=w, inc=((sT, 1),) if k == 3 else ())
                ei, ew = bEB.acq()
                esl = EB[:, ei * 1024: ei * 1024 + 512]
                op("scalar", lambda e, esl=esl, pa=pa: e.copy(esl, pa),
                   wait=((sT, None),) + ew, inc=((sA, 1),))

                def sink(t, pts, oc=oc):
                    zi, zw = bZV.acq()
                    zv = ZS[:, 16384 + zi * 256: 16384 + zi * 256 + 256]
                    op("vector", lambda e, zv=zv, pts=pts: e.tensor_copy(zv[:, 0:64], pts[:, 0:64]),
                       wait=((sT, None),) + zw, inc=((sV, 1),))
                    op("vector", lambda e, zv=zv, pts=pts: e.tensor_copy(zv[:, 65:129], pts[:, 64:128]), inc=((sV, 1),))
                    op("vector", lambda e, zv=zv: e.memset(zv[:, 64:65], 1.0), inc=((sV, 1),))
                    op("vector", lambda e, zv=zv: e.memset(zv[:, 129:130], 1.0), inc=((sV, 1),))
                    _last_dma = dma("sync", vtv[:, t * (KVH * 65) + (2 * oc) * 65: t * (KVH * 65) + (2 * oc) * 65 + 130],
                        zv[:, 0:130], wait=((sV, None),))
                    bZV.free_at[zi].append(_last_dma)
                transpose_to_tok(esl, sp, sink)
                bEB.rel(ei, sT)
        op("gpsimd", lambda e: e.collective_compute(
            "AllGather", ALU.bypass,
            replica_groups=[[0, 1], [2, 3], [4, 5], [6, 7]],
            ins=[cci.opt()], outs=[cco.opt()]),
           wait=(*ALLD(), *ALLG()), inc=((sCC, 1),))
        ccw = (sCC, sCC.n)

        op("vector", lambda e: e.tensor_scalar_mul(ROWS[0:1, 96:112], ROWS[0:1, 16:32], 8.0), inc=((sV, 1),))
        op("vector", lambda e: e.reciprocal(ROWS[0:1, 32:48], ROWS[0:1, 96:112]),
           inc=((sV, 1),))
        qw_ = dma("sync", ROWS[0:1, 48:64], qpa)
        op("scalar", lambda e: e.activation(ROWS[0:1, 64:80], ROWS[0:1, 48:64], AF.Sin),
           wait=(qw_,), inc=((sA, 1),))
        op("scalar", lambda e: e.activation(ROWS[0:1, 80:96], ROWS[0:1, 48:64], AF.Sin, bias=PI2C[0:1, 0:1]),
           inc=((sA, 1),))
        w1_ = dma("sync", rowsd[0:1, 0:16], ROWS[0:1, 64:80], wait=((sA, None),))
        w2_ = dma("sync", rowsd[0:1, 16:32], ROWS[0:1, 80:96], wait=((sA, None),))
        w3_ = dma("sync", rowsd[0:1, 32:48], ROWS[0:1, 32:48], wait=((sA, None),))
        wsin = dma("sync", SM16[:, 80:96], rowsd[0:1, 0:16].broadcast_to([128, 16]),
                   wait=(w1_, (sA, None), (sT, None), (sV, None)))
        wcos = dma("sync", SM16[:, 96:112], rowsd[0:1, 16:32].broadcast_to([128, 16]), wait=(w2_,))
        wrs8 = dma("sync", SM16[:, 112:128], rowsd[0:1, 32:48].broadcast_to([128, 16]), wait=(w3_,))

        wqrot = WSA
        for k in range(4):
            wqw = dma("sync", WQSTG[:, 0:1024], wq[k * 128:(k + 1) * 128, :], wait=((sV, None),))
            for h in range(H):
                wq_h = WQSTG[:, h * 64:(h + 1) * 64]
                ev = wq_h.rearrange("p (a b) -> p a b", b=2)[:, :, 0].rearrange("p a -> p a")
                od = wq_h.rearrange("p (a b) -> p a b", b=2)[:, :, 1].rearrange("p a -> p a")
                cc_ = SM16[:, 96 + h:97 + h]; ss_ = SM16[:, 80 + h:81 + h]; rr_ = SM16[:, 112 + h:113 + h]
                t1 = WQSTG[:, 1024:1056]; t2 = WQSTG[:, 1056:1088]
                ww = (wqw, wsin, wcos, wrs8, (sT, None)) if (h == 0) else ()
                op("vector", lambda e, ev=ev, cc_=cc_, t1=t1: e.tensor_scalar_mul(t1, ev, cc_), wait=ww, inc=((sV, 1),))
                op("vector", lambda e, od=od, ss_=ss_, t2=t2: e.tensor_scalar_mul(t2, od, ss_), inc=((sV, 1),))
                op("vector", lambda e, t1=t1, t2=t2: e.tensor_sub(t1, t1, t2), inc=((sV, 1),))
                op("vector", lambda e, t1=t1, rr_=rr_: e.tensor_scalar_mul(t1, t1, rr_), inc=((sV, 1),))
                op("vector", lambda e, ev=ev, ss_=ss_, t2=t2: e.tensor_scalar_mul(t2, ev, ss_), inc=((sV, 1),))
                op("vector", lambda e, od=od, cc_=cc_, od2=WQSTG[:, 1088:1120]: e.tensor_scalar_mul(od2, od, cc_), inc=((sV, 1),))
                op("vector", lambda e, t2=t2, od2=WQSTG[:, 1088:1120]: e.tensor_add(t2, t2, od2), inc=((sV, 1),))
                op("vector", lambda e, t2=t2, rr_=rr_: e.tensor_scalar_mul(t2, t2, rr_), inc=((sV, 1),))
                wr_ev = wqrot[:, k * 1024 + h * 64:(k * 1024) + (h + 1) * 64].rearrange("p (a b) -> p a b", b=2)[:, :, 0].rearrange("p a -> p a")
                wr_od = wqrot[:, k * 1024 + h * 64:(k * 1024) + (h + 1) * 64].rearrange("p (a b) -> p a b", b=2)[:, :, 1].rearrange("p a -> p a")
                op("vector", lambda e, t1=t1, wr_ev=wr_ev: e.tensor_copy(wr_ev, t1), inc=((sV, 1),))
                op("vector", lambda e, t2=t2, wr_od=wr_od: e.tensor_copy(wr_od, t2), inc=((sV, 1),))

        for oc in range(DC):
            for sp in range(4):
                idx = oc * 4 + sp
                bank = [P0, P1][idx % 2]
                pa = bank[:, 0:512] if (idx % 4) < 2 else bank[:, 512:1024]
                for k in range(4):
                    w = ((sV, None), (sA, None)) if k == 0 else ()
                    op("tensor", lambda e, pa=pa, k=k, oc=oc, sp=sp:
                       e.matmul(pa, wqrot[:, k * 1024 + oc * 128: k * 1024 + (oc + 1) * 128],
                                FMB[:, (4 + k) * 2048 + sp * 512: (4 + k) * 2048 + (sp + 1) * 512],
                                start=(k == 0), stop=(k == 3)),
                       wait=w, inc=((sT, 1),) if k == 3 else ())
                op("scalar", lambda e, pa=pa, oc=oc, sp=sp:
                   e.copy(QB[:, oc * 2048 + sp * 512: oc * 2048 + (sp + 1) * 512], pa),
                   wait=((sT, None),), inc=((sA, 1),))

        khA = WSB
        for kvh in range(KVH):
            pcb = kvh * 4096
            for r in range(2):
                src = cco[r:r + 1, 0:KT2_ELE].rearrange("a (p c) -> (a p) c", p=64)[0:64, kvh * TOK:(kvh + 1) * TOK]
                src = src.rearrange("p (lt c) -> p lt c", c=128)
                for hb in range(2):
                    dst = khA[hb * 64:hb * 64 + 64, pcb:pcb + 4096].rearrange("p (lt c) -> p lt c", c=256)
                    dma("sync", dst[:, :, r * 128:(r + 1) * 128], src, wait=(ccw, (sT, None)))
                vsrc = cco[r:r + 1, KT2_ELE:KT2_ELE + VT2_ELE].rearrange("a (p c) -> (a p) c", p=128)
                vsrc = vsrc.rearrange("p (t x) -> p t x", x=KVH * 65)[:, :, kvh * 65:(kvh + 1) * 65]
                vdst = vAt[:, 0:8320].rearrange("p (cc x) -> p cc x", x=260)[:, :, kvh * 65:(kvh + 1) * 65]
                vdst = vdst.rearrange("p (lt two) x -> p lt two x", two=2)[:, :, r, :]
                dma("sync", vdst, vsrc, wait=(ccw, (sT, None), (sA, None)))
        kvw = ALLD()

        dump("qa", QB[:], ((sA, None),))
        dump("kha", WSB[:], (*ALLD(),))
        MPW = ((sA, sA.n), (sV, sV.n))
        for t in range(NT):
            npieces = (2 * t + 2 + 7) // 8
            for h in range(H):
                kvh = h // 4
                prow = (h % 2) * 64
                pcb = kvh * 4096
                for p_i in range(npieces):
                    klo = p_i * 1024
                    khi = min((2 * t + 2) * 128, klo + 1024)
                    si, sw = bSP.acq()
                    bank = [P0, P1, PT][si]
                    a = klo
                    first = True
                    while a < khi:
                        b_ = min(a + 512, khi)
                        w = kvw + ACW_T + MPW + sw if first else ()
                        op("tensor", lambda e, bank=bank, a=a, b_=b_, klo=klo, h=h, t=t, prow=prow, pcb=pcb:
                           e.matmul(bank[:, a - klo: b_ - klo],
                                    QB[prow:prow + 64, (h // 2) * 2048 + t * 128:(h // 2) * 2048 + (t + 1) * 128],
                                    khA[prow:prow + 64, pcb + a: pcb + b_],
                                    start=True, stop=False),
                           wait=w)
                        op("tensor", lambda e, bank=bank, a=a, b_=b_, klo=klo, t=t:
                           e.matmul(bank[:, a - klo: b_ - klo],
                                    QRELS[:, t * 128:(t + 1) * 128], KRELS[:, a:b_],
                                    start=False, stop=True),
                           inc=((sT, 1),))
                        first = False
                        a = b_
                    if khi == (2 * t + 2) * 128:
                        off = (khi - klo) - 256
                        op("vector", lambda e, bank=bank, off=off:
                           e.tensor_add(bank[:, off:off + 256], bank[:, off:off + 256], AMS[:]),
                           wait=((sT, None),), inc=((sV, 1),))
                    op("vector", lambda e, bank=bank, p_i=p_i, khi=khi, klo=klo:
                       e.reduce_max(GSC[:, 24 + p_i: 25 + p_i], bank[:, 0:khi - klo], AX.X),
                       wait=((sT, None),), inc=((sV, 1),))
                    bSP.rel(si, sV)
                op("vector", lambda e, npieces=npieces, h=h, t=t:
                   e.reduce_max(MBUF[:, h * 16 + t: h * 16 + t + 1], GSC[:, 24:24 + npieces], AX.X),
                   inc=((sV, 1),))
        for h in range(H):
            op("vector", lambda e, h=h: e.reduce_max(MH16[:, h:h + 1], MBUF[:, h * 16:(h + 1) * 16], AX.X),
               inc=((sV, 1),))
        op("vector", lambda e: e.tensor_mul(MH16[:], MH16[:], SM16[:, 0:16]), inc=((sV, 1),))
        op("vector", lambda e: e.tensor_copy(MHB[:], MH16[:]), inc=((sV, 1),))
        op("tensor", lambda e: e.matmul(PT[0:16, 0:128], MHB[:], IDB[:], start=True, stop=True),
           wait=((sV, None), (sA, None)), inc=((sT, 1),))
        op("vector", lambda e: e.tensor_copy(XB[0:16, 0:128], PT[0:16, 0:128]),
           wait=((sT, None),), inc=((sV, 1),))
        op("vector", lambda e: e.reduce_max(XB[0:16, 128:129], XB[0:16, 0:128], AX.X), inc=((sV, 1),))
        wmh = dma("sync", rowsd[0:1, 100:116], XB[0:16, 128:129], wait=((sV, None),))
        wmh2 = dma("sync", ROWS[0:1, 200:216], rowsd[0:1, 100:116], wait=(wmh,))
        op("vector", lambda e: e.tensor_scalar_mul(ROWS[0:1, 216:232], ROWS[0:1, 200:216], -1.0),
           wait=(wmh2,), inc=((sV, 1),))
        nmhw = bounce_bcast(ROWS[0:1, 216:232], 16, SM16[:, 16:32], 1, 128, wait=((sV, None),))
        expm_w = {}
        for h in range(H):
            op("scalar", lambda e, h=h:
               e.activation(EXPM[:, h * 16:(h + 1) * 16], MBUF[:, h * 16:(h + 1) * 16], AF.Exp,
                            bias=SM16[:, 16 + h:17 + h], scale=SM16[:, h:h + 1]),
               wait=(nmhw, (sV, None)), inc=((sA, 1),))
            op("vector", lambda e, h=h: e.tensor_copy(MHB[:], EXPM[:, h * 16:(h + 1) * 16]),
               wait=((sA, None), (sT, None)), inc=((sV, 1),))
            op("tensor", lambda e: e.matmul(PT[0:16, 0:128], MHB[:], IDB[:], start=True, stop=True),
               wait=((sV, None), (sA, None)), inc=((sT, 1),))
            op("scalar", lambda e: e.copy(XB[0:16, 0:128], PT[0:16, 0:128]),
               wait=((sT, None),) + ((expm_w[h - 1],) if h > 0 else ()), inc=((sA, 1),))
            expm_w[h] = dma("sync",
                            rowsd[10 + (h % 8):11 + (h % 8), (h // 8) * 2048:(h // 8) * 2048 + 2048]
                            .rearrange("a (t q) -> (a t) q", q=128),
                            XB[0:16, 0:128], wait=((sA, None),))

        nmark2 = [()]
        ESL3b = [EB[:, 0:1024], EB[:, 1024:2048], SCR[:, 18432:19456]]
        eslb_free = [(), (), ()]
        eslb_cnt = [0]
        QAW = ((sA, sA.n),)
        for h in range(H):
            kvh = h // 4
            prow = (h % 2) * 64
            pcb = kvh * 4096
            bslot = BIASB[:, (h % 2) * 32:(h % 2) * 32 + 32]
            op("vector", lambda e, bslot=bslot, h=h:
               e.tensor_scalar(bslot, KCOL, SM16[:, 32 + h:33 + h], SM16[:, 16 + h:17 + h],
                               ALU.mult, ALU.add),
               wait=(kclw, nmhw, (sA, None)), inc=((sV, 1),))
            bwh = ((sV, sV.n),)
            for qs in range(2):
                cmaxv = 16 * qs + 16
                pend = []
                lastb0 = 7 if qs == 0 else 23

                def emit_av(p, h=h, qs=qs, kvh=kvh, cmaxv=cmaxv, lastb0=lastb0):
                    esl, c, spans, avm, ei = p
                    for sx, (a, b_) in enumerate(spans):
                        stop_ = (c == lastb0) if a < 512 else (c == cmaxv - 1)
                        op("tensor", lambda e, esl=esl, a=a, b_=b_, c=c, kvh=kvh, stop_=stop_:
                           e.matmul(P2[0:65, a:b_], vAt[:, c * 260 + kvh * 65: c * 260 + kvh * 65 + 65],
                                    esl[:, a:b_], start=(c == 0), stop=stop_),
                           wait=avm if sx == 0 else (), inc=((sT, 1),) if (sx == len(spans) - 1 and c == cmaxv - 1) else ())
                    eslb_free[ei] = ((sT, sT.n),)

                for c in range(cmaxv):
                    t0 = max(c // 2 - 8 * qs, 0)
                    cols0 = t0 * 128
                    base = (h // 2) * 2048 + qs * 1024
                    si, sw = bSP.acq()
                    sbank = [P0, P1, PT][si]
                    spans = []
                    if cols0 < 512:
                        spans.append((cols0, 512))
                        spans.append((512, 1024))
                    else:
                        spans.append((cols0, 1024))
                    for sx, (a, b_) in enumerate(spans):
                        w = sw + QAW if sx == 0 else ()
                        op("tensor", lambda e, sbank=sbank, a=a, b_=b_, c=c, base=base, prow=prow, pcb=pcb:
                           e.matmul(sbank[:, a:b_], khA[prow:prow + 64, pcb + c * 128: pcb + (c + 1) * 128],
                                    QB[prow:prow + 64, base + a: base + b_],
                                    start=True, stop=True),
                           wait=w, inc=((sT, 1),))
                    ei = eslb_cnt[0] % 3
                    eslb_cnt[0] += 1
                    ew = eslb_free[ei]
                    esl = ESL3b[ei]
                    op("scalar", lambda e, esl=esl, sbank=sbank, c=c, bslot=bslot, h=h, cols0=cols0:
                       e.activation(esl[:, cols0:1024], sbank[:, cols0:1024], AF.Exp,
                                    bias=bslot[:, c:c + 1], scale=SM16[:, h:h + 1]),
                       wait=((sT, None),) + bwh + ew, inc=((sA, 1),))
                    bSP.rel(si, sA)
                    bt = c // 2
                    if 8 * qs <= bt < 8 * qs + 8:
                        bc = (bt - 8 * qs) * 128
                        op("vector", lambda e, esl=esl, bc=bc, c=c:
                           e.tensor_mul(esl[:, bc:bc + 128], esl[:, bc:bc + 128],
                                        CMS[:, (c % 2) * 128:(c % 2) * 128 + 128]),
                           wait=((sA, None),), inc=((sV, 1),))
                        avm = ((sV, sV.n),)
                    else:
                        avm = ((sA, sA.n),)
                    if c == 0:
                        avm = avm + (nmark2[0] if nmark2[0] else ())
                    pend.append((esl, c, spans, avm, ei))
                    if len(pend) > 2:
                        emit_av(pend.pop(0))
                for p in pend:
                    emit_av(p)
                OSC = SCRF[:, 8192:9216]
                oi, ow = bOS.acq()
                op("vector", lambda e, OSC=OSC: e.tensor_copy(OSC[0:65, :], P2[0:65, 0:1024]),
                   wait=((sT, None),) + ow, inc=((sV, 1),))
                nmark2[0] = ((sV, sV.n),)
                ri, rw = bRW.acq()
                erow = ROWS[0:1, 512 + ri * 1024: 512 + ri * 1024 + 1024]
                wb2 = dma("sync", erow,
                          rowsd[10 + (h % 8):11 + (h % 8), (h // 8) * 2048 + qs * 1024:(h // 8) * 2048 + qs * 1024 + 1024],
                          wait=rw + (expm_w[h],))
                op("vector", lambda e, OSC=OSC: e.tensor_copy(XB[0:1, 2048:3072], OSC[64:65, :]),
                   wait=((sV, None),), inc=((sV, 1),))
                op("vector", lambda e, erow=erow: e.tensor_add(erow, erow, XB[0:1, 2048:3072]),
                   wait=(wb2,), inc=((sV, 1),))
                op("vector", lambda e, erow=erow: e.reciprocal(erow, erow), inc=((sV, 1),))
                rbw = bounce_bcast(erow, 1024, RCPB[0:64, ri * 1024: ri * 1024 + 1024], 4 + ri, 64,
                                   wait=((sV, None),), wait2=((sV, None),))
                op("vector", lambda e, h=h, qs=qs, ri=ri, OSC=OSC:
                   e.scalar_tensor_tensor(FMB[(h % 2) * 64:(h % 2) * 64 + 64,
                                              (h // 2) * 2048 + qs * 1024:(h // 2) * 2048 + (qs + 1) * 1024],
                                          OSC[0:64, :], 1.0, RCPB[0:64, ri * 1024: ri * 1024 + 1024],
                                          ALU.mult, ALU.mult),
                   wait=(rbw,), inc=((sV, 1),))
                bRW.rel(ri, sV)
                bOS.rel(oi, sV)

        dump("aofm", FMB[:], ((sV, None),))
        dump("mbuf", MBUF[:], ((sV, None),))
        wow = load_wslab(WSA[:, 8192:16384], wo, DC, 1024, wait=((sT, None),))
        wosl = WSA[:, 8192:16384]

        def evwo(oc, sp, pa):
            ei, ew = bEB.acq()
            esl = EB[:, ei * 1024: ei * 1024 + 512]
            op("scalar", lambda e, esl=esl, pa=pa: e.copy(esl, pa),
               wait=((sT, None),) + ew, inc=((sA, 1),))

            def sink(t, pts, oc=oc):
                op("vector", lambda e, t=t, oc=oc, pts=pts:
                   e.tensor_copy(QB[:, t * 1024 + oc * 128: t * 1024 + (oc + 1) * 128], pts),
                   wait=((sT, None),), inc=((sV, 1),))
            transpose_to_tok(esl, sp, sink)
            bEB.rel(ei, sT)
        projection(FMB, wosl, DC, evwo, first_wait=(wow,))

        for t in range(NT):
            i, w = bXB.acq()
            xo = XB[:, i * 1024:(i + 1) * 1024]
            xw = dma("sync", xo, src_dram[t * 128:(t + 1) * 128, :], wait=w + (((sV, None),) if t == 0 else ()))
            op("vector", lambda e, xo=xo, t=t:
               e.tensor_add(xo, xo, QB[:, t * 1024:(t + 1) * 1024]),
               wait=(xw,), inc=((sV, 1),))
            _last_dma = dma("sync", dst_dram[t * 128:(t + 1) * 128, :], xo, wait=((sV, None),))
            bXB.free_at[i].append(_last_dma)

    ALLBUFS = [bXB, bEB, bSP, bWH, bB1, bLNG, bOS, bST, bRW, bZV]
    KPHASE = int(os.environ.get("KPHASE", "3"))
    if KPHASE == 1:
        ffn(x_in, out, 0)
    elif KPHASE == 2:
        ffn(x_in, xs, 0)
        attn(xs, out)
    else:
        ffn(x_in, xs, 0)
        if DBG:
            dump("xs1", xs, (*ALLD(),))
        attn(xs, xs)
        if DBG:
            dump("xs2", xs, (*ALLD(),))
        ffn(xs, out, 2)
    op("sync", lambda e: e.nop(), wait=(*ALLD(), *ALLG(), (sCC, None)))

    with nc.Block() as blk:
        sch.flush(blk)
    stack.close()
    return nc


_NC_CACHE = None


def kernel(**inputs):
    global _NC_CACHE
    xfull = np.ascontiguousarray(np.asarray(inputs["L0_attn_x"], dtype=np.float32))

    def cvt(a):
        return np.ascontiguousarray(np.asarray(a), dtype=np.float32)

    base = dict(
        wq=cvt(inputs["L0_attn_Wq"]), wk=cvt(inputs["L0_attn_Wk"]), wv=cvt(inputs["L0_attn_Wv"]),
        wo=cvt(inputs["L0_attn_Wo"]),
        qpa=cvt(inputs["L0_attn_q_phase_angle"]).reshape(1, H),
        als=cvt(inputs["L0_attn_alibi_log_slopes"]).reshape(1, H),
        fq=cvt(inputs["L0_ffn_q_w"]), fk=cvt(inputs["L0_ffn_k_w"]), fv=cvt(inputs["L0_ffn_v_w"]),
        fow=cvt(inputs["L0_ffn_out_w"]), fgw=cvt(inputs["L0_ffn_gate_w"]),
        fgbT=np.ascontiguousarray(cvt(inputs["L0_ffn_gate_b"]).reshape(DC, 128).T),
        fswT=np.ascontiguousarray(cvt(inputs["L0_ffn_scale_w"]).T),
        n1g=cvt(inputs["L0_ffn_n1_g"]).reshape(1, D), n1b=cvt(inputs["L0_ffn_n1_b"]).reshape(1, D),
        n2g=cvt(inputs["L0_ffn_n2_g"]).reshape(1, D), n2b=cvt(inputs["L0_ffn_n2_b"]).reshape(1, D),
        mgw=cvt(inputs["L0_ffn_moe_gate_w"]), mgb=cvt(inputs["L0_ffn_moe_gate_b"]).reshape(1, EE),
        w1=cvt(inputs["L0_ffn_w1"]), b1=cvt(inputs["L0_ffn_b1"]),
        lngT=np.ascontiguousarray(cvt(inputs["L0_ffn_ln_g"]).reshape(EE, FC, 128).transpose(0, 2, 1)),
        lnbT=np.ascontiguousarray(cvt(inputs["L0_ffn_ln_b"]).reshape(EE, FC, 128).transpose(0, 2, 1)),
        w2=cvt(inputs["L0_ffn_w2"]), b2=cvt(inputs["L0_ffn_b2"]),
        rsc=cvt(inputs["L0_ffn_res_scale"]).reshape(1, EE),
        identb=np.eye(128, dtype=BF), identf=np.eye(128, dtype=np.float32),
        krel=np.stack([np.ones(T), -32.0 * (np.arange(T) // 32), np.ones(T),
                       -1.0 * (np.arange(T) % 32)]).astype(BF),
        iot8=np.arange(EE, dtype=np.float32).reshape(1, EE),
        kcol=(np.arange(NC2, dtype=np.float32)[None, :] * 128.0
              + np.arange(128, dtype=np.float32)[:, None]),
    )

    in_maps = []
    for c in range(8):
        b, half = c // 2, c % 2
        rows = np.concatenate([np.arange((2 * t + half) * 128, (2 * t + half + 1) * 128) for t in range(NT)])
        m = dict(base)
        m["x"] = np.ascontiguousarray(xfull[b][rows])
        m["qrel"] = np.stack([np.zeros(TOK), np.ones(TOK), np.zeros(TOK), np.ones(TOK)]).astype(BF)
        r_ = np.arange(128)[:, None]
        i_ = np.arange(128)[None, :]
        am = np.zeros((2, 128, 128), np.float32)
        cm = np.zeros((2, 128, 128), np.float32)
        if half == 0:
            am[0][i_ > r_] = -1e30
            am[1][:, :] = -1e30
            cm[0] = (np.arange(128)[None, :] >= np.arange(128)[:, None]).astype(np.float32)
            cm[1][:, :] = 0.0
        else:
            am[0][:, :] = 0.0
            am[1][i_ > r_] = -1e30
            cm[0][:, :] = 1.0
            cm[1] = (np.arange(128)[None, :] >= np.arange(128)[:, None]).astype(np.float32)
        m["amask"] = am
        m["cmask"] = cm.astype(BF)
        in_maps.append(m)

    if _NC_CACHE is None:
        _NC_CACHE = build_program()
    res = run_bass_kernel_spmd(_NC_CACHE, in_maps, core_ids=list(range(8)))
    import os as _os
    if int(_os.environ.get("KDBG", "0")):
        kernel.last_results = res.results
    outp = np.zeros((B, T, D), np.float32)
    for c in range(8):
        b, half = c // 2, c % 2
        rows = np.concatenate([np.arange((2 * t + half) * 128, (2 * t + half + 1) * 128) for t in range(NT)])
        outp[b][rows] = res.results[c]["out"]
    return outp
